# version 02
# baseline (speedup 1.0000x reference)
import numpy as np
import ml_dtypes
import concourse.bass as bass
import concourse.mybir as mybir
from concourse.bass_utils import run_bass_kernel_spmd

F32 = mybir.dt.float32
BF16 = mybir.dt.bfloat16
AF = mybir.ActivationFunctionType
ALU = mybir.AluOpType
AX = mybir.AxisListType
BF = ml_dtypes.bfloat16

NCORES = 8
T = 1024
TH = T + 3
D = 4096
SEQ = 8192
NDC = D // 128
NTC = T // 128
NH, HDm, SS, NG, CHK = 128, 64, 128, 8, 128
INTER = NH * HDm
CONV_DIM = INTER + 2 * NG * SS
PROJ = INTER + CONV_DIM + NH
NOC_G = INTER // 128
NOC_X = CONV_DIM // 128
NOC_P = PROJ // 128
NCH = T // CHK
HPG = NH // NG
EPS = 1e-5
CAP = float(int(3.0 * SEQ / 8))
NQH, NKV, HDIM = 32, 8, 128
QKVD = (NQH + 2 * NKV) * HDIM
ISQ = float(1.0 / np.sqrt(128.0))


def tile_w(W):
    O, K = W.shape
    H = W.reshape(O // 128, 128, K // 128, 128).transpose(3, 0, 2, 1)
    return np.ascontiguousarray(H.reshape(128, O // 128, (K // 128) * 128))

def col_layout(v):
    return np.ascontiguousarray(np.asarray(v).reshape(-1, 128).T)


ENG_SEM = {"t": "pe", "v": "ve", "sc": "ac", "g": "po", "s": "sy"}
DMA_SEM = {"s": "dsy", "g": "dpo", "sc": "dsc", "v": "dve"}

class K:
    def __init__(self, nc):
        self.nc = nc
        self.sems = {}
        self.cnt = {n: 0 for n in list(ENG_SEM.values()) + ["cc"]}
        for n in list(self.cnt):
            self.sems[n] = nc.alloc_semaphore("S" + n)
        self.q = {e: [] for e in ENG_SEM}
        self.last_w = {}
        self.readers = {}
        self.satisfied = {e: {} for e in ENG_SEM}
        self.nops = 0
        self.ndrain = 0

    def _need(self, eng, waits, sem, val):
        cur = self.satisfied[eng].get(sem, 0)
        if val <= cur:
            return
        waits[sem] = max(waits.get(sem, 0), val)

    def op(self, eng, fn, rd=(), wr=(), ptr=(), dma=False, n_dma=1, tag=None):
        waits = {}
        for kk in tuple(rd) + tuple(ptr):
            lw = self.last_w.get(kk)
            if lw is not None:
                self._need(eng, waits, lw[0], lw[1])
        for kk in wr:
            lw = self.last_w.get(kk)
            if lw is not None:
                self._need(eng, waits, lw[0], lw[1])
            for sem, val in self.readers.get(kk, ()):
                self._need(eng, waits, sem, val)
        if dma:
            dk = tag if tag is not None else (wr[0] if wr else rd[0])
            sem = "D" + dk
            if sem not in self.cnt:
                self.cnt[sem] = 0
                self.sems[sem] = self.nc.alloc_semaphore(sem)
        else:
            sem = ENG_SEM[eng]
        inc = 16 * n_dma if dma else 1
        self.cnt[sem] += inc
        done = self.cnt[sem]
        self.q[eng].append((dict(waits), fn, sem, 16 if dma else 1))
        for s_, v_ in waits.items():
            self.satisfied[eng][s_] = max(self.satisfied[eng].get(s_, 0), v_)
        for kk in wr:
            self.last_w[kk] = (sem, done)
            self.readers[kk] = []
        for kk in tuple(rd) + tuple(ptr):
            self.readers.setdefault(kk, []).append((sem, done))
        self.nops += 1
        return done

    def cc_op(self, fn, rd=(), wr=()):
        waits = {}
        for kk in rd:
            lw = self.last_w.get(kk)
            if lw is not None:
                self._need("g", waits, lw[0], lw[1])
        for kk in wr:
            lw = self.last_w.get(kk)
            if lw is not None:
                self._need("g", waits, lw[0], lw[1])
            for sem, val in self.readers.get(kk, ()):
                self._need("g", waits, sem, val)
        self.cnt["cc"] += 1
        done = self.cnt["cc"]
        self.q["g"].append((dict(waits), fn, "cc", 1))
        for s_, v_ in waits.items():
            self.satisfied["g"][s_] = max(self.satisfied["g"].get(s_, 0), v_)
        for kk in wr:
            self.last_w[kk] = ("cc", done)
            self.readers[kk] = []
        for kk in rd:
            self.readers.setdefault(kk, []).append(("cc", done))
        return done

    def flush(self, name):
        nc = self.nc
        qs = self.q
        self.q = {e: [] for e in ENG_SEM}
        with nc.Block(name) as blk:
            def body(engkey):
                def run(e):
                    for waits, fn, sem, inc in qs[engkey]:
                        for s_, v_ in sorted(waits.items()):
                            e.wait_ge(self.sems[s_], v_)
                        res = fn(e)
                        if res is None:
                            raise RuntimeError(f"op fn returned None in {name}/{engkey}")
                        if isinstance(res, (list, tuple)):
                            for r in res:
                                r.then_inc(self.sems[sem], inc)
                        else:
                            res.then_inc(self.sems[sem], inc)
                return run
            if qs["s"]:
                blk.sync(body("s"))
            if qs["t"]:
                blk.tensor(body("t"))
            if qs["v"]:
                blk.vector(body("v"))
            if qs["sc"]:
                blk.scalar(body("sc"))
            if qs["g"]:
                blk.gpsimd(body("g"))

    def drain(self):
        tgt = dict(self.cnt)
        def body(e):
            for s_, v_ in tgt.items():
                if v_ > 0:
                    e.wait_ge(self.sems[s_], v_)
        self.ndrain += 1
        with self.nc.Block(f"drain{self.ndrain}") as blk:
            blk.sync(body); blk.tensor(body); blk.vector(body); blk.scalar(body); blk.gpsimd(body)
        for e in ENG_SEM:
            for s_, v_ in tgt.items():
                self.satisfied[e][s_] = max(self.satisfied[e].get(s_, 0), v_)



def mkconsts(nc):
    c = {}
    c["ident_bf"] = nc.inline_tensor(np.eye(128, dtype=BF), "c_ident_bf")
    c["ident_f32"] = nc.inline_tensor(np.eye(128, dtype=np.float32), "c_ident_f32")
    c["ones_row"] = nc.inline_tensor(np.ones((1, 128), dtype=np.float32), "c_ones_row")
    c["ones_col"] = nc.inline_tensor(np.ones((128, 1), dtype=np.float32), "c_ones_col")
    c["ones_col_bf"] = nc.inline_tensor(np.ones((128, 1), dtype=BF), "c_ones_col_bf")
    c["tri01"] = nc.inline_tensor(np.triu(np.ones((128, 128), dtype=np.float32)), "c_tri01")
    c["acol"] = nc.inline_tensor((-np.arange(1, 129, dtype=np.float32)).reshape(128, 1), "c_acol")
    bv = np.zeros((128, 4), dtype=np.float32)
    bv[:, 0] = 1e-5
    bv[:, 1] = 1.0
    c["biasv"] = nc.inline_tensor(bv, "c_biasv")
    e8 = np.zeros((8, 128), dtype=np.float32)
    for e in range(8):
        e8[e, e * 16:(e + 1) * 16] = 1.0
    c["e8"] = nc.inline_tensor(e8, "c_e8")
    return c


def sbuf_load_const(k, sbdict, name, dram, shape, dtype):
    t = k.nc.alloc_sbuf_tensor("sb_" + name, shape, dtype)
    sbdict[name] = t
    k.op("s", lambda e, d_=dram, t_=t: [e.dma_start(out=t_[:, :], in_=d_.ap())],
         wr=(name,), dma=True, n_dma=1)
    return t



LN_PARTS = ("stats", "apply", "tr")
LN_MAXCH = 99

def emit_ln(k, B, src_main, src_halo, xn_fm, tw, tag):
    nc = k.nc
    xtm, xsq, xbtm, st1, PST = B["xtm"], B["xsq"], B["xbtm"], B["st1"], B["PST"]
    identb = B["ident_bf"]
    nch = min(NTC + (1 if src_halo is not None else 0), LN_MAXCH)
    off = 3 if src_halo is not None else 0
    for i in range(nch):
        par = i % 2
        nr = 128 if i < NTC else 3
        r = slice(0, nr)
        co = 8 * par
        halo = i >= NTC
        if not halo:
            k.op("s", lambda e, i_=i, p_=par: [e.dma_start(out=xtm[p_][:], in_=src_main.ap()[128 * i_:128 * (i_ + 1), :])],
                 wr=(f"{tag}xtm{par}",), dma=True)
        else:
            k.op("s", lambda e, p_=par: [e.dma_start(out=xtm[p_][0:3, :], in_=src_halo.ap())],
                 wr=(f"{tag}xtm{par}",), dma=True)
        if "stats" not in LN_PARTS:
            continue
        k.op("sc", lambda e, p_=par, r_=r: e.activation(xsq[r_, :], xtm[p_][r_, :], AF.Square),
             rd=(f"{tag}xtm{par}",), wr=(f"{tag}xsq",))
        k.op("v", lambda e, r_=r, c_=co: e.reduce_sum(st1[r_, c_ + 1:c_ + 2], xsq[r_, :], AX.X),
             rd=(f"{tag}xsq",), wr=(f"{tag}s2{par}",))
        k.op("v", lambda e, p_=par, r_=r, c_=co: e.reduce_sum(st1[r_, c_ + 0:c_ + 1], xtm[p_][r_, :], AX.X),
             rd=(f"{tag}xtm{par}",), wr=(f"{tag}s1{par}",))
        k.op("v", lambda e, r_=r, c_=co: e.tensor_scalar(st1[r_, c_ + 2:c_ + 3], st1[r_, c_ + 0:c_ + 1],
                                                         1.0 / D, None, ALU.mult),
             rd=(f"{tag}s1{par}",), wr=(f"{tag}mn{par}",))
        k.op("v", lambda e, r_=r, c_=co: e.tensor_scalar(st1[r_, c_ + 3:c_ + 4], st1[r_, c_ + 1:c_ + 2],
                                                         1.0 / D, None, ALU.mult),
             rd=(f"{tag}s2{par}",), wr=(f"{tag}m2{par}",))
        k.op("v", lambda e, r_=r, c_=co: e.tensor_tensor(st1[r_, c_ + 4:c_ + 5], st1[r_, c_ + 2:c_ + 3],
                                                         st1[r_, c_ + 2:c_ + 3], ALU.mult),
             rd=(f"{tag}mn{par}",), wr=(f"{tag}mv{par}",))
        k.op("v", lambda e, r_=r, c_=co: e.tensor_tensor(st1[r_, c_ + 4:c_ + 5], st1[r_, c_ + 3:c_ + 4],
                                                         st1[r_, c_ + 4:c_ + 5], ALU.subtract),
             rd=(f"{tag}m2{par}", f"{tag}mv{par}"), wr=(f"{tag}mv{par}",))
        k.op("sc", lambda e, r_=r, c_=co: e.activation(st1[r_, c_ + 5:c_ + 6], st1[r_, c_ + 4:c_ + 5],
                                                       AF.Ln, bias=B["biasv"][r_, 0:1]),
             rd=(f"{tag}mv{par}", "biasv"), wr=(f"{tag}rs{par}",))
        k.op("sc", lambda e, r_=r, c_=co: e.activation(st1[r_, c_ + 5:c_ + 6], st1[r_, c_ + 5:c_ + 6],
                                                       AF.Exp, scale=-0.5),
             rd=(f"{tag}rs{par}",), wr=(f"{tag}rs{par}",))
        if "apply" not in LN_PARTS:
            continue
        k.op("v", lambda e, p_=par, r_=r, c_=co: e.tensor_scalar(xbtm[r_, :], xtm[p_][r_, :], st1[r_, c_ + 2:c_ + 3],
                                                                 st1[r_, c_ + 5:c_ + 6], ALU.subtract, ALU.mult),
             rd=(f"{tag}xtm{par}", f"{tag}mn{par}", f"{tag}rs{par}"), wr=(f"{tag}xb",))
        if "tr" not in LN_PARTS:
            continue
        for j in range(NDC):
            qp = j % 2
            k.op("t", lambda e, j_=j, qp_=qp: e.transpose(PST[:, 1024 * qp_:1024 * qp_ + 128],
                                                          xbtm[:, 128 * j_:128 * (j_ + 1)], identb[:, :]),
                 rd=(f"{tag}xb",), wr=(f"pst{qp}",))
            def evac(e, i_=i, j_=j, qp_=qp, halo_=halo):
                ncol = 128 if not halo_ else 3
                coff = (off + 128 * i_) if not halo_ else 0
                return e.activation(bass.AP(xn_fm, j_ * tw + coff, [[tw * NDC, 128], [1, ncol]]),
                                    PST[:, 1024 * qp_:1024 * qp_ + ncol], AF.Copy)
            k.op("sc", evac, rd=(f"pst{qp}",), wr=(f"{tag}xn",))



def emit_inproj(k, B, d):
    nc = k.nc
    xn_fm = B["xn_fm"]
    wbuf, gstage, ci, cbuf, sstage = B["wbuf"], B["gstage"], B["ci"], B["cbuf"], B["sstage"]
    dtraw, cw, cb = B["dtraw"], B["cw"], B["cb"]
    PS = [B["PS_A"], B["PS_B"]]
    PSTL = [B["PS_C"], B["PS_D"]]

    def fmslc(j, start, n):
        return xn_fm[:, j * TH + start: j * TH + start + n]

    for o in range(NOC_P):
        par = o % 2
        is_x = NOC_G <= o < NOC_G + NOC_X
        k.op("g", lambda e, o_=o, p_=par: [e.dma_start(out=wbuf[p_][:], in_=d["wip"].ap()[:, o_, :])],
             wr=(f"wb{par}",), dma=True)
        def mmg(e, o_=o, p_=par, isx=is_x):
            for j in range(NDC):
                w = wbuf[p_][:, 128 * j:128 * (j + 1)]
                st, sp = (j == 0), (j == NDC - 1)
                if isx:
                    e.matmul(PS[p_][:, 0:512], w, fmslc(j, 0, 512), start=st, stop=sp)
                    e.matmul(PS[p_][:, 512:1024], w, fmslc(j, 512, 512), start=st, stop=sp)
                    ti = e.matmul(PSTL[p_][:, 0:3], w, fmslc(j, 1024, 3), start=st, stop=sp)
                else:
                    e.matmul(PS[p_][:, 0:512], w, fmslc(j, 3, 512), start=st, stop=sp)
                    ti = e.matmul(PS[p_][:, 512:1024], w, fmslc(j, 515, 512), start=st, stop=sp)
            return ti
        wrps = (("psA0", "psA1", "psC") if par == 0 else ("psB0", "psB1", "psD")) if is_x else (("psA0", "psA1") if par == 0 else ("psB0", "psB1"))
        k.op("t", mmg, rd=(f"wb{par}", "ln0xn"), wr=wrps)
        if o < NOC_G:
            k.op("sc", lambda e, p_=par: e.activation(gstage[p_][:], PS[p_][:], AF.Copy),
                 rd=("psA0", "psA1") if par == 0 else ("psB0", "psB1"), wr=(f"gst{par}",))
            k.op("s", lambda e, o_=o, p_=par: [e.dma_start(out=d["gate"].ap()[128 * o_:128 * (o_ + 1), :],
                                                           in_=gstage[p_][:])],
                 rd=(f"gst{par}",), dma=True)
        elif is_x:
            cc = o - NOC_G
            parc = cc % 2
            def evx(e, p_=par, pc_=parc):
                e.activation(ci[pc_][:, 0:1024], PS[p_][:], AF.Copy)
                return e.activation(ci[pc_][:, 1024:1027], PSTL[p_][:, 0:3], AF.Copy)
            k.op("sc", evx, rd=("psA0", "psA1", "psC") if par == 0 else ("psB0", "psB1", "psD"), wr=(f"ci{parc}",))
            ceng = "v"
            k.op(ceng, lambda e, pc_=parc, cc_=cc: e.tensor_scalar(cbuf[pc_][:], ci[pc_][:, 0:1024],
                                                                   cw[:, 4 * cc_:4 * cc_ + 1], None, ALU.mult),
                 rd=(f"ci{parc}", "cw"), wr=(f"cb{parc}",))
            for jj in range(1, 4):
                k.op(ceng, lambda e, pc_=parc, cc_=cc, jj_=jj: e.scalar_tensor_tensor(
                         cbuf[pc_][:], ci[pc_][:, jj_:jj_ + 1024],
                         cw[:, 4 * cc_ + jj_:4 * cc_ + jj_ + 1], cbuf[pc_][:], ALU.mult, ALU.add),
                     rd=(f"ci{parc}", "cw", f"cb{parc}"), wr=(f"cb{parc}",))
            k.op("sc", lambda e, pc_=parc, cc_=cc: e.activation(sstage[pc_][:], cbuf[pc_][:], AF.Silu,
                                                                bias=cb[:, cc_:cc_ + 1]),
                 rd=(f"cb{parc}", "cbc"), wr=(f"ss{parc}",))
            if cc < 64:
                dst = d["xc"].ap()[128 * cc:128 * (cc + 1), :]
            elif cc < 72:
                dst = d["bmat"].ap()[128 * (cc - 64):128 * (cc - 63), :]
            else:
                dst = d["cmat"].ap()[128 * (cc - 72):128 * (cc - 71), :]
            k.op("s", lambda e, dst_=dst, pc_=parc: [e.dma_start(out=dst_, in_=sstage[pc_][:])],
                 rd=(f"ss{parc}",), dma=True)
        else:
            k.op("sc", lambda e, p_=par: e.activation(dtraw[:], PS[p_][:], AF.Copy),
                 rd=("psA0", "psA1") if par == 0 else ("psB0", "psB1"), wr=("dtraw",))



def emit_dtpost(k, B, d):
    nc = k.nc
    dtraw, dtclip, cs, expcs = B["dtraw"], B["dtclip"], B["cs"], B["expcs"]
    csT, dtT, cdall, cdrep, zz = B["csT"], B["dtT"], B["cdall"], B["cdrep"], B["zz"]
    t1, t2 = B["t1"], B["t2"]
    dtb, acol, identf, onesr = B["dtb"], B["acol"], B["ident_f32"], B["ones_row"]
    PSD, PSA = B["PS_D"], B["PS_A"]

    k.op("sc", lambda e: e.activation(t1[:], dtraw[:], AF.Abs, bias=dtb[:, :]),
         rd=("dtraw", "dtb"), wr=("sp_t1",))
    k.op("sc", lambda e: e.activation(t1[:], t1[:], AF.Exp, scale=-1.0), rd=("sp_t1",), wr=("sp_t1",))
    k.op("sc", lambda e: e.activation(t1[:], t1[:], AF.Ln, bias=B["biasv"][:, 1:2]),
         rd=("sp_t1", "biasv"), wr=("sp_t1",))
    k.op("sc", lambda e: e.activation(t2[:], dtraw[:], AF.Relu, bias=dtb[:, :]),
         rd=("dtraw", "dtb"), wr=("sp_t2",))
    k.op("v", lambda e: e.tensor_tensor(dtclip[:], t2[:], t1[:], ALU.add),
         rd=("sp_t1", "sp_t2"), wr=("dtclip",))
    k.op("v", lambda e: e.tensor_scalar(dtclip[:], dtclip[:], 0.001, 100.0, ALU.max, ALU.min),
         rd=("dtclip",), wr=("dtclip",))
    k.op("v", lambda e: e.tensor_scalar(t1[:], dtclip[:], acol[:, :], None, ALU.mult),
         rd=("dtclip", "acolc"), wr=("dA",))
    def mkcs(e):
        for kk in range(NCH):
            vi = e.tensor_tensor_scan(cs[:, 128 * kk:128 * (kk + 1)], t1[:, 128 * kk:128 * (kk + 1)],
                                      zz[:, :], 0.0, ALU.add, ALU.add)
        return vi
    k.op("v", mkcs, rd=("dA", "zz"), wr=("cs",))
    k.op("sc", lambda e: e.activation(expcs[:], cs[:], AF.Exp), rd=("cs",), wr=("expcs",))
    for q in range(16):
        src_ = cs if q < 8 else dtclip
        dstb = csT if q < 8 else dtT
        kx = q % 8
        qp = q % 2
        PCD = [B["PS_C"], B["PS_D"]]
        k.op("t", lambda e, s_=src_, kx_=kx, qp_=qp: e.transpose(PCD[qp_][:, 0:128],
                                                                 s_[:, 128 * kx_:128 * (kx_ + 1)], identf[:, :]),
             rd=("cs" if q < 8 else "dtclip",), wr=("psC" if qp == 0 else "psD",))
        k.op("sc", lambda e, d_=dstb, kx_=kx, qp_=qp: e.activation(d_[:, 128 * kx_:128 * (kx_ + 1)],
                                                                   PCD[qp_][:, 0:128], AF.Copy),
             rd=("psC" if qp == 0 else "psD",), wr=("csT" if q < 8 else "dtT",))
    k.op("v", lambda e: e.tensor_copy(cdall[:], bass.AP(expcs, 127, [[1024, 128], [128, 8]])),
         rd=("expcs",), wr=("cdall",))
    k.op("v", lambda e: e.tensor_tensor_scan(B["cdprod"][:], cdall[:], zz[:, 0:8], 1.0, ALU.mult, ALU.add),
         rd=("cdall",), wr=("cdprod",))
    k.op("s", lambda e: [e.dma_start(out=d["csf"].ap(), in_=cs[:])], rd=("cs",), wr=("csf_d",), dma=True)
    k.op("s", lambda e: [e.dma_start(out=d["ecsf"].ap(), in_=expcs[:])], rd=("expcs",), wr=("ecsf_d",), dma=True)
    k.op("s", lambda e: [e.dma_start(out=d["cdf"].ap(), in_=cdall[:])], rd=("cdall",), wr=("cdf_d",), dma=True)
    k.op("s", lambda e: [e.dma_start(out=B["cdflat"][0:1, :], in_=bass.AP(d["cdf"], 0, [[1024, 1], [1, 1024]]))],
         rd=("cdf_d",), wr=("cdflat",), dma=True)
    def cdmm(e):
        e.matmul(PSA[:, 0:512], onesr[:, :], B["cdflat"][0:1, 0:512], start=True, stop=True)
        return e.matmul(PSA[:, 512:1024], onesr[:, :], B["cdflat"][0:1, 512:1024], start=True, stop=True)
    k.op("t", cdmm, rd=("cdflat",), wr=("psA0", "psA1"))
    k.op("sc", lambda e: e.activation(cdrep[:], PSA[:], AF.Copy), rd=("psA0", "psA1"), wr=("cdrep",))



def emit_ffn0_early(k, B, d):
    nc = k.nc
    xn_fm = B["xn_fm"]
    PSD, PSA, PSB, PST = B["PS_D"], B["PS_A"], B["PS_B"], B["PST"]
    wbuf, gstage = B["wbuf"], B["gstage"]
    identf, onesr, onesc = B["ident_f32"], B["ones_row"], B["ones_col"]
    hfm, htm, hwrk = B["hfm"], B["htm"], B["hwrk"]
    gfm, gtm = B["gfm"], B["gtm"]
    st8 = B["st8"]
    m1w, m2w = B["m1w"], B["m2w"]
    w1T, w2T, cnt_sb = B["w1T"], B["w2T"], B["cnt_sb"]
    tmp_fm = B["tmp_fm"]
    gamB, betB = B["gamB"], B["betB"]

    def fmslc(j, start, n):
        return xn_fm[:, j * TH + start: j * TH + start + n]

    PSC = B["PS_C"]
    def gb(e):
        e.matmul(PSC[:, 0:32], onesr[:, :], B["g0gam"][0:1, :], start=True, stop=True)
        return e.matmul(PSC[:, 32:64], onesr[:, :], B["g0bet"][0:1, :], start=True, stop=True)
    k.op("t", gb, rd=("g0gam", "g0bet"), wr=("psC",))
    def gbev(e):
        e.activation(gamB[:], PSC[:, 0:32], AF.Copy)
        return e.activation(betB[:], PSC[:, 32:64], AF.Copy)
    k.op("sc", gbev, rd=("psC",), wr=("gamB",))
    k.op("v", lambda e: e.memset(cnt_sb[0:8, :], 0.0), wr=("cnt",))

    for tt in range(2):
        PCD = [B["PS_C"], B["PS_D"]]
        def hmm(e, tt_=tt):
            for j in range(NDC):
                ti = e.matmul(PCD[tt_][0:32, 0:512], B["gw1t"][:, 32 * j:32 * (j + 1)],
                              fmslc(j, 3 + 512 * tt_, 512), start=(j == 0), stop=(j == NDC - 1))
            return ti
        k.op("t", hmm, rd=("gw1t", "ln0xn"), wr=("psC" if tt == 0 else "psD",))
        k.op("sc", lambda e, tt_=tt: e.activation(hfm[0:32, 512 * tt_:512 * (tt_ + 1)], PCD[tt_][0:32, 0:512],
                                                  AF.Identity, bias=B["gb1c"][0:32, :]),
             rd=(("psC" if tt == 0 else "psD"), "gb1c"), wr=("hfm",))
    for i in range(NTC):
        k.op("t", lambda e, i_=i: e.transpose(PSA[:, 512 * (i_ % 2):512 * (i_ % 2) + 32],
                                              hfm[0:32, 128 * i_:128 * (i_ + 1)], identf[0:32, 0:32]),
             rd=("hfm",), wr=(f"psA{i % 2}",))
        k.op("sc", lambda e, i_=i: e.activation(htm[:, 32 * (i_ % 2):32 * (i_ % 2) + 32],
                                                PSA[:, 512 * (i_ % 2):512 * (i_ % 2) + 32], AF.Copy),
             rd=(f"psA{i % 2}",), wr=(f"htm{i % 2}",))
        k.op("sc", lambda e, i_=i: e.activation(hwrk[:, 0:32], htm[:, 32 * (i_ % 2):32 * (i_ % 2) + 32],
                                                AF.Square),
             rd=(f"htm{i % 2}",), wr=("hwsq",))
        k.op("v", lambda e: e.reduce_sum(st8[:, 1:2], hwrk[:, 0:32], AX.X),
             rd=("hwsq",), wr=("st8s2",))
        k.op("v", lambda e, i_=i: e.reduce_sum(st8[:, 0:1], htm[:, 32 * (i_ % 2):32 * (i_ % 2) + 32], AX.X),
             rd=(f"htm{i % 2}",), wr=("st8s1",))
        k.op("v", lambda e: e.tensor_scalar(st8[:, 2:3], st8[:, 0:1], 1.0 / 32, None, ALU.mult),
             rd=("st8s1",), wr=("st8mn",))
        k.op("v", lambda e: e.tensor_scalar(st8[:, 3:4], st8[:, 1:2], 1.0 / 32, None, ALU.mult),
             rd=("st8s2",), wr=("st8m2",))
        k.op("v", lambda e: e.tensor_tensor(st8[:, 4:5], st8[:, 2:3], st8[:, 2:3], ALU.mult),
             rd=("st8mn",), wr=("st8mv",))
        k.op("v", lambda e: e.tensor_tensor(st8[:, 4:5], st8[:, 3:4], st8[:, 4:5], ALU.subtract),
             rd=("st8m2", "st8mv"), wr=("st8mv",))
        k.op("sc", lambda e: e.activation(st8[:, 5:6], st8[:, 4:5], AF.Ln, bias=B["biasv"][:, 0:1]),
             rd=("st8mv", "biasv"), wr=("st8rs",))
        k.op("sc", lambda e: e.activation(st8[:, 5:6], st8[:, 5:6], AF.Exp, scale=-0.5),
             rd=("st8rs",), wr=("st8rs",))
        k.op("v", lambda e, i_=i: e.tensor_scalar(hwrk[:, 0:32], htm[:, 32 * (i_ % 2):32 * (i_ % 2) + 32],
                                                  st8[:, 2:3], st8[:, 5:6], ALU.subtract, ALU.mult),
             rd=(f"htm{i % 2}", "st8mn", "st8rs"), wr=(f"hr{i % 2}",))
        k.op("v", lambda e: e.tensor_tensor(hwrk[:, 0:32], hwrk[:, 0:32], gamB[:], ALU.mult),
             rd=(f"hr{i % 2}", "gamB"), wr=(f"hr{i % 2}",))
        k.op("v", lambda e: e.tensor_tensor(hwrk[:, 0:32], hwrk[:, 0:32], betB[:], ALU.add),
             rd=(f"hr{i % 2}", "gamB"), wr=(f"hr{i % 2}",))
        k.op("v", lambda e: e.tensor_scalar(hwrk[:, 0:32], hwrk[:, 0:32], 0.0, None, ALU.max),
             rd=(f"hr{i % 2}",), wr=(f"hr{i % 2}",))
        k.op("t", lambda e, i_=i: e.transpose(PSB[0:32, 512 * (i_ % 2):512 * (i_ % 2) + 128],
                                              hwrk[:, 0:32], identf[:, :]),
             rd=(f"hr{i % 2}",), wr=(f"psB{i % 2}",))
        k.op("sc", lambda e, i_=i: e.activation(hfm[0:32, 1024 + 128 * i_:1024 + 128 * (i_ + 1)],
                                                PSB[0:32, 512 * (i_ % 2):512 * (i_ % 2) + 128], AF.Copy),
             rd=(f"psB{i % 2}",), wr=("hrfm",))
    for tt in range(2):
        PCD = [B["PS_C"], B["PS_D"]]
        k.op("t", lambda e, tt_=tt: e.matmul(PCD[tt_][0:8, 0:512], B["gw2t"][0:32, :],
                                             hfm[0:32, 1024 + 512 * tt_:1024 + 512 * (tt_ + 1)], start=True, stop=True),
             rd=("gw2t", "hrfm"), wr=("psC" if tt == 0 else "psD",))
        k.op("sc", lambda e, tt_=tt: e.activation(gfm[0:8, 512 * tt_:512 * (tt_ + 1)], PCD[tt_][0:8, 0:512],
                                                  AF.Identity, bias=B["gb2c"][0:8, :]),
             rd=(("psC" if tt == 0 else "psD"), "gb2c"), wr=("gfm",))
    for i in range(NTC):
        k.op("t", lambda e, i_=i: e.transpose(PSA[:, 512 * (i_ % 2):512 * (i_ % 2) + 8],
                                              gfm[0:8, 128 * i_:128 * (i_ + 1)], identf[0:8, 0:8]),
             rd=("gfm",), wr=(f"psA{i % 2}",))
        k.op("sc", lambda e, i_=i: e.activation(gtm[:, 0:8], PSA[:, 512 * (i_ % 2):512 * (i_ % 2) + 8], AF.Copy),
             rd=(f"psA{i % 2}",), wr=("gtm",))
        k.op("v", lambda e: e.reduce_max(st8[:, 8:9], gtm[:, 0:8], AX.X), rd=("gtm",), wr=("m1s",))
        k.op("v", lambda e: e.tensor_single_scalar(m1w[:, 0:8], gtm[:, 0:8], st8[:, 8:9], ALU.is_equal),
             rd=("gtm", "m1s"), wr=("m1w",))
        k.op("v", lambda e: e.scalar_tensor_tensor(gtm[:, 8:16], m1w[:, 0:8], -1e30, gtm[:, 0:8], ALU.mult, ALU.add),
             rd=("gtm", "m1w"), wr=("gtm2",))
        k.op("v", lambda e: e.reduce_max(st8[:, 9:10], gtm[:, 8:16], AX.X), rd=("gtm2",), wr=("m2s",))
        k.op("v", lambda e: e.tensor_single_scalar(m2w[:, 0:8], gtm[:, 8:16], st8[:, 9:10], ALU.is_equal),
             rd=("gtm2", "m2s"), wr=("m2w",))
        k.op("v", lambda e: e.tensor_tensor(st8[:, 10:11], st8[:, 9:10], st8[:, 8:9], ALU.subtract),
             rd=("m1s", "m2s"), wr=("dms",))
        k.op("sc", lambda e: e.activation(st8[:, 11:12], st8[:, 10:11], AF.Exp), rd=("dms",), wr=("expd",))
        k.op("v", lambda e: e.tensor_scalar(st8[:, 12:13], st8[:, 11:12], 1.0, None, ALU.add),
             rd=("expd",), wr=("sgp",))
        k.op("v", lambda e: e.reciprocal(st8[:, 13:14], st8[:, 12:13]), rd=("sgp",), wr=("sg1",))
        k.op("v", lambda e: e.tensor_scalar(st8[:, 14:15], st8[:, 13:14], 1.0, -1.0, ALU.subtract, ALU.mult),
             rd=("sg1",), wr=("sgs",))
        def cntmm(e, i_=i):
            e.matmul(B["PS_C"][0:8, 0:1], m1w[:, 0:8], onesc[:, :], start=True, stop=True)
            return e.matmul(B["PS_C"][0:8, 1:2], m2w[:, 0:8], onesc[:, :], start=True, stop=True)
        k.op("t", cntmm, rd=("m1w", "m2w", "onesc"), wr=("psC",))
        k.op("v", lambda e: e.tensor_tensor(cnt_sb[0:8, :], cnt_sb[0:8, :], B["PS_C"][0:8, 0:2], ALU.add),
             rd=("psC", "cnt"), wr=("cnt",))
        k.op("v", lambda e: e.tensor_scalar(m1w[:, 8:16], m1w[:, 0:8], st8[:, 13:14], None, ALU.mult),
             rd=("m1w", "sg1"), wr=("w1tm",))
        k.op("v", lambda e: e.tensor_scalar(m2w[:, 8:16], m2w[:, 0:8], st8[:, 14:15], None, ALU.mult),
             rd=("m2w", "sgs"), wr=("w2tm",))
        k.op("t", lambda e, i_=i: e.transpose(PSB[0:8, 512 * (i_ % 2):512 * (i_ % 2) + 128],
                                              m1w[:, 8:16], identf[:, :]),
             rd=("w1tm",), wr=(f"psB{i % 2}",))
        k.op("sc", lambda e, i_=i: e.activation(w1T[0:8, 128 * i_:128 * (i_ + 1)],
                                                PSB[0:8, 512 * (i_ % 2):512 * (i_ % 2) + 128], AF.Copy),
             rd=(f"psB{i % 2}",), wr=("w1T",))
        k.op("t", lambda e, i_=i: e.transpose(PSB[0:8, 512 * ((i_ + 1) % 2):512 * ((i_ + 1) % 2) + 128],
                                              m2w[:, 8:16], identf[:, :]),
             rd=("w2tm",), wr=(f"psB{(i + 1) % 2}",))
        k.op("sc", lambda e, i_=i: e.activation(w2T[0:8, 128 * i_:128 * (i_ + 1)],
                                                PSB[0:8, 512 * ((i_ + 1) % 2):512 * ((i_ + 1) % 2) + 128], AF.Copy),
             rd=(f"psB{(i + 1) % 2}",), wr=("w2T",))
    
    k.op("g", lambda e: [e.dma_start(out=B["wbuf"][0][:], in_=d["a0t"].ap()[:, 0, :])], wr=("wb0",), dma=True)
    for tt in range(2):
        def tmm(e, tt_=tt):
            for j in range(NDC):
                ti = e.matmul(PSA[:, 512 * tt_:512 * (tt_ + 1)], B["wbuf"][0][:, 128 * j:128 * (j + 1)],
                              fmslc(j, 3 + 512 * tt_, 512), start=(j == 0), stop=(j == NDC - 1))
            return ti
        k.op("t", tmm, rd=("wb0", "ln0xn"), wr=(f"psA{tt}",))
        k.op("sc", lambda e, tt_=tt: e.activation(tmp_fm[:, 512 * tt_:512 * (tt_ + 1)], PSA[:, 512 * tt_:512 * (tt_ + 1)], AF.Copy),
             rd=(f"psA{tt}",), wr=("tmp",))
    for o in range(NDC):
        par = o % 2
        k.op("g", lambda e, o_=o, p_=par: [e.dma_start(out=B["wbuf"][p_][:], in_=d["w0t"].ap()[:, o_, :])],
             wr=(f"wb{par}",), dma=True)
        def mm(e, o_=o, p_=par):
            for j in range(NDC):
                w = B["wbuf"][p_][:, 128 * j:128 * (j + 1)]
                st, sp = (j == 0), (j == NDC - 1)
                e.matmul(PSA[:, 0:512] if p_ == 0 else PSB[:, 0:512], w, fmslc(j, 3, 512), start=st, stop=sp)
                ti = e.matmul(PSA[:, 512:1024] if p_ == 0 else PSB[:, 512:1024], w, fmslc(j, 515, 512), start=st, stop=sp)
            return ti
        k.op("t", mm, rd=(f"wb{par}", "ln0xn"), wr=("psA0", "psA1") if par == 0 else ("psB0", "psB1"))
        k.op("sc", lambda e, p_=par: e.activation(B["gstage"][p_][:], (PSA if p_ == 0 else PSB)[:], AF.Copy),
             rd=("psA0", "psA1") if par == 0 else ("psB0", "psB1"), wr=(f"gst{par}",))
        k.op("s", lambda e, o_=o, p_=par: [e.dma_start(out=d["ffmain"].ap()[128 * o_:128 * (o_ + 1), :],
                                                       in_=B["gstage"][p_][:])],
             rd=(f"gst{par}",), dma=True)



def emit_ssda(k, B, d):
    nc = k.nc
    PSA, PSB, PSC, PSD, PST = B["PS_A"], B["PS_B"], B["PS_C"], B["PS_D"], B["PST"]
    xch, bbf, cbf = B["xch"], B["bbf"], B["cbf"]
    csrow = B["csrow"]
    csT, dtT, cdrep = B["csT"], B["dtT"], B["cdrep"]
    gm, btok, xt, xdtp, xdd = B["gm"], B["btok"], B["xt"], B["xdtp"], B["xdd"]
    seg, m4, sm = B["seg"], B["m4"], B["sm"]
    carry, sts, yst = B["carry"], B["sts"], B["yst"]
    tri, onesr, identb = B["tri01"], B["ones_row"], B["ident_bf"]
    Dc = B["Dcols"]

    for kk in range(NCH):
        par = kk % 2
        for grp8 in range(8):
            def ldx(e, kk_=kk, g8=grp8, p_=par):
                out = []
                for q in range(8):
                    pr = 8 * g8 + q
                    out.append(e.dma_start(out=xch[p_][:, 128 * pr:128 * (pr + 1)],
                                           in_=d["xc"].ap()[128 * pr:128 * (pr + 1), 128 * kk_:128 * (kk_ + 1)]))
                return out
            k.op("s", ldx, wr=(f"xch{par}",), dma=True, n_dma=8)
        def ldb(e, kk_=kk, p_=par):
            return [e.dma_start(out=bbf[p_][:, 128 * g:128 * (g + 1)],
                                in_=d["bmat"].ap()[128 * g:128 * (g + 1), 128 * kk_:128 * (kk_ + 1)]) for g in range(8)]
        k.op("s", ldb, wr=(f"bbf{par}",), dma=True, n_dma=8)
        def ldc(e, kk_=kk, p_=par):
            return [e.dma_start(out=cbf[p_][:, 128 * g:128 * (g + 1)],
                                in_=d["cmat"].ap()[128 * g:128 * (g + 1), 128 * kk_:128 * (kk_ + 1)]) for g in range(8)]
        k.op("s", ldc, wr=(f"cbf{par}",), dma=True, n_dma=8)
        for g in range(NG):
            if g % 4 == 0:
                half = g // 4
                def ldcs(e, kk_=kk, h_=half):
                    src = bass.AP(d["csf"], 64 * h_ * 1024 + 128 * kk_, [[1024, 64], [1, 128]])
                    return [e.dma_start(out=csrow[0:1, :], in_=src)]
                k.op("s", ldcs, wr=("csrow",), dma=True)
            gp = g % 2
            k.op("t", lambda e, g_=g, p_=par, gp_=gp: e.matmul(
                     PSA[:, 512 * gp_:512 * gp_ + 128], bbf[p_][:, 128 * g_:128 * (g_ + 1)],
                     cbf[p_][:, 128 * g_:128 * (g_ + 1)], start=True, stop=True),
                 rd=(f"bbf{par}", f"cbf{par}"), wr=(f"psA{gp}",)),
            k.op("v", lambda e, gp_=gp: e.tensor_tensor(gm[gp_][:], PSA[:, 512 * gp_:512 * gp_ + 128],
                                                        tri[:, :], ALU.mult),
                 rd=(f"psA{gp}", "tri"), wr=(f"gm{gp}",))
            k.op("t", lambda e, g_=g, p_=par, gp_=gp: e.transpose(
                     PST[:, 1024 * gp_:1024 * gp_ + 128], bbf[p_][:, 128 * g_:128 * (g_ + 1)], identb[:, :]),
                 rd=(f"bbf{par}",), wr=(f"pst{gp}",))
            k.op("sc", lambda e, gp_=gp: e.activation(btok[gp_][:], PST[:, 1024 * gp_:1024 * gp_ + 128], AF.Copy),
                 rd=(f"pst{gp}",), wr=(f"btok{gp}",))
            for bq in range(4):
                h0 = 16 * g + 4 * bq
                bp = (4 * g + bq) % 2
                for pr2 in range(2):
                    jp = h0 // 2 + pr2
                    jpp = jp % 2
                    k.op("t", lambda e, jp_=jp, p_=par, jpp_=jpp: e.transpose(
                             PST[:, 1024 * jpp_ + 128:1024 * jpp_ + 256],
                             xch[p_][:, 128 * jp_:128 * (jp_ + 1)], identb[:, :]),
                         rd=(f"xch{par}",), wr=(f"pst{jpp}",))
                    k.op("sc", lambda e, jpp_=jpp: e.activation(xt[jpp_][:],
                             PST[:, 1024 * jpp_ + 128:1024 * jpp_ + 256], AF.Copy),
                         rd=(f"pst{jpp}",), wr=(f"xt{jpp}",))
                    k.op("v", lambda e, jpp_=jpp, kk_=kk, jp_=jp: e.tensor_tensor(
                             xdtp[jpp_][:].rearrange("p (a l) -> p a l", a=2),
                             xt[jpp_][:].rearrange("p (a l) -> p a l", a=2),
                             dtT[:, 128 * kk_ + 2 * jp_: 128 * kk_ + 2 * jp_ + 2].unsqueeze(2).broadcast_to((128, 2, 64)),
                             ALU.mult),
                         rd=(f"xt{jpp}", "dtT"), wr=(f"xdtp{jpp}",))
                k.op("t", lambda e, h0_=h0, bp_=bp: e.matmul(
                         PSA[:, 512 * bp_:512 * (bp_ + 1)], onesr[:, :],
                         csrow[0:1, 128 * (h0_ % 64):128 * (h0_ % 64) + 512], start=True, stop=True),
                     rd=("csrow", "onesr"), wr=(f"psA{bp}",))
                def segop(e, h0_=h0, bp_=bp, kk_=kk):
                    e.tensor_tensor(seg[bp_][:].rearrange("p (a l) -> p a l", a=4),
                                    PSA[:, 512 * bp_:512 * (bp_ + 1)].rearrange("p (a l) -> p a l", a=4),
                                    csT[:, 128 * kk_ + h0_:128 * kk_ + h0_ + 4].unsqueeze(2).broadcast_to((128, 4, 128)),
                                    ALU.subtract)
                    return e.tensor_scalar(seg[bp_][:], seg[bp_][:], 0.0, None, ALU.min)
                k.op("v", segop, rd=(f"psA{bp}", "csT"), wr=(f"seg{bp}",))
                k.op("sc", lambda e, bp_=bp: e.activation(m4[bp_][:], seg[bp_][:], AF.Exp),
                     rd=(f"seg{bp}",), wr=(f"m4{bp}",))
                k.op("v", lambda e, bp_=bp, gp_=gp: e.tensor_tensor(
                         sm[bp_][:].rearrange("p (a l) -> p a l", a=4),
                         m4[bp_][:].rearrange("p (a l) -> p a l", a=4),
                         gm[gp_][:].unsqueeze(1).broadcast_to((128, 4, 128)), ALU.mult),
                     rd=(f"m4{bp}", f"gm{gp}"), wr=(f"sm{bp}",))
                for pr2 in range(2):
                    jp = h0 // 2 + pr2
                    jpp = jp % 2
                    k.op("v", lambda e, jpp_=jpp, bp_=bp, pr2_=pr2: e.tensor_tensor(
                             xdd[jpp_][:].rearrange("p (a l) -> p a l", a=2),
                             xdtp[jpp_][:].rearrange("p (a l) -> p a l", a=2),
                             bass.AP(m4[bp_], 127 + 256 * pr2_, [[512, 128], [128, 2], [0, 64]]),
                             ALU.mult),
                         rd=(f"xdtp{jpp}", f"m4{bp}"), wr=(f"xdd{jpp}",))
                for pr2 in range(2):
                    jp = h0 // 2 + pr2
                    jpp = jp % 2
                    def ydiag(e, jpp_=jpp, bp_=bp, pr2_=pr2):
                        e.matmul(PSB[0:64, 512 * jpp_:512 * jpp_ + 128],
                                 xdtp[jpp_][:, 0:64], sm[bp_][:, 256 * pr2_:256 * pr2_ + 128], start=True, stop=True)
                        return e.matmul(PSB[64:128, 512 * jpp_:512 * jpp_ + 128],
                                        xdtp[jpp_][:, 64:128], sm[bp_][:, 256 * pr2_ + 128:256 * pr2_ + 256],
                                        start=True, stop=True)
                    k.op("t", ydiag, rd=(f"xdtp{jpp}", f"sm{bp}"), wr=(f"psB{jpp}",))
                PCD = [PSC, PSD]
                def stmm(e, bp_=bp, gp_=gp, h0_=h0):
                    for a in range(4):
                        jpp_ = ((h0_ + a) // 2) % 2
                        ti = e.matmul(PCD[bp_][:, 64 * a:64 * (a + 1)],
                                      btok[gp_][:], xdd[jpp_][:, 64 * (a % 2):64 * (a % 2) + 64],
                                      start=True, stop=True)
                    return ti
                k.op("t", stmm, rd=(f"btok{gp}", "xdd0", "xdd1"), wr=("psC" if bp == 0 else "psD",))
                k.op("sc", lambda e, bp_=bp, h0_=h0: e.activation(sts[:, 64 * h0_:64 * (h0_ + 4)],
                                                                  PCD[bp_][:, 0:256], AF.Copy),
                     rd=("psC" if bp == 0 else "psD",), wr=("sts",))
                def carryop(e, h0_=h0, kk_=kk):
                    for a in range(4):
                        h = h0_ + a
                        vi = e.scalar_tensor_tensor(carry[:, 64 * h:64 * (h + 1)], carry[:, 64 * h:64 * (h + 1)],
                                                    cdrep[:, 8 * h + kk_:8 * h + kk_ + 1],
                                                    sts[:, 64 * h:64 * (h + 1)], ALU.mult, ALU.add)
                    return vi
                k.op("v", carryop, rd=("sts", "carry", "cdrep"), wr=("carry",))
                for pr2 in range(2):
                    jp = h0 // 2 + pr2
                    jpp = jp % 2
                    ys = jp % 4
                    def ya(e, jp_=jp, jpp_=jpp, ys_=ys, p_=par):
                        return e.scalar_tensor_tensor(yst[ys_][:], xch[p_][:, 128 * jp_:128 * (jp_ + 1)],
                                                      Dc[:, jp_:jp_ + 1], PSB[:, 512 * jpp_:512 * jpp_ + 128],
                                                      ALU.mult, ALU.add)
                    k.op("v", ya, rd=(f"psB{jpp}", f"xch{par}", "Dcols"), wr=(f"yst{ys}",))
                    k.op("s", lambda e, jp_=jp, kk_=kk, ys_=ys: [e.dma_start(
                             out=d["ya"].ap()[128 * jp_:128 * (jp_ + 1), 128 * kk_:128 * (kk_ + 1)],
                             in_=yst[ys_][:])],
                         rd=(f"yst{ys}",), dma=True)
        k.op("s", lambda e, kk_=kk: [e.dma_start(out=d["std"].ap()[kk_, :, :], in_=sts[:, :])],
             rd=("sts",), dma=True)
    k.op("s", lambda e: [e.dma_start(out=bass.AP(d["ccin"], 0, [[8192, 128], [1, 8192]]), in_=carry[:, :])],
         rd=("carry",), wr=("ccin",), dma=True)
    k.op("s", lambda e: [e.dma_start(out=bass.AP(d["ccin"], 1048576, [[1, 128], [1, 1]]), in_=B["cdprod"][:, 7:8])],
         rd=("cdprod",), wr=("ccin",), dma=True)
    k.op("s", lambda e: [e.dma_start(out=bass.AP(d["ccin"], 1048704, [[2, 8], [1, 2]]), in_=B["cnt_sb"][0:8, :])],
         rd=("cnt",), wr=("ccin",), dma=True)



def build_launchA(external_scratch=True, phases=("ln", "ip", "dt", "ffn", "ssda"), dbg_xn=False):
    from contextlib import ExitStack
    nc = bass.Bass("TRN2", target_bir_lowering=False, detect_race_conditions=False)
    kind_s = "ExternalOutput" if external_scratch else "Internal"
    d = {}
    d["xin"] = nc.dram_tensor("xin", [T, D], F32, kind="ExternalInput")
    d["xhalo"] = nc.dram_tensor("xhalo", [3, D], F32, kind="ExternalInput")
    d["wip"] = nc.dram_tensor("wip", [128, NOC_P, NDC * 128], F32, kind="ExternalInput")
    d["cwc"] = nc.dram_tensor("cwc", [128, 320], F32, kind="ExternalInput")
    d["cbc"] = nc.dram_tensor("cbc", [128, 80], F32, kind="ExternalInput")
    d["dtbc"] = nc.dram_tensor("dtbc", [128, 1], F32, kind="ExternalInput")
    d["dcols"] = nc.dram_tensor("dcols", [128, 64], F32, kind="ExternalInput")
    d["gw1td"] = nc.dram_tensor("gw1td", [128, 1024], F32, kind="ExternalInput")
    d["gb1cd"] = nc.dram_tensor("gb1cd", [32, 1], F32, kind="ExternalInput")
    d["g0gamd"] = nc.dram_tensor("g0gamd", [1, 32], F32, kind="ExternalInput")
    d["g0betd"] = nc.dram_tensor("g0betd", [1, 32], F32, kind="ExternalInput")
    d["gw2td"] = nc.dram_tensor("gw2td", [32, 8], F32, kind="ExternalInput")
    d["gb2cd"] = nc.dram_tensor("gb2cd", [8, 1], F32, kind="ExternalInput")
    d["a0t"] = nc.dram_tensor("a0t", [128, 1, NDC * 128], F32, kind="ExternalInput")
    d["w0t"] = nc.dram_tensor("w0t", [128, NDC, NDC * 128], F32, kind="ExternalInput")
    d["gate"] = nc.dram_tensor("gate", [INTER, T], F32, kind=kind_s)
    d["xc"] = nc.dram_tensor("xc", [INTER, T], BF16, kind=kind_s)
    d["bmat"] = nc.dram_tensor("bmat", [NG * SS, T], BF16, kind=kind_s)
    d["cmat"] = nc.dram_tensor("cmat", [NG * SS, T], BF16, kind=kind_s)
    d["ya"] = nc.dram_tensor("ya", [INTER, T], F32, kind=kind_s)
    d["std"] = nc.dram_tensor("std", [NCH, 128, INTER], F32, kind=kind_s)
    d["csf"] = nc.dram_tensor("csf", [128, T], F32, kind=kind_s)
    d["ecsf"] = nc.dram_tensor("ecsf", [128, T], F32, kind=kind_s)
    d["cdf"] = nc.dram_tensor("cdf", [128, 8], F32, kind="Internal")
    d["ccin"] = nc.dram_tensor("ccin", [1, 1048720], F32, kind=kind_s)
    d["tmpd"] = nc.dram_tensor("tmpd", [128, T], F32, kind=kind_s)
    d["w12d"] = nc.dram_tensor("w12d", [16, T], F32, kind=kind_s)
    d["cdrepd"] = nc.dram_tensor("cdrepd", [128, T], F32, kind=kind_s)
    d["ffmain"] = nc.dram_tensor("ffmain", [D, T], F32, kind=kind_s)

    c = mkconsts(nc)
    k = K(nc)
    B = {}
    B["PS_A"] = nc.alloc_psum_tensor("PS_A", [128, 1024], F32)
    B["PS_B"] = nc.alloc_psum_tensor("PS_B", [128, 1024], F32)
    B["PS_C"] = nc.alloc_psum_tensor("PS_C", [128, 512], F32)
    B["PS_D"] = nc.alloc_psum_tensor("PS_D", [128, 512], F32)
    B["PST"] = nc.alloc_psum_tensor("PST", [128, 2048], BF16)
    per = [("ident_bf", c["ident_bf"], [128, 128], BF16, "identb"),
           ("ident_f32", c["ident_f32"], [128, 128], F32, "identf"),
           ("ones_row", c["ones_row"], [1, 128], F32, "onesr"),
           ("ones_col", c["ones_col"], [128, 1], F32, "onesc"),
           ("tri01", c["tri01"], [128, 128], F32, "tri"),
           ("acol", c["acol"], [128, 1], F32, "acolc"),
           ("cw", d["cwc"], [128, 320], F32, "cw"),
           ("cb", d["cbc"], [128, 80], F32, "cbc"),
           ("dtb", d["dtbc"], [128, 1], F32, "dtb"),
           ("Dcols", d["dcols"], [128, 64], F32, "Dcols"),
           ("gb1c", d["gb1cd"], [32, 1], F32, "gb1c"),
           ("g0gam", d["g0gamd"], [1, 32], F32, "g0gam"),
           ("g0bet", d["g0betd"], [1, 32], F32, "g0bet"),
           ("biasv", c["biasv"], [128, 4], F32, "biasv"),
           ("gw2t", d["gw2td"], [32, 8], F32, "gw2t"),
           ("gb2c", d["gb2cd"], [8, 1], F32, "gb2c")]
    for name, src, shape, dt_, key in per:
        t = nc.alloc_sbuf_tensor("sb_" + name, shape, dt_)
        B[name] = t
        k.op("s", lambda e, s_=src, t_=t: [e.dma_start(out=t_[:, :], in_=s_.ap())], wr=(key,), dma=True)
    B["gw1t"] = nc.alloc_sbuf_tensor("sb_gw1t", [128, 1024], BF16)
    k.op("g", lambda e: [e.dma_start(out=B["gw1t"][:, :], in_=d["gw1td"].ap())], wr=("gw1t",), dma=True)
    k.flush("consts")
    k.drain()
    for nm, shape, dt_ in [("csT", [128, 1024], F32), ("dtT", [128, 1024], F32),
                           ("cdrep", [128, 1024], F32), ("cdall", [128, 8], F32),
                           ("cdprod", [128, 8], F32), ("zz", [128, 128], F32),
                           ("cdflat", [1, 1024], F32), ("tmp_fm", [128, 1024], F32),
                           ("w1T", [8, 1024], F32), ("w2T", [8, 1024], F32),
                           ("cnt_sb", [8, 2], F32), ("st8", [128, 16], F32)]:
        B[nm] = nc.alloc_sbuf_tensor("sb_" + nm, shape, dt_)
    k.op("v", lambda e: e.memset(B["zz"][:], 0.0), wr=("zz",))

    stack = ExitStack()
    xn_g = nc.sbuf_tensor("xn_fm", [128, NDC * TH], BF16)
    B["xn_fm"] = stack.enter_context(xn_g)

    ln_stack = ExitStack()
    B["xtm"] = [ln_stack.enter_context(nc.sbuf_tensor(f"xtm{i}", [128, D], F32)) for i in range(2)]
    B["xsq"] = ln_stack.enter_context(nc.sbuf_tensor("xsq", [128, D], F32))
    B["xbtm"] = ln_stack.enter_context(nc.sbuf_tensor("xbtm", [128, D], BF16))
    B["st1"] = ln_stack.enter_context(nc.sbuf_tensor("st1", [128, 16], F32))
    if "ln" in phases:
        emit_ln(k, B, d["xin"], d["xhalo"], B["xn_fm"], TH, "ln0")
    if dbg_xn:
        d["xndbg"] = nc.dram_tensor("xndbg", [128, NDC * TH], BF16, kind="ExternalOutput")
        k.op("s", lambda e: [e.dma_start(out=d["xndbg"].ap(), in_=B["xn_fm"][:, :])], rd=("ln0xn",), dma=True)
    k.flush("ln0")
    k.drain()
    ln_stack.close()

    ip_stack = ExitStack()
    B["wbuf"] = [ip_stack.enter_context(nc.sbuf_tensor(f"wbuf{i}", [128, 4096], BF16)) for i in range(2)]
    B["gstage"] = [ip_stack.enter_context(nc.sbuf_tensor(f"gstage{i}", [128, 1024], F32)) for i in range(2)]
    B["ci"] = [ip_stack.enter_context(nc.sbuf_tensor(f"ci{i}", [128, 1027], F32)) for i in range(2)]
    B["cbuf"] = [ip_stack.enter_context(nc.sbuf_tensor(f"cbuf{i}", [128, 1024], F32)) for i in range(2)]
    B["sstage"] = [ip_stack.enter_context(nc.sbuf_tensor(f"sstage{i}", [128, 1024], BF16)) for i in range(2)]
    B["dtraw"] = ip_stack.enter_context(nc.sbuf_tensor("dtraw", [128, 1024], F32))
    B["dtclip"] = ip_stack.enter_context(nc.sbuf_tensor("dtclip", [128, 1024], F32))
    B["cs"] = ip_stack.enter_context(nc.sbuf_tensor("cs", [128, 1024], F32))
    B["expcs"] = ip_stack.enter_context(nc.sbuf_tensor("expcs", [128, 1024], F32))
    B["t1"] = ip_stack.enter_context(nc.sbuf_tensor("t1", [128, 1024], F32))
    B["t2"] = ip_stack.enter_context(nc.sbuf_tensor("t2", [128, 1024], F32))
    if "ip" in phases:
        emit_inproj(k, B, d)
    k.flush("inproj")
    if "dt" in phases:
        emit_dtpost(k, B, d)
    k.op("s", lambda e: [e.dma_start(out=d["cdrepd"].ap(), in_=B["cdrep"][:, :])], rd=("cdrep",), dma=True)
    k.flush("dtpost")
    k.drain()
    ip_stack.close()

    f_stack = ExitStack()
    B["wbuf"] = [f_stack.enter_context(nc.sbuf_tensor(f"fwbuf{i}", [128, 4096], BF16)) for i in range(2)]
    B["gstage"] = [f_stack.enter_context(nc.sbuf_tensor(f"fgst{i}", [128, 1024], F32)) for i in range(2)]
    B["hfm"] = f_stack.enter_context(nc.sbuf_tensor("hfm", [32, 2048], F32))
    B["htm"] = f_stack.enter_context(nc.sbuf_tensor("htm", [128, 64], F32))
    B["hwrk"] = f_stack.enter_context(nc.sbuf_tensor("hwrk", [128, 32], F32))
    B["gfm"] = f_stack.enter_context(nc.sbuf_tensor("gfm", [8, 1024], F32))
    B["gtm"] = f_stack.enter_context(nc.sbuf_tensor("gtm", [128, 16], F32))
    B["m1w"] = f_stack.enter_context(nc.sbuf_tensor("m1w", [128, 16], F32))
    B["m2w"] = f_stack.enter_context(nc.sbuf_tensor("m2w", [128, 16], F32))
    B["gamB"] = f_stack.enter_context(nc.sbuf_tensor("gamB", [128, 32], F32))
    B["betB"] = f_stack.enter_context(nc.sbuf_tensor("betB", [128, 32], F32))
    if "ffn" in phases:
        emit_ffn0_early(k, B, d)
    k.op("s", lambda e: [e.dma_start(out=d["tmpd"].ap(), in_=B["tmp_fm"][:, :])], rd=("tmp",), dma=True)
    k.op("s", lambda e: [e.dma_start(out=d["w12d"].ap()[0:8, :], in_=B["w1T"][0:8, :])], rd=("w1T",), dma=True)
    k.op("s", lambda e: [e.dma_start(out=d["w12d"].ap()[8:16, :], in_=B["w2T"][0:8, :])], rd=("w2T",), dma=True)
    k.flush("ffn0e")
    k.drain()
    f_stack.close()
    stack.close()

    s_stack = ExitStack()
    B["xch"] = [s_stack.enter_context(nc.sbuf_tensor(f"xch{i}", [128, INTER], BF16)) for i in range(2)]
    B["bbf"] = [s_stack.enter_context(nc.sbuf_tensor(f"bbf{i}", [128, 1024], BF16)) for i in range(2)]
    B["cbf"] = [s_stack.enter_context(nc.sbuf_tensor(f"cbf{i}", [128, 1024], BF16)) for i in range(2)]
    B["csrow"] = s_stack.enter_context(nc.sbuf_tensor("csrow", [1, 8192], F32))
    B["gm"] = [s_stack.enter_context(nc.sbuf_tensor(f"gmb{i}", [128, 128], BF16)) for i in range(2)]
    B["btok"] = [s_stack.enter_context(nc.sbuf_tensor(f"btok{i}", [128, 128], BF16)) for i in range(2)]
    B["xt"] = [s_stack.enter_context(nc.sbuf_tensor(f"xtb{i}", [128, 128], BF16)) for i in range(2)]
    B["xdtp"] = [s_stack.enter_context(nc.sbuf_tensor(f"xdtp{i}", [128, 128], BF16)) for i in range(2)]
    B["xdd"] = [s_stack.enter_context(nc.sbuf_tensor(f"xdd{i}", [128, 128], BF16)) for i in range(2)]
    B["seg"] = [s_stack.enter_context(nc.sbuf_tensor(f"seg{i}", [128, 512], F32)) for i in range(2)]
    B["m4"] = [s_stack.enter_context(nc.sbuf_tensor(f"m4{i}", [128, 512], BF16)) for i in range(2)]
    B["sm"] = [s_stack.enter_context(nc.sbuf_tensor(f"smb{i}", [128, 512], BF16)) for i in range(2)]
    B["carry"] = s_stack.enter_context(nc.sbuf_tensor("carry", [128, INTER], F32))
    B["sts"] = s_stack.enter_context(nc.sbuf_tensor("sts", [128, INTER], F32))
    B["yst"] = [s_stack.enter_context(nc.sbuf_tensor(f"yst{i}", [128, 128], F32)) for i in range(4)]
    k.op("v", lambda e: e.memset(B["carry"][:], 0.0), wr=("carry",))
    if "ssda" in phases:
        emit_ssda(k, B, d)
    k.flush("ssda")
    k.drain()
    s_stack.close()
    return nc, k, d



def prep_launchA_inputs(ins, core):
    x = np.asarray(ins["L0_attn_hidden_states"]).reshape(SEQ, D)
    t0 = T * core
    xin = np.ascontiguousarray(x[t0:t0 + T])
    xhalo = np.zeros((3, D), np.float32) if core == 0 else np.ascontiguousarray(x[t0 - 3:t0])
    W = np.asarray(ins["L0_attn_in_proj_w"])
    m = {
        "xin": xin, "xhalo": xhalo,
        "wip": tile_w(W),
        "cwc": np.ascontiguousarray(np.asarray(ins["L0_attn_conv_w"])[:, 0, :].reshape(80, 128, 4).transpose(1, 0, 2).reshape(128, 320)),
        "cbc": col_layout(ins["L0_attn_conv_b"]),
        "dtbc": np.ascontiguousarray(np.asarray(ins["L0_attn_dt_bias"]).reshape(128, 1)),
        "gw1td": np.ascontiguousarray(np.asarray(ins["L0_ffn_gw1"]).T.reshape(32, 128, 32).transpose(1, 0, 2).reshape(128, 1024)),
        "gb1cd": np.ascontiguousarray(np.asarray(ins["L0_ffn_gb1"]).reshape(32, 1)),
        "g0gamd": np.ascontiguousarray(np.asarray(ins["L0_ffn_ln_gamma"]).reshape(1, 32)),
        "g0betd": np.ascontiguousarray(np.asarray(ins["L0_ffn_ln_beta"]).reshape(1, 32)),
        "gw2td": np.ascontiguousarray(np.asarray(ins["L0_ffn_gw2"]).T),
        "gb2cd": np.ascontiguousarray(np.asarray(ins["L0_ffn_gb2"]).reshape(8, 1)),
        "a0t": tile_w(np.ascontiguousarray(np.asarray(ins["L0_ffn_lora_A"]).transpose(1, 0, 2).reshape(D, 128)).T),
        "w0t": tile_w(np.asarray(ins["L0_ffn_weight"])),
    }
    Dv = np.asarray(ins["L0_attn_D"])
    dc = np.zeros((128, 64), np.float32)
    dc[0:64, :] = Dv[0::2][None, :]
    dc[64:128, :] = Dv[1::2][None, :]
    m["dcols"] = dc
    return m



def emit_prefix(k, B, d):
    nc = k.nc
    drow, dprow, fall = B["drow"], B["dprow"], B["fall"]
    k.op("s", lambda e: [e.dma_start(out=drow[0:1, :], in_=bass.AP(d["ccg"], 1048576, [[1048720, 8], [1, 128]]))],
         wr=("drow",), dma=True)
    k.op("v", lambda e: e.tensor_tensor(dprow[0:1, :], drow[0:1, :], B["mrow"][0:1, :], ALU.mult),
         rd=("drow", "mrow"), wr=("dprow",))
    k.op("v", lambda e: e.tensor_tensor(dprow[0:1, :], dprow[0:1, :], B["invmrow"][0:1, :], ALU.add),
         rd=("dprow", "mrow"), wr=("dprow",))
    sfx = B["sfxrow"]
    k.op("v", lambda e: e.memset(sfx[0:1, :], 1.0), wr=("sfx",))
    for b in range(7, -1, -1):
        k.op("v", lambda e, b_=b: e.tensor_tensor(fall[0:1, 128 * b_:128 * (b_ + 1)], sfx[0:1, :],
                                                  B["mrow"][0:1, 128 * b_:128 * (b_ + 1)], ALU.mult),
             rd=("sfx", "mrow"), wr=("fall",))
        if b > 0:
            k.op("v", lambda e, b_=b: e.tensor_tensor(sfx[0:1, :], sfx[0:1, :],
                                                      dprow[0:1, 128 * b_:128 * (b_ + 1)], ALU.mult),
                 rd=("sfx", "dprow"), wr=("sfx",))
    for hh in range(2):
        k.op("v", lambda e, hh_=hh: e.memset(B["prev"][hh_][:], 0.0), wr=(f"prev{hh}",))
    for b in range(8):
        bp = b % 2
        k.op("t", lambda e, b_=b, bp_=bp: e.matmul(B["PS_A"][:, 512 * bp_:512 * bp_ + 128], B["ones_row"][:, :],
                                                   fall[0:1, 128 * b_:128 * (b_ + 1)], start=True, stop=True),
             rd=("fall", "onesr"), wr=(f"psA{bp}",))
        k.op("sc", lambda e, bp_=bp: e.activation(B["fbb"][bp_][:], B["PS_A"][:, 512 * bp_:512 * bp_ + 128], AF.Copy),
             rd=(f"psA{bp}",), wr=(f"fbb{bp}",))
        for hh in range(2):
            k.op("s", lambda e, b_=b, hh_=hh, bp_=bp: [e.dma_start(
                     out=B["lbuf"][bp_][:, :],
                     in_=bass.AP(d["ccg"], b_ * 1048720 + 4096 * hh_, [[8192, 128], [1, 4096]]))],
                 wr=(f"lbuf{bp}",), dma=True, tag=f"lbuf{bp}")
            k.op("v", lambda e, b_=b, hh_=hh, bp_=bp: e.tensor_tensor(
                     B["ptmp"][:].rearrange("p (h q) -> p h q", h=64),
                     B["lbuf"][bp_][:].rearrange("p (h q) -> p h q", h=64),
                     B["fbb"][bp_][:, 64 * hh_:64 * hh_ + 64].unsqueeze(2).broadcast_to((128, 64, 64)),
                     ALU.mult),
                 rd=(f"lbuf{bp}", f"fbb{bp}"), wr=("ptmp",))
            k.op("v", lambda e, hh_=hh: e.tensor_tensor(B["prev"][hh_][:], B["prev"][hh_][:], B["ptmp"][:], ALU.add),
                 rd=("ptmp", f"prev{hh}"), wr=(f"prev{hh}",))
    k.op("s", lambda e: [e.dma_start(out=B["cntg"][0:8, :], in_=bass.AP(d["ccg"], 1048704, [[2, 8], [1048720, 8], [1, 2]]))],
         wr=("cntg",), dma=True)
    k.op("v", lambda e: e.reduce_sum(B["cnt2"][0:8, 0:2],
                                     bass.AP(B["cntg"], 0, [[16, 8], [1, 2], [2, 8]]), AX.X),
         rd=("cntg",), wr=("cnt2",))
    k.op("v", lambda e: e.tensor_scalar(B["cnt2"][0:8, 2:4], B["cnt2"][0:8, 0:2], CAP + 0.5, 1.0 / 16.0,
                                        ALU.is_le, ALU.mult),
         rd=("cnt2",), wr=("a01",))
    k.op("v", lambda e: e.tensor_scalar(B["wT"][0:8, :], B["w12"][0:8, 0:1024], B["cnt2"][0:8, 2:3], None, ALU.mult),
         rd=("w12", "a01"), wr=("wTt",))
    k.op("v", lambda e: e.scalar_tensor_tensor(B["wT"][0:8, :], B["w12"][0:8, 1024:2048], B["cnt2"][0:8, 3:4],
                                               B["wT"][0:8, :], ALU.mult, ALU.add),
         rd=("w12", "a01", "wTt"), wr=("wTt",))


def emit_ssdb(k, B, d):
    nc = k.nc
    PSA, PSB, PSC, PSD = B["PS_A"], B["PS_B"], B["PS_C"], B["PS_D"]
    onesr = B["ones_row"]
    ecsrow, cbufB = B["ecsrow"], B["cbufB"]
    stbuf = B["stbuf"]
    prev, prevb = B["prev"], B["prevb"]
    yab, gateb, ygb = B["yab"], B["gateb"], B["ygb"]
    sq, ynst = B["sqb"], B["ynst"]
    csc = B["csc"]
    nw = B["nwcols"]
    PCD = [PSC, PSD]
    for hh in range(2):
        for kk in range(NCH):
            par = kk % 2
            k.op("s", lambda e, hh_=hh, kk_=kk, p_=par: [e.dma_start(
                     out=stbuf[p_][:, :], in_=d["std"].ap()[kk_, :, 4096 * hh_:4096 * (hh_ + 1)])],
                 wr=(f"stbuf{par}",), dma=True)
            def ldc(e, hh_=hh, kk_=kk, p_=par):
                return [e.dma_start(out=cbufB[p_][:, 128 * g:128 * (g + 1)],
                                    in_=d["cmat"].ap()[128 * (4 * hh_ + g):128 * (4 * hh_ + g + 1),
                                                       128 * kk_:128 * (kk_ + 1)]) for g in range(4)]
            k.op("s", ldc, wr=(f"cbufB{par}",), dma=True, n_dma=4)
            k.op("s", lambda e, hh_=hh, kk_=kk: [e.dma_start(
                     out=ecsrow[0:1, :], in_=bass.AP(d["ecsf"], 64 * hh_ * 1024 + 128 * kk_, [[1024, 64], [1, 128]]))],
                 wr=("ecsrow",), dma=True)
            def ldya(e, hh_=hh, kk_=kk, p_=par):
                return [e.dma_start(out=yab[p_][:, 128 * jp:128 * (jp + 1)],
                                    in_=d["ya"].ap()[128 * (32 * hh_ + jp):128 * (32 * hh_ + jp + 1),
                                                     128 * kk_:128 * (kk_ + 1)]) for jp in range(32)]
            k.op("s", ldya, wr=("yab",), dma=True, n_dma=32)
            def ldg(e, hh_=hh, kk_=kk, p_=par):
                return [e.dma_start(out=gateb[p_][:, 128 * jp:128 * (jp + 1)],
                                    in_=d["gate"].ap()[128 * (32 * hh_ + jp):128 * (32 * hh_ + jp + 1),
                                                       128 * kk_:128 * (kk_ + 1)]) for jp in range(32)]
            k.op("s", ldg, wr=("gateb",), dma=True, n_dma=32)
            k.op("v", lambda e, hh_=hh: e.tensor_copy(prevb[:, :], prev[hh_][:, :]),
                 rd=(f"prev{hh}",), wr=("prevb",))
            for h in range(64):
                hp = h % 2
                jp = h // 2
                jpp = jp % 2
                k.op("t", lambda e, h_=h, hp_=hp: e.matmul(PSA[:, 512 * hp_:512 * hp_ + 128], onesr[:, :],
                                                           ecsrow[0:1, 128 * h_:128 * (h_ + 1)], start=True, stop=True),
                     rd=("ecsrow", "onesr"), wr=(f"psA{hp}",))
                gw = (h // 16)
                k.op("v", lambda e, h_=h, hp_=hp, gw_=gw, p_=par: e.tensor_tensor(
                         csc[hp_][:, :], cbufB[p_][:, 128 * gw_:128 * (gw_ + 1)],
                         PSA[:, 512 * hp_:512 * hp_ + 128], ALU.mult),
                     rd=(f"psA{hp}", f"cbufB{par}"), wr=(f"csc{hp}",))
                def yoff(e, h_=h, hp_=hp):
                    return e.matmul(PSB[64 * (h_ % 2):64 * (h_ % 2) + 64, 512 * ((h_ // 2) % 2):512 * ((h_ // 2) % 2) + 128],
                                    prevb[:, 64 * h_:64 * (h_ + 1)], csc[hp_][:, :], start=True, stop=True)
                k.op("t", yoff, rd=("prevb", f"csc{hp}"), wr=(f"psB{jpp}",))
                if h % 2 == 1:
                    k.op("v", lambda e, jp_=jp, jpp_=jpp, p_=par: e.tensor_tensor(
                             ygb[:, 128 * jp_:128 * (jp_ + 1)], yab[p_][:, 128 * jp_:128 * (jp_ + 1)],
                             PSB[:, 512 * jpp_:512 * jpp_ + 128], ALU.add),
                         rd=(f"psB{jpp}", "yab"), wr=("ygb",))
                    k.op("sc", lambda e, jp_=jp, p_=par: e.activation(
                             sq[:, 128 * (jp_ % 2):128 * (jp_ % 2) + 128],
                             gateb[p_][:, 128 * jp_:128 * (jp_ + 1)], AF.Silu),
                         rd=("gateb",), wr=(f"sgate{jp % 2}",))
                    k.op("v", lambda e, jp_=jp: e.tensor_tensor(
                             ygb[:, 128 * jp_:128 * (jp_ + 1)], ygb[:, 128 * jp_:128 * (jp_ + 1)],
                             sq[:, 128 * (jp_ % 2):128 * (jp_ % 2) + 128], ALU.mult),
                         rd=("ygb", f"sgate{jp % 2}"), wr=("ygb",))
            def scan(e, hh_=hh, kk_=kk, p_=par):
                for h in range(64):
                    hg = 64 * hh_ + h
                    vi = e.scalar_tensor_tensor(prev[hh_][:, 64 * h:64 * (h + 1)], prev[hh_][:, 64 * h:64 * (h + 1)],
                                                B["cdrepB"][:, 8 * hg + kk_:8 * hg + kk_ + 1],
                                                stbuf[p_][:, 64 * h:64 * (h + 1)], ALU.mult, ALU.add)
                return vi
            k.op("v", scan, rd=(f"prev{hh}", f"stbuf{par}", "cdrepB", "prevb"), wr=(f"prev{hh}",))
            for g in range(4):
                for q in range(8):
                    jp = 8 * g + q
                    k.op("sc", lambda e, jp_=jp, q_=q: e.activation(sq[:, 256 + 128 * (q_ % 2):256 + 128 * (q_ % 2) + 128],
                                                                   ygb[:, 128 * jp_:128 * (jp_ + 1)], AF.Square),
                         rd=("ygb",), wr=(f"sqq{q % 2}",))
                    k.op("t", lambda e, q_=q, g_=g: e.matmul(PCD[g_ % 2][0:1, 0:128], B["ones_col"][:, :],
                                                             sq[:, 256 + 128 * (q_ % 2):256 + 128 * (q_ % 2) + 128],
                                                             start=(q_ == 0), stop=(q_ == 7)),
                         rd=(f"sqq{q % 2}", "onesc"), wr=(f"ps{'C' if g % 2 == 0 else 'D'}",))
                k.op("sc", lambda e, g_=g: e.activation(B["rstdrow"][0:1, :], PCD[g_ % 2][0:1, 0:128],
                                                        AF.Ln, scale=1.0 / 1024.0, bias=B["biasv"][0:1, 0:1]),
                     rd=(f"ps{'C' if g % 2 == 0 else 'D'}", "biasv"), wr=("rstdrow",))
                k.op("sc", lambda e: e.activation(B["rstdrow"][0:1, :], B["rstdrow"][0:1, :], AF.Exp, scale=-0.5),
                     rd=("rstdrow",), wr=("rstdrow",))
                k.op("t", lambda e, g_=g: e.matmul(PCD[(g_ + 1) % 2][:, 0:128], onesr[:, :], B["rstdrow"][0:1, :],
                                                   start=True, stop=True),
                     rd=("rstdrow", "onesr"), wr=(f"ps{'C' if (g + 1) % 2 == 0 else 'D'}",))
                for q in range(8):
                    jp = 8 * g + q
                    hg2 = 32 * hh + jp
                    k.op("v", lambda e, jp_=jp, g_=g: e.tensor_tensor(
                             ynst[:, 128 * (jp_ % 2):128 * (jp_ % 2) + 128], ygb[:, 128 * jp_:128 * (jp_ + 1)],
                             PCD[(g_ + 1) % 2][:, 0:128], ALU.mult),
                         rd=("ygb", f"ps{'C' if (g + 1) % 2 == 0 else 'D'}"), wr=(f"ynst{jp % 2}",))
                    k.op("v", lambda e, jp_=jp, hg2_=hg2: e.tensor_scalar(
                             B["ynstb"][:, 128 * (jp_ % 2):128 * (jp_ % 2) + 128],
                             ynst[:, 128 * (jp_ % 2):128 * (jp_ % 2) + 128],
                             nw[:, hg2_:hg2_ + 1], None, ALU.mult),
                         rd=(f"ynst{jp % 2}", "nwcols"), wr=(f"ynstb{jp % 2}",))
                    k.op("s", lambda e, jp_=jp, kk_=kk, hg2_=hg2: [e.dma_start(
                             out=d["yn"].ap()[128 * hg2_:128 * (hg2_ + 1), 128 * kk_:128 * (kk_ + 1)],
                             in_=B["ynstb"][:, 128 * (jp_ % 2):128 * (jp_ % 2) + 128])],
                         rd=(f"ynstb{jp % 2}",), dma=True, tag=f"ynstb{jp % 2}")


def emit_outproj(k, B, d):
    nc = k.nc
    PSA, PSB = B["PS_A"], B["PS_B"]
    ynb, wbuf = B["ynb"], B["wbuf"]
    for tt in range(2):
        pt = tt % 2
        def ldyn(e, tt_=tt):
            return [e.dma_start(out=ynb[:, 512 * j:512 * (j + 1)],
                                in_=d["yn"].ap()[128 * j:128 * (j + 1), 512 * tt_:512 * (tt_ + 1)])
                    for j in range(64)]
        k.op("s", ldyn, wr=("ynb",), dma=True, n_dma=64)
        for o in range(NDC):
            par = o % 2
            k.op("g", lambda e, o_=o, p_=par: [e.dma_start(out=wbuf[p_][:, :], in_=d["wopt"].ap()[:, o_, :])],
                 wr=(f"wb{par}",), dma=True)
            def mm(e, o_=o, p_=par, tt_=tt, pt_=pt):
                for j in range(64):
                    ti = e.matmul((PSA if p_ == 0 else PSB)[:, 512 * tt_:512 * (tt_ + 1)],
                                  wbuf[p_][:, 128 * j:128 * (j + 1)], ynb[:, 512 * j:512 * (j + 1)],
                                  start=(j == 0), stop=(j == 63))
                return ti
            k.op("t", mm, rd=(f"wb{par}", "ynb"), wr=(f"psA{tt}",) if par == 0 else (f"psB{tt}",))
            k.op("sc", lambda e, o_=o, p_=par, tt_=tt: e.activation(
                     B["sgst"][p_][:, :], (PSA if p_ == 0 else PSB)[:, 512 * tt_:512 * (tt_ + 1)], AF.Copy),
                 rd=(f"psA{tt}",) if par == 0 else (f"psB{tt}",), wr=(f"sgst{par}",))
            k.op("s", lambda e, o_=o, p_=par, tt_=tt: [e.dma_start(
                     out=d["sgd"].ap()[128 * o_:128 * (o_ + 1), 512 * tt_:512 * (tt_ + 1)],
                     in_=B["sgst"][p_][:, :])],
                 rd=(f"sgst{par}",), dma=True)


def emit_ffn0_late(k, B, d):
    nc = k.nc
    PSA, PSB, PSC, PSD = B["PS_A"], B["PS_B"], B["PS_C"], B["PS_D"]
    e8, wT = B["e8sb"], B["wT"]
    tmpw, bstk = B["tmpw"], B["bstk"]
    mainb = B["mainb"]
    PCD = [PSC, PSD]
    for tt in range(2):
        k.op("t", lambda e, tt_=tt: e.matmul(PCD[tt_][:, 0:512], e8[0:8, :], wT[0:8, 512 * tt_:512 * (tt_ + 1)],
                                             start=True, stop=True),
             rd=("e8c", "wTt"), wr=("psC" if tt == 0 else "psD",))
        k.op("v", lambda e, tt_=tt: e.tensor_tensor(tmpw[:, 512 * tt_:512 * (tt_ + 1)],
                                                    B["tmpsb"][:, 512 * tt_:512 * (tt_ + 1)],
                                                    PCD[tt_][:, 0:512], ALU.mult),
             rd=(("psC" if tt == 0 else "psD"), "tmpsb"), wr=("tmpw",))
    k.op("v", lambda e: e.tensor_copy(B["tmpwb"][:, :], tmpw[:, :]), rd=("tmpw",), wr=("tmpwb",))
    for o in range(NDC):
        par = o % 2
        PS = PSA if par == 0 else PSB
        k.op("s", lambda e, o_=o, p_=par: [e.dma_start(out=mainb[p_][:, :],
                                                       in_=d["ffmain"].ap()[128 * o_:128 * (o_ + 1), :])],
             wr=(f"mainb{par}",), dma=True)
        def lmm(e, o_=o, p_=par):
            e.matmul((PSA if p_ == 0 else PSB)[:, 0:512], bstk[:, 128 * o_:128 * (o_ + 1)],
                     B["tmpwb"][:, 0:512], start=True, stop=True)
            return e.matmul((PSA if p_ == 0 else PSB)[:, 512:1024], bstk[:, 128 * o_:128 * (o_ + 1)],
                            B["tmpwb"][:, 512:1024], start=True, stop=True)
        k.op("t", lmm, rd=("bstk", "tmpwb"), wr=("psA0", "psA1") if par == 0 else ("psB0", "psB1"))
        k.op("v", lambda e, p_=par: e.tensor_tensor(B["g0st"][p_][:, :], mainb[p_][:, :],
                                                    (PSA if p_ == 0 else PSB)[:, :], ALU.add),
             rd=(("psA0", "psA1", f"mainb{par}") if par == 0 else ("psB0", "psB1", f"mainb{par}")),
             wr=(f"g0st{par}",))
        k.op("sc", lambda e, p_=par: e.activation(B["g0st"][p_][:, :], B["g0st"][p_][:, :], AF.Silu),
             rd=(f"g0st{par}",), wr=(f"g0st{par}",))
        k.op("s", lambda e, o_=o, p_=par: [e.dma_start(out=B["s0b"][p_][:, :],
                                                       in_=d["sgd"].ap()[128 * o_:128 * (o_ + 1), :])],
             wr=(f"s0b{par}",), dma=True)
        k.op("v", lambda e, p_=par: e.tensor_tensor(B["g0st"][p_][:, :], B["g0st"][p_][:, :],
                                                    B["s0b"][p_][:, :], ALU.mult),
             rd=(f"g0st{par}", f"s0b{par}"), wr=(f"g0st{par}",))
        k.op("s", lambda e, o_=o, p_=par: [e.dma_start(out=d["sgd"].ap()[128 * o_:128 * (o_ + 1), :],
                                                       in_=B["g0st"][p_][:, :])],
             rd=(f"g0st{par}",), dma=True, tag=f"g0out{par}")


def emit_combine(k, B, d, src_res, dst, sgsrc):
    nc = k.nc
    PSC, PSD = B["PS_C"], B["PS_D"]
    PCD = [PSC, PSD]
    xres, x1st = B["xres"], B["x1st"]
    sgt = B["sgtile"]
    for i in range(NTC):
        par = i % 2
        k.op("s", lambda e, i_=i, p_=par: [e.dma_start(out=xres[p_][:, :], in_=src_res.ap()[128 * i_:128 * (i_ + 1), :])],
             wr=(f"xres{par}",), dma=True)
        for o in range(NDC):
            op_ = o % 2
            k.op("s", lambda e, i_=i, o_=o, op2=op_: [e.dma_start(
                     out=sgt[op2][:, :], in_=sgsrc.ap()[128 * o_:128 * (o_ + 1), 128 * i_:128 * (i_ + 1)])],
                 wr=(f"sgtile{op_}",), dma=True)
            k.op("t", lambda e, op2=op_: e.transpose(PCD[op2][:, 0:128], sgt[op2][:, :], B["ident_f32"][:, :]),
                 rd=(f"sgtile{op_}", "identf"), wr=("psC" if op_ == 0 else "psD",))
            k.op("sc", lambda e, o_=o, op2=op_: e.activation(x1st[:, 128 * o_:128 * (o_ + 1)], PCD[op2][:, 0:128], AF.Copy),
                 rd=("psC" if op_ == 0 else "psD",), wr=("x1st",))
        k.op("v", lambda e, p_=par: e.tensor_tensor(x1st[:, :], x1st[:, :], xres[p_][:, :], ALU.add),
             rd=(f"xres{par}", "x1st"), wr=("x1st",))
        k.op("s", lambda e, i_=i: [e.dma_start(out=dst.ap()[128 * i_:128 * (i_ + 1), :], in_=x1st[:, :])],
             rd=("x1st",), dma=True, tag="x1out")


def build_launchB(external_scratch=True):
    from contextlib import ExitStack
    nc = bass.Bass("TRN2", target_bir_lowering=False, detect_race_conditions=False)
    kind_s = "ExternalOutput" if external_scratch else "Internal"
    kin = "ExternalInput" if external_scratch else "Internal"
    d = {}
    d["xin"] = nc.dram_tensor("xin", [T, D], F32, kind="ExternalInput")
    d["gate"] = nc.dram_tensor("gate", [INTER, T], F32, kind=kin)
    d["cmat"] = nc.dram_tensor("cmat", [NG * SS, T], BF16, kind=kin)
    d["ya"] = nc.dram_tensor("ya", [INTER, T], F32, kind=kin)
    d["std"] = nc.dram_tensor("std", [NCH, 128, INTER], F32, kind=kin)
    d["ecsf"] = nc.dram_tensor("ecsf", [128, T], F32, kind=kin)
    d["ccg"] = nc.dram_tensor("ccg", [8, 1048720], F32, kind=kin)
    d["tmpd"] = nc.dram_tensor("tmpd", [128, T], F32, kind=kin)
    d["w12d"] = nc.dram_tensor("w12d", [16, T], F32, kind=kin)
    d["cdrepd"] = nc.dram_tensor("cdrepd", [128, T], F32, kind=kin)
    d["ffmain"] = nc.dram_tensor("ffmain", [D, T], F32, kind=kin)
    d["mrow"] = nc.dram_tensor("mrow", [1, 1024], F32, kind="ExternalInput")
    d["invmrow"] = nc.dram_tensor("invmrow", [1, 1024], F32, kind="ExternalInput")
    d["nwc"] = nc.dram_tensor("nwc", [128, 64], F32, kind="ExternalInput")
    d["wopt"] = nc.dram_tensor("wopt", [128, NDC, 64 * 128], F32, kind="ExternalInput")
    d["bstk0"] = nc.dram_tensor("bstk0", [128, D], F32, kind="ExternalInput")
    d["yn"] = nc.dram_tensor("yn", [INTER, T], BF16, kind=kind_s)
    d["sgd"] = nc.dram_tensor("sgd", [D, T], F32, kind=kind_s)
    d["x1"] = nc.dram_tensor("x1", [T, D], F32, kind="ExternalOutput")

    c = mkconsts(nc)
    k = K(nc)
    B = {}
    B["PS_A"] = nc.alloc_psum_tensor("PS_A", [128, 1024], F32)
    B["PS_B"] = nc.alloc_psum_tensor("PS_B", [128, 1024], F32)
    B["PS_C"] = nc.alloc_psum_tensor("PS_C", [128, 512], F32)
    B["PS_D"] = nc.alloc_psum_tensor("PS_D", [128, 512], F32)
    B["PST"] = nc.alloc_psum_tensor("PST", [128, 2048], BF16)
    per = [("ident_f32", c["ident_f32"], [128, 128], F32, "identf"),
           ("ones_row", c["ones_row"], [1, 128], F32, "onesr"),
           ("ones_col", c["ones_col"], [128, 1], F32, "onesc"),
           ("biasv", c["biasv"], [128, 4], F32, "biasv"),
           ("e8sb", c["e8"], [8, 128], F32, "e8c"),
           ("mrow", d["mrow"], [1, 1024], F32, "mrow"),
           ("invmrow", d["invmrow"], [1, 1024], F32, "mrow2"),
           ("nwcols", d["nwc"], [128, 64], F32, "nwcols")]
    for name, srcd, shape, dt_, key in per:
        t = nc.alloc_sbuf_tensor("sb_" + name, shape, dt_)
        B[name] = t
        k.op("s", lambda e, s_=srcd, t_=t: [e.dma_start(out=t_[:, :], in_=s_.ap())], wr=(key,), dma=True)
    B["cdrepB"] = nc.alloc_sbuf_tensor("cdrepB", [128, 1024], F32)
    k.op("s", lambda e: [e.dma_start(out=B["cdrepB"][:, :], in_=d["cdrepd"].ap())], wr=("cdrepB",), dma=True)
    B["w12"] = nc.alloc_sbuf_tensor("w12", [8, 2048], F32)
    k.op("s", lambda e: [e.dma_start(out=B["w12"][0:8, 0:1024], in_=d["w12d"].ap()[0:8, :]),
                         e.dma_start(out=B["w12"][0:8, 1024:2048], in_=d["w12d"].ap()[8:16, :])],
         wr=("w12",), dma=True, n_dma=2)
    B["tmpsb"] = nc.alloc_sbuf_tensor("tmpsb", [128, 1024], F32)
    k.op("s", lambda e: [e.dma_start(out=B["tmpsb"][:, :], in_=d["tmpd"].ap())], wr=("tmpsb",), dma=True)
    k.flush("constsB")
    k.drain()
    B["wT"] = nc.alloc_sbuf_tensor("wT", [8, 1024], F32)
    B["drow"] = nc.alloc_sbuf_tensor("drow", [1, 1024], F32)
    B["dprow"] = nc.alloc_sbuf_tensor("dprow", [1, 1024], F32)
    B["fall"] = nc.alloc_sbuf_tensor("fall", [1, 1024], F32)
    B["sfxrow"] = nc.alloc_sbuf_tensor("sfxrow", [1, 128], F32)
    B["cntg"] = nc.alloc_sbuf_tensor("cntg", [8, 16], F32)
    B["cnt2"] = nc.alloc_sbuf_tensor("cnt2", [8, 4], F32)
    B["prev"] = [nc.alloc_sbuf_tensor(f"prev{i}", [128, 4096], F32) for i in range(2)]

    from contextlib import ExitStack
    st1 = ExitStack()
    B["fbb"] = [st1.enter_context(nc.sbuf_tensor(f"fbb{i}", [128, 128], F32)) for i in range(2)]
    B["lbuf"] = [st1.enter_context(nc.sbuf_tensor(f"lbufs{i}", [128, 4096], F32)) for i in range(2)]
    B["ptmp"] = st1.enter_context(nc.sbuf_tensor("ptmp", [128, 4096], F32))
    emit_prefix(k, B, d)
    k.flush("prefix")
    k.drain()
    st1.close()

    st2 = ExitStack()
    B["stbuf"] = [st2.enter_context(nc.sbuf_tensor(f"stbuf{i}", [128, 4096], F32)) for i in range(2)]
    B["prevb"] = st2.enter_context(nc.sbuf_tensor("prevbb", [128, 4096], BF16))
    B["cbufB"] = [st2.enter_context(nc.sbuf_tensor(f"cbufB{i}", [128, 512], BF16)) for i in range(2)]
    B["ecsrow"] = st2.enter_context(nc.sbuf_tensor("ecsrow", [1, 8192], F32))
    B["yab"] = [st2.enter_context(nc.sbuf_tensor("yab0", [128, 4096], F32))] * 2
    B["gateb"] = [st2.enter_context(nc.sbuf_tensor("gatebb0", [128, 4096], F32))] * 2
    B["ygb"] = st2.enter_context(nc.sbuf_tensor("ygb", [128, 4096], F32))
    B["sqb"] = st2.enter_context(nc.sbuf_tensor("sqb", [128, 512], F32))
    B["ynst"] = st2.enter_context(nc.sbuf_tensor("ynst", [128, 256], F32))
    B["ynstb"] = st2.enter_context(nc.sbuf_tensor("ynstb", [128, 256], BF16))
    B["csc"] = [st2.enter_context(nc.sbuf_tensor(f"cscb{i}", [128, 128], BF16)) for i in range(2)]
    B["rstdrow"] = st2.enter_context(nc.sbuf_tensor("rstdrow", [1, 128], F32))
    emit_ssdb(k, B, d)
    k.flush("ssdb")
    k.drain()
    st2.close()

    st3 = ExitStack()
    B["ynb"] = st3.enter_context(nc.sbuf_tensor("ynb", [128, 64 * 512], BF16))
    B["sgst"] = [st3.enter_context(nc.sbuf_tensor(f"sgst{i}", [128, 512], F32)) for i in range(2)]
    B["wbuf"] = [st3.enter_context(nc.sbuf_tensor(f"wbufB{i}", [128, 64 * 128], BF16)) for i in range(2)]
    emit_outproj(k, B, d)
    k.flush("outproj")
    k.drain()
    st3.close()

    st4 = ExitStack()
    B["tmpw"] = st4.enter_context(nc.sbuf_tensor("tmpw", [128, 1024], F32))
    B["tmpwb"] = st4.enter_context(nc.sbuf_tensor("tmpwb", [128, 1024], BF16))
    B["bstk"] = st4.enter_context(nc.sbuf_tensor("bstk", [128, D], BF16))
    k.op("g", lambda e: [e.dma_start(out=B["bstk"][:, :], in_=d["bstk0"].ap())], wr=("bstk",), dma=True)
    B["mainb"] = [st4.enter_context(nc.sbuf_tensor(f"mainb{i}", [128, 1024], F32)) for i in range(2)]
    B["s0b"] = [st4.enter_context(nc.sbuf_tensor(f"s0b{i}", [128, 1024], F32)) for i in range(2)]
    B["g0st"] = [st4.enter_context(nc.sbuf_tensor(f"g0st{i}", [128, 1024], F32)) for i in range(2)]
    emit_ffn0_late(k, B, d)
    k.flush("ffn0l")
    k.drain()
    st4.close()

    st5 = ExitStack()
    B["xres"] = [st5.enter_context(nc.sbuf_tensor(f"xres{i}", [128, D], F32)) for i in range(2)]
    B["sgtile"] = [st5.enter_context(nc.sbuf_tensor(f"sgtile{i}", [128, 128], F32)) for i in range(2)]
    B["x1st"] = st5.enter_context(nc.sbuf_tensor("x1st", [128, D], F32))
    emit_combine(k, B, d, d["xin"], d["x1"], d["sgd"])
    k.flush("comb0")
    k.drain()
    st5.close()
    return nc, k, d


def prep_launchB_inputs(ins, core, la_outs):
    x = np.asarray(ins["L0_attn_hidden_states"]).reshape(SEQ, D)
    o = la_outs[core]
    ccg = np.stack([np.asarray(la_outs[b]["ccin"]).reshape(-1) for b in range(8)])
    mrow = np.zeros((1, 1024), np.float32)
    for b in range(core):
        mrow[0, 128 * b:128 * (b + 1)] = 1.0
    m = {
        "xin": np.ascontiguousarray(x[T * core:T * (core + 1)]),
        "gate": np.asarray(o["gate"]), "cmat": np.asarray(o["cmat"]),
        "ya": np.asarray(o["ya"]), "std": np.asarray(o["std"]),
        "ecsf": np.asarray(o["ecsf"]), "ccg": ccg,
        "tmpd": np.asarray(o["tmpd"]), "w12d": np.asarray(o["w12d"]),
        "cdrepd": np.asarray(o["cdrepd"]), "ffmain": np.asarray(o["ffmain"]),
        "mrow": mrow, "invmrow": 1.0 - mrow,
        "nwc": col_layout(ins["L0_attn_norm_w"]),
        "wopt": tile_w(np.asarray(ins["L0_attn_out_proj_w"])),
        "bstk0": np.ascontiguousarray(np.asarray(ins["L0_ffn_lora_B"]).reshape(128, D)),
    }
    return m



def emit_qkv(k, B, d):
    nc = k.nc
    xn_fm = B["xn_fm"]
    wbuf, qst = B["wbuf"], B["qst"]
    PSA, PSB = B["PS_A"], B["PS_B"]

    def fmslc(j, start, n):
        return xn_fm[:, j * T + start: j * T + start + n]

    for o in range(QKVD // 128):
        par = o % 2
        PS = PSA if par == 0 else PSB
        k.op("g", lambda e, o_=o, p_=par: [e.dma_start(out=wbuf[p_][:, :], in_=d["wqkvt"].ap()[:, o_, :])],
             wr=(f"wb{par}",), dma=True)
        def mmg(e, o_=o, p_=par):
            for j in range(NDC):
                w = wbuf[p_][:, 128 * j:128 * (j + 1)]
                st, sp = (j == 0), (j == NDC - 1)
                (PSA if p_ == 0 else PSB)[:, 0:512]
                e.matmul((PSA if p_ == 0 else PSB)[:, 0:512], w, fmslc(j, 0, 512), start=st, stop=sp)
                ti = e.matmul((PSA if p_ == 0 else PSB)[:, 512:1024], w, fmslc(j, 512, 512), start=st, stop=sp)
            return ti
        k.op("t", mmg, rd=(f"wb{par}", "ln1xn"), wr=("psA0", "psA1") if par == 0 else ("psB0", "psB1"))
        k.op("sc", lambda e, o_=o, p_=par: e.activation(qst[p_][:, :], (PSA if p_ == 0 else PSB)[:, :],
                                                        AF.Identity, bias=B["bqc"][:, o_:o_ + 1]),
             rd=(("psA0", "psA1", "bqc") if par == 0 else ("psB0", "psB1", "bqc")), wr=(f"qst{par}",))
        k.op("s", lambda e, o_=o, p_=par: [e.dma_start(out=d["qkvc"].ap()[128 * o_:128 * (o_ + 1), :],
                                                       in_=qst[p_][:, :])],
             rd=(f"qst{par}",), dma=True)


def emit_router_ffn1(k, B, d):
    nc = k.nc
    xn_fm = B["xn_fm"]
    PSA, PSB, PSC, PSD = B["PS_A"], B["PS_B"], B["PS_C"], B["PS_D"]
    PCD = [PSC, PSD]
    gfm, gtm, st8 = B["gfm"], B["gtm"], B["st8"]
    m1w, m2w = B["m1w"], B["m2w"]
    wT = B["wT"]
    identf = B["ident_f32"]

    def fmslc(j, start, n):
        return xn_fm[:, j * T + start: j * T + start + n]

    for tt in range(2):
        def rmm(e, tt_=tt):
            for j in range(NDC):
                ti = e.matmul(PCD[tt_][0:8, 0:512], B["wrt"][:, 8 * j:8 * (j + 1)],
                              fmslc(j, 512 * tt_, 512), start=(j == 0), stop=(j == NDC - 1))
            return ti
        k.op("t", rmm, rd=("wrt", "ln1xn"), wr=("psC" if tt == 0 else "psD",))
        k.op("sc", lambda e, tt_=tt: e.activation(gfm[0:8, 512 * tt_:512 * (tt_ + 1)], PCD[tt_][0:8, 0:512], AF.Copy),
             rd=("psC" if tt == 0 else "psD",), wr=("gfm",))
    for i in range(NTC):
        k.op("t", lambda e, i_=i: e.transpose(PSA[:, 512 * (i_ % 2):512 * (i_ % 2) + 8],
                                              gfm[0:8, 128 * i_:128 * (i_ + 1)], identf[0:8, 0:8]),
             rd=("gfm", "identf"), wr=(f"psA{i % 2}",))
        k.op("sc", lambda e, i_=i: e.activation(gtm[:, 0:8], PSA[:, 512 * (i_ % 2):512 * (i_ % 2) + 8], AF.Copy),
             rd=(f"psA{i % 2}",), wr=("gtm",))
        k.op("v", lambda e: e.reduce_max(st8[:, 8:9], gtm[:, 0:8], AX.X), rd=("gtm",), wr=("m1s",))
        k.op("v", lambda e: e.tensor_single_scalar(m1w[:, 0:8], gtm[:, 0:8], st8[:, 8:9], ALU.is_equal),
             rd=("gtm", "m1s"), wr=("m1w",))
        k.op("v", lambda e: e.scalar_tensor_tensor(gtm[:, 8:16], m1w[:, 0:8], -1e30, gtm[:, 0:8], ALU.mult, ALU.add),
             rd=("gtm", "m1w"), wr=("gtm2",))
        k.op("v", lambda e: e.reduce_max(st8[:, 9:10], gtm[:, 8:16], AX.X), rd=("gtm2",), wr=("m2s",))
        k.op("v", lambda e: e.tensor_single_scalar(m2w[:, 0:8], gtm[:, 8:16], st8[:, 9:10], ALU.is_equal),
             rd=("gtm2", "m2s"), wr=("m2w",))
        k.op("v", lambda e: e.tensor_tensor(st8[:, 10:11], st8[:, 9:10], st8[:, 8:9], ALU.subtract),
             rd=("m1s", "m2s"), wr=("dms",))
        k.op("sc", lambda e: e.activation(st8[:, 11:12], st8[:, 10:11], AF.Exp), rd=("dms",), wr=("expd",))
        k.op("v", lambda e: e.tensor_scalar(st8[:, 12:13], st8[:, 11:12], 1.0, None, ALU.add),
             rd=("expd",), wr=("sgp",))
        k.op("v", lambda e: e.reciprocal(st8[:, 13:14], st8[:, 12:13]), rd=("sgp",), wr=("sg1",))
        k.op("v", lambda e: e.tensor_scalar(st8[:, 14:15], st8[:, 13:14], 1.0, -1.0, ALU.subtract, ALU.mult),
             rd=("sg1",), wr=("sgs",))
        k.op("v", lambda e: e.tensor_scalar(m1w[:, 8:16], m1w[:, 0:8], st8[:, 13:14], None, ALU.mult),
             rd=("m1w", "sg1"), wr=("w1tm",))
        k.op("v", lambda e: e.scalar_tensor_tensor(m1w[:, 8:16], m2w[:, 0:8], st8[:, 14:15], m1w[:, 8:16],
                                                   ALU.mult, ALU.add),
             rd=("m2w", "sgs", "w1tm"), wr=("w1tm",))
        k.op("v", lambda e: e.tensor_scalar(m1w[:, 8:16], m1w[:, 8:16], 2.0, None, ALU.mult),
             rd=("w1tm",), wr=("w1tm",))
        k.op("t", lambda e, i_=i: e.transpose(PSB[0:8, 512 * (i_ % 2):512 * (i_ % 2) + 128],
                                              m1w[:, 8:16], identf[:, :]),
             rd=("w1tm", "identf"), wr=(f"psB{i % 2}",))
        k.op("sc", lambda e, i_=i: e.activation(wT[0:8, 128 * i_:128 * (i_ + 1)],
                                                PSB[0:8, 512 * (i_ % 2):512 * (i_ % 2) + 128], AF.Copy),
             rd=(f"psB{i % 2}",), wr=("wTt",))
    k.op("g", lambda e: [e.dma_start(out=B["wbuf"][0][:, :], in_=d["a1t"].ap()[:, 0, :])], wr=("wb0",), dma=True)
    for tt in range(2):
        def tmm(e, tt_=tt):
            for j in range(NDC):
                ti = e.matmul(PSA[:, 512 * tt_:512 * (tt_ + 1)], B["wbuf"][0][:, 128 * j:128 * (j + 1)],
                              fmslc(j, 512 * tt_, 512), start=(j == 0), stop=(j == NDC - 1))
            return ti
        k.op("t", tmm, rd=("wb0", "ln1xn"), wr=(f"psA{tt}",))
        k.op("sc", lambda e, tt_=tt: e.activation(B["tmpsb"][:, 512 * tt_:512 * (tt_ + 1)],
                                                  PSA[:, 512 * tt_:512 * (tt_ + 1)], AF.Copy),
             rd=(f"psA{tt}",), wr=("tmpsb",))
    for tt in range(2):
        k.op("t", lambda e, tt_=tt: e.matmul(PCD[tt_][:, 0:512], B["e8sb"][0:8, :], wT[0:8, 512 * tt_:512 * (tt_ + 1)],
                                             start=True, stop=True),
             rd=("e8c", "wTt"), wr=("psC" if tt == 0 else "psD",))
        k.op("v", lambda e, tt_=tt: e.tensor_tensor(B["tmpw"][:, 512 * tt_:512 * (tt_ + 1)],
                                                    B["tmpsb"][:, 512 * tt_:512 * (tt_ + 1)],
                                                    PCD[tt_][:, 0:512], ALU.mult),
             rd=(("psC" if tt == 0 else "psD"), "tmpsb"), wr=("tmpw",))
    k.op("v", lambda e: e.tensor_copy(B["tmpwb"][:, :], B["tmpw"][:, :]), rd=("tmpw",), wr=("tmpwb",))
    k.op("g", lambda e: [e.dma_start(out=B["bstk"][:, :], in_=d["bstk1"].ap())], wr=("bstk",), dma=True)
    for o in range(NDC):
        par = o % 2
        k.op("g", lambda e, o_=o, p_=par: [e.dma_start(out=B["wbuf"][p_][:, :], in_=d["wbaset"].ap()[:, o_, :])],
             wr=(f"wb{par}",), dma=True)
        def mm(e, o_=o, p_=par):
            for j in range(NDC):
                w = B["wbuf"][p_][:, 128 * j:128 * (j + 1)]
                st = (j == 0)
                e.matmul((PSA if p_ == 0 else PSB)[:, 0:512], w, fmslc(j, 0, 512), start=st, stop=False)
                ti = e.matmul((PSA if p_ == 0 else PSB)[:, 512:1024], w, fmslc(j, 512, 512), start=st, stop=False)
            e.matmul((PSA if p_ == 0 else PSB)[:, 0:512], B["bstk"][:, 128 * o_:128 * (o_ + 1)],
                     B["tmpwb"][:, 0:512], start=False, stop=True)
            return e.matmul((PSA if p_ == 0 else PSB)[:, 512:1024], B["bstk"][:, 128 * o_:128 * (o_ + 1)],
                            B["tmpwb"][:, 512:1024], start=False, stop=True)
        k.op("t", mm, rd=(f"wb{par}", "ln1xn", "bstk", "tmpwb"),
             wr=("psA0", "psA1") if par == 0 else ("psB0", "psB1"))
        k.op("sc", lambda e, o_=o, p_=par: e.activation(B["g1st"][p_][:, :], (PSA if p_ == 0 else PSB)[:, :],
                                                        AF.Silu, bias=B["bbc"][:, o_:o_ + 1]),
             rd=(("psA0", "psA1", "bbc") if par == 0 else ("psB0", "psB1", "bbc")), wr=(f"g1st{par}",))
        k.op("s", lambda e, o_=o, p_=par: [e.dma_start(out=d["g1d"].ap()[128 * o_:128 * (o_ + 1), :],
                                                       in_=B["g1st"][p_][:, :])],
             rd=(f"g1st{par}",), dma=True)


def build_launchC(external_scratch=True):
    from contextlib import ExitStack
    nc = bass.Bass("TRN2", target_bir_lowering=False, detect_race_conditions=False)
    kind_s = "ExternalOutput" if external_scratch else "Internal"
    kin = "ExternalInput" if external_scratch else "Internal"
    d = {}
    d["x1"] = nc.dram_tensor("x1", [T, D], F32, kind=kin)
    d["wqkvt"] = nc.dram_tensor("wqkvt", [128, QKVD // 128, NDC * 128], F32, kind="ExternalInput")
    d["bqc"] = nc.dram_tensor("bqc", [128, QKVD // 128], F32, kind="ExternalInput")
    d["wrt"] = nc.dram_tensor("wrt", [128, 256], F32, kind="ExternalInput")
    d["a1t"] = nc.dram_tensor("a1t", [128, 1, NDC * 128], F32, kind="ExternalInput")
    d["bstk1"] = nc.dram_tensor("bstk1", [128, D], F32, kind="ExternalInput")
    d["wbaset"] = nc.dram_tensor("wbaset", [128, NDC, NDC * 128], F32, kind="ExternalInput")
    d["bbc"] = nc.dram_tensor("bbc", [128, NDC], F32, kind="ExternalInput")
    d["qkvc"] = nc.dram_tensor("qkvc", [QKVD, T], BF16, kind="ExternalOutput")
    d["g1d"] = nc.dram_tensor("g1d", [D, T], F32, kind=kind_s)

    c = mkconsts(nc)
    k = K(nc)
    B = {}
    B["PS_A"] = nc.alloc_psum_tensor("PS_A", [128, 1024], F32)
    B["PS_B"] = nc.alloc_psum_tensor("PS_B", [128, 1024], F32)
    B["PS_C"] = nc.alloc_psum_tensor("PS_C", [128, 512], F32)
    B["PS_D"] = nc.alloc_psum_tensor("PS_D", [128, 512], F32)
    B["PST"] = nc.alloc_psum_tensor("PST", [128, 2048], BF16)
    per = [("ident_bf", c["ident_bf"], [128, 128], BF16, "identb"),
           ("ident_f32", c["ident_f32"], [128, 128], F32, "identf"),
           ("ones_row", c["ones_row"], [1, 128], F32, "onesr"),
           ("ones_col", c["ones_col"], [128, 1], F32, "onesc"),
           ("biasv", c["biasv"], [128, 4], F32, "biasv"),
           ("e8sb", c["e8"], [8, 128], F32, "e8c"),
           ("bqc", d["bqc"], [128, QKVD // 128], F32, "bqc"),
           ("bbc", d["bbc"], [128, NDC], F32, "bbc")]
    for name, srcd, shape, dt_, key in per:
        t = nc.alloc_sbuf_tensor("sb_" + name, shape, dt_)
        B[name] = t
        k.op("s", lambda e, s_=srcd, t_=t: [e.dma_start(out=t_[:, :], in_=s_.ap())], wr=(key,), dma=True)
    B["wrt"] = nc.alloc_sbuf_tensor("sb_wrt", [128, 256], BF16)
    k.op("g", lambda e: [e.dma_start(out=B["wrt"][:, :], in_=d["wrt"].ap())], wr=("wrt",), dma=True)
    k.flush("constsC")
    k.drain()

    stack = ExitStack()
    B["xn_fm"] = stack.enter_context(nc.sbuf_tensor("xn_fm", [128, NDC * T], BF16))

    ln_stack = ExitStack()
    B["xtm"] = [ln_stack.enter_context(nc.sbuf_tensor(f"xtm{i}", [128, D], F32)) for i in range(2)]
    B["xsq"] = ln_stack.enter_context(nc.sbuf_tensor("xsq", [128, D], F32))
    B["xbtm"] = ln_stack.enter_context(nc.sbuf_tensor("xbtm", [128, D], BF16))
    B["st1"] = ln_stack.enter_context(nc.sbuf_tensor("st1", [128, 16], F32))
    emit_ln(k, B, d["x1"], None, B["xn_fm"], T, "ln1")
    k.flush("ln1")
    k.drain()
    ln_stack.close()

    q_stack = ExitStack()
    B["wbuf"] = [q_stack.enter_context(nc.sbuf_tensor(f"wbuf{i}", [128, 4096], BF16)) for i in range(2)]
    B["qst"] = [q_stack.enter_context(nc.sbuf_tensor(f"qst{i}", [128, 1024], BF16)) for i in range(2)]
    emit_qkv(k, B, d)
    k.flush("qkv")
    k.drain()
    B["qst"] = None
    B["gfm"] = q_stack.enter_context(nc.sbuf_tensor("gfm", [8, 1024], F32))
    B["gtm"] = q_stack.enter_context(nc.sbuf_tensor("gtm", [128, 16], F32))
    B["st8"] = q_stack.enter_context(nc.sbuf_tensor("st8", [128, 16], F32))
    B["m1w"] = q_stack.enter_context(nc.sbuf_tensor("m1w", [128, 16], F32))
    B["m2w"] = q_stack.enter_context(nc.sbuf_tensor("m2w", [128, 16], F32))
    B["wT"] = q_stack.enter_context(nc.sbuf_tensor("wT", [8, 1024], F32))
    B["tmpsb"] = q_stack.enter_context(nc.sbuf_tensor("tmpsb", [128, 1024], F32))
    B["tmpw"] = q_stack.enter_context(nc.sbuf_tensor("tmpw", [128, 1024], F32))
    B["tmpwb"] = q_stack.enter_context(nc.sbuf_tensor("tmpwb", [128, 1024], BF16))
    B["bstk"] = q_stack.enter_context(nc.sbuf_tensor("bstk", [128, D], BF16))
    B["g1st"] = [q_stack.enter_context(nc.sbuf_tensor(f"g1st{i}", [128, 1024], F32)) for i in range(2)]
    emit_router_ffn1(k, B, d)
    k.flush("rffn1")
    k.drain()
    q_stack.close()
    stack.close()
    return nc, k, d


def permute_qkv_rows():
    deint = list(range(0, 128, 2)) + list(range(1, 128, 2))
    perm = []
    for h in range(NQH + NKV):
        base = h * 128
        perm += [base + j for j in deint]
    base = (NQH + NKV) * 128
    perm += list(range(base, base + NKV * 128))
    return np.array(perm)


def prep_launchC_inputs(ins, core, x1):
    perm = permute_qkv_rows()
    Wq = np.asarray(ins["L1_attn_w_qkv"])[perm]
    bq = np.asarray(ins["L1_attn_b_qkv"])[perm]
    Wr = np.asarray(ins["L1_ffn_W_router"])
    wrt = np.ascontiguousarray(Wr.T.reshape(32, 128, 8).transpose(1, 0, 2).reshape(128, 256))
    A1 = np.asarray(ins["L1_ffn_A"])
    astk1 = np.ascontiguousarray(A1.transpose(2, 0, 1).reshape(D, 128))
    B1 = np.asarray(ins["L1_ffn_B"])
    bstk1 = np.ascontiguousarray(B1.transpose(0, 2, 1).reshape(128, D))
    return {
        "x1": np.ascontiguousarray(x1[T * core:T * (core + 1)]),
        "wqkvt": tile_w(Wq), "bqc": col_layout(bq), "wrt": wrt,
        "a1t": tile_w(astk1.T), "bstk1": bstk1,
        "wbaset": tile_w(np.asarray(ins["L1_ffn_W_base"])),
        "bbc": col_layout(ins["L1_ffn_b_base"]),
    }



NQT = SEQ // 512
NKC = SEQ // 128

def emit_rope(k, B, key_in, key_out, nheads, tag):
    nc = k.nc
    cosb, sinb = B["cosb"], B["sinb"]
    TL = 2048
    ta, tb, tc, td, todd = B["rta"], B["rtb"], B["rtc"], B["rtd"], B["rtodd"]
    for h in range(nheads):
        src = B[key_in][h]
        for q in range(SEQ // TL):
            sl = slice(TL * q, TL * (q + 1))
            k.op("v", lambda e, s_=src, sl_=sl: e.tensor_copy(todd[:, :], s_[64:128, sl_]),
                 rd=(f"{tag}{h}",), wr=("rtodd",))
            k.op("v", lambda e, s_=src, sl_=sl: e.tensor_tensor(ta[:, :], s_[0:64, sl_], cosb[:, sl_], ALU.mult),
                 rd=(f"{tag}{h}", "cosb"), wr=("rta",))
            k.op("v", lambda e, sl_=sl: e.tensor_tensor(tb[:, :], todd[:, :], sinb[:, sl_], ALU.mult),
                 rd=("rtodd", "sinb"), wr=("rtb",))
            k.op("v", lambda e, s_=src, sl_=sl: e.tensor_tensor(tc[:, :], s_[0:64, sl_], sinb[:, sl_], ALU.mult),
                 rd=(f"{tag}{h}", "sinb"), wr=("rtc",))
            k.op("v", lambda e, sl_=sl: e.tensor_tensor(td[:, :], todd[:, :], cosb[:, sl_], ALU.mult),
                 rd=("rtodd", "cosb"), wr=("rtd",))
            k.op("v", lambda e, s_=src, sl_=sl: e.tensor_tensor(s_[0:64, sl_], ta[:, :], tb[:, :], ALU.subtract),
                 rd=("rta", "rtb"), wr=(f"{tag}{h}",))
            k.op("v", lambda e, s_=src, sl_=sl: e.tensor_tensor(s_[64:128, sl_], tc[:, :], td[:, :], ALU.add),
                 rd=("rtc", "rtd"), wr=(f"{tag}{h}",))


def emit_attn(k, B, d):
    nc = k.nc
    PSA, PSB, PSC, PSD = B["PS_A"], B["PS_B"], B["PS_C"], B["PS_D"]
    PST = B["PST"]
    onesr, onesc, identb = B["ones_row"], B["ones_col"], B["ident_bf"]
    kf = B["kf"][0]
    vt = B["vt"]
    denacc = None
    for cc2 in range(NKC):
        qp = cc2 % 2
        k.op("t", lambda e, c_=cc2, qp_=qp: e.transpose(PST[:, 1024 * qp_:1024 * qp_ + 128],
                                                        B["vf"][0][:, 128 * c_:128 * (c_ + 1)], identb[:, :]),
             rd=("vf0", "identb"), wr=(f"pst{qp}",))
        k.op("sc", lambda e, c_=cc2, qp_=qp: e.activation(vt[:, 128 * c_:128 * (c_ + 1)],
                                                          PST[:, 1024 * qp_:1024 * qp_ + 128], AF.Copy),
             rd=(f"pst{qp}",), wr=("vt",))
    sqt, nrow = B["sqt"], B["nrow"]

    def norms(tag, src, rdk):
        for w in range(16):
            qp = w % 2
            k.op("sc", lambda e, s_=src, w_=w, qp_=qp: e.activation(sqt[qp_][:, :], s_[:, 512 * w_:512 * (w_ + 1)],
                                                                    AF.Square),
                 rd=(rdk,), wr=(f"sqt{qp}",))
            k.op("t", lambda e, w_=w, qp_=qp: e.matmul(PSC[0:1, 0:512] if qp_ == 0 else PSD[0:1, 0:512],
                                                       onesc[:, :], sqt[qp_][:, :], start=True, stop=True),
                 rd=(f"sqt{qp}", "onesc"), wr=("psC" if qp == 0 else "psD",))
            k.op("sc", lambda e, w_=w, qp_=qp, t_=tag: e.activation(nrow[t_][0:1, 512 * w_:512 * (w_ + 1)],
                                                                    (PSC if qp_ == 0 else PSD)[0:1, 0:512], AF.Copy),
                 rd=("psC" if qp == 0 else "psD",), wr=("nrow",))

    norms("k", kf, "kf0")
    k.op("v", lambda e: e.reduce_max(B["sc1"][0:1, 0:1], nrow["k"][0:1, :], AX.X), rd=("nrow",), wr=("kmax",))
    k.op("sc", lambda e: e.activation(B["sc1"][0:1, 1:2], B["sc1"][0:1, 0:1], AF.Sqrt), rd=("kmax",), wr=("kmax",))
    k.op("v", lambda e: e.tensor_scalar(B["sc1"][0:1, 1:2], B["sc1"][0:1, 1:2], 1.02, None, ALU.mult),
         rd=("kmax",), wr=("kmax",))
    for h in range(4):
        norms(f"q{h}", B["qf"][h], f"qf{h}")
        k.op("sc", lambda e, h_=h: e.activation(nrow[f"q{h_}"][0:1, :], nrow[f"q{h_}"][0:1, :], AF.Sqrt),
             rd=("nrow",), wr=("nrow",))
        k.op("v", lambda e, h_=h: e.tensor_single_scalar(nrow[f"q{h_}"][0:1, :], nrow[f"q{h_}"][0:1, :],
                                                         B["sc1"][0:1, 1:2], ALU.mult),
             rd=("nrow", "kmax"), wr=("nrow",))
        k.op("v", lambda e, h_=h: e.tensor_single_scalar(nrow[f"q{h_}"][0:1, :], nrow[f"q{h_}"][0:1, :],
                                                         1.02, ALU.mult),
             rd=("nrow",), wr=("nrow",))
        k.op("v", lambda e, h_=h: e.reduce_max(B["btile"][0:1, 16 * h_:16 * (h_ + 1)],
                                               bass.AP(nrow[f"q{h_}"], 0, [[8192, 1], [512, 16], [1, 512]]), AX.X),
             rd=("nrow",), wr=(f"bt{h}",))
        k.op("v", lambda e, h_=h: e.tensor_scalar(B["btn"][0:1, 16 * h_:16 * (h_ + 1)],
                                                  B["btile"][0:1, 16 * h_:16 * (h_ + 1)], -ISQ, None, ALU.mult),
             rd=(f"bt{h}",), wr=(f"btn{h}",))
        k.op("t", lambda e, h_=h: e.matmul(PSC[:, 0:16], onesr[:, :], B["btn"][0:1, 16 * h_:16 * (h_ + 1)],
                                           start=True, stop=True),
             rd=(f"btn{h}", "onesr"), wr=("psC",))
        k.op("sc", lambda e, h_=h: e.activation(B["negb"][:, 16 * h_:16 * (h_ + 1)], PSC[:, 0:16], AF.Copy),
             rd=("psC",), wr=("negb",))
    PT = B["ptb"]
    for h in range(4):
        qf = B["qf"][h]
        for qt in range(NQT):
            ncc = 4 * (qt + 1)
            for cc2 in range(ncc):
                sp = cc2 % 2
                k.op("t", lambda e, qf_=qf, qt_=qt, c_=cc2, sp_=sp: e.matmul(
                         PSA[:, 512 * sp_:512 * (sp_ + 1)], kf[:, 128 * c_:128 * (c_ + 1)],
                         qf_[:, 512 * qt_:512 * (qt_ + 1)], start=True, stop=True),
                     rd=("kf0", f"qf{h}"), wr=(f"psA{sp}",))
                diag = cc2 >= 4 * qt
                if diag:
                    p = cc2 - 4 * qt
                    k.op("v", lambda e, p_=p, sp_=sp: e.tensor_tensor(B["mskbuf"][:, :],
                                                                      PSA[:, 512 * sp_:512 * (sp_ + 1)],
                                                                      B["maskd"][:, 512 * p_:512 * (p_ + 1)], ALU.add),
                         rd=(f"psA{sp}", "maskd"), wr=("mskbuf",))
                    k.op("sc", lambda e, h_=h, qt_=qt, sp_=sp: e.activation(
                             PT[sp_][:, :], B["mskbuf"][:, :], AF.Exp, scale=ISQ,
                             bias=B["negb"][:, 16 * h_ + qt_:16 * h_ + qt_ + 1]),
                         rd=("mskbuf", "negb"), wr=(f"pt{sp}",))
                else:
                    k.op("sc", lambda e, h_=h, qt_=qt, sp_=sp: e.activation(
                             PT[sp_][:, :], PSA[:, 512 * sp_:512 * (sp_ + 1)], AF.Exp, scale=ISQ,
                             bias=B["negb"][:, 16 * h_ + qt_:16 * h_ + qt_ + 1]),
                         rd=(f"psA{sp}", "negb"), wr=(f"pt{sp}",))
                k.op("t", lambda e, c_=cc2, sp_=sp, first=(cc2 == 0), last=(cc2 == ncc - 1): e.matmul(
                         PSB[:, 0:512], vt[:, 128 * c_:128 * (c_ + 1)], PT[sp_][:, :],
                         start=first, stop=last),
                     rd=("vt", f"pt{sp}"), wr=("psB0",))
                k.op("t", lambda e, c_=cc2, sp_=sp, first=(cc2 == 0), last=(cc2 == ncc - 1): e.matmul(
                         PSD[0:1, 0:512], B["ones_col_bf"][:, :], PT[sp_][:, :], start=first, stop=last),
                     rd=(f"pt{sp}", "onescb"), wr=("psD",))
            k.op("sc", lambda e: e.activation(B["denr"][0:1, :], PSD[0:1, 0:512], AF.Copy),
                 rd=("psD",), wr=("denr",))
            k.op("v", lambda e: e.reciprocal(B["denri"][0:1, :], B["denr"][0:1, :]), rd=("denr",), wr=("denri",))
            k.op("t", lambda e: e.matmul(PSC[:, 0:512], onesr[:, :], B["denri"][0:1, :], start=True, stop=True),
                 rd=("denri", "onesr"), wr=("psC",))
            k.op("sc", lambda e: e.activation(B["recb"][:, :], PSC[:, 0:512], AF.Copy), rd=("psC",), wr=("recb",))
            k.op("v", lambda e: e.tensor_tensor(B["aost"][:, :], PSB[:, 0:512], B["recb"][:, :], ALU.mult),
                 rd=("psB0", "recb"), wr=("aost",))
            k.op("s", lambda e, h_=h, qt_=qt: [e.dma_start(
                     out=d["attnc"].ap()[128 * h_:128 * (h_ + 1), 512 * qt_:512 * (qt_ + 1)],
                     in_=B["aost"][:, :])],
                 rd=("aost",), dma=True, tag="aost")


def build_launchD():
    from contextlib import ExitStack
    nc = bass.Bass("TRN2", target_bir_lowering=False, detect_race_conditions=False)
    d = {}
    d["qin"] = nc.dram_tensor("qin", [512, SEQ], BF16, kind="ExternalInput")
    d["kin"] = nc.dram_tensor("kin", [128, SEQ], BF16, kind="ExternalInput")
    d["vin"] = nc.dram_tensor("vin", [128, SEQ], BF16, kind="ExternalInput")
    d["cosd"] = nc.dram_tensor("cosd", [64, SEQ], F32, kind="ExternalInput")
    d["sind"] = nc.dram_tensor("sind", [64, SEQ], F32, kind="ExternalInput")
    d["maskd"] = nc.dram_tensor("maskd", [128, 2048], F32, kind="ExternalInput")
    d["attnc"] = nc.dram_tensor("attnc", [512, SEQ], BF16, kind="ExternalOutput")
    c = mkconsts(nc)
    k = K(nc)
    B = {}
    B["PS_A"] = nc.alloc_psum_tensor("PS_A", [128, 1024], F32)
    B["PS_B"] = nc.alloc_psum_tensor("PS_B", [128, 1024], F32)
    B["PS_C"] = nc.alloc_psum_tensor("PS_C", [128, 512], F32)
    B["PS_D"] = nc.alloc_psum_tensor("PS_D", [128, 512], F32)
    B["PST"] = nc.alloc_psum_tensor("PST", [128, 2048], BF16)
    per = [("ident_bf", c["ident_bf"], [128, 128], BF16, "identb"),
           ("ones_row", c["ones_row"], [1, 128], F32, "onesr"),
           ("ones_col", c["ones_col"], [128, 1], F32, "onesc"),
           ("ones_col_bf", c["ones_col_bf"], [128, 1], BF16, "onescb"),
           ("maskd", d["maskd"], [128, 2048], F32, "maskd")]
    for name, srcd, shape, dt_, key in per:
        t = nc.alloc_sbuf_tensor("sb_" + name, shape, dt_)
        B[name] = t
        k.op("s", lambda e, s_=srcd, t_=t: [e.dma_start(out=t_[:, :], in_=s_.ap())], wr=(key,), dma=True)
    B["cosb"] = nc.alloc_sbuf_tensor("sb_cosb", [64, SEQ], BF16)
    B["sinb"] = nc.alloc_sbuf_tensor("sb_sinb", [64, SEQ], BF16)
    k.op("g", lambda e: [e.dma_start(out=B["cosb"][:, :], in_=d["cosd"].ap())], wr=("cosb",), dma=True)
    k.op("g", lambda e: [e.dma_start(out=B["sinb"][:, :], in_=d["sind"].ap())], wr=("sinb",), dma=True)
    B["qf"] = [nc.alloc_sbuf_tensor(f"qf{h}", [128, SEQ], BF16) for h in range(4)]
    B["kf"] = [nc.alloc_sbuf_tensor("kf0", [128, SEQ], BF16)]
    B["vf"] = [nc.alloc_sbuf_tensor("vf0", [128, SEQ], BF16)]
    for h in range(4):
        k.op("s", lambda e, h_=h: [e.dma_start(out=B["qf"][h_][:, :], in_=d["qin"].ap()[128 * h_:128 * (h_ + 1), :])],
             wr=(f"qf{h}",), dma=True)
    k.op("s", lambda e: [e.dma_start(out=B["kf"][0][:, :], in_=d["kin"].ap())], wr=("kf0",), dma=True)
    k.op("s", lambda e: [e.dma_start(out=B["vf"][0][:, :], in_=d["vin"].ap())], wr=("vf0",), dma=True)
    B["vt"] = nc.alloc_sbuf_tensor("vt", [128, SEQ], BF16)
    k.flush("constsD")
    k.drain()

    st = ExitStack()
    for nm in ["rta", "rtb", "rtc", "rtd", "rtodd"]:
        B[nm] = st.enter_context(nc.sbuf_tensor(nm, [64, 2048], F32))
    emit_rope(k, B, "kf", None, 1, "kf")
    emit_rope(k, B, "qf", None, 4, "qf")
    k.flush("rope")
    k.drain()
    st.close()

    st2 = ExitStack()
    B["sqt"] = [st2.enter_context(nc.sbuf_tensor(f"sqt{i}", [128, 512], F32)) for i in range(2)]
    nr1 = st2.enter_context(nc.sbuf_tensor("nrow0", [2, SEQ], F32))
    B["nrow"] = {t_: nr1 for t_ in ["k", "q0", "q1", "q2", "q3"]}
    B["sc1"] = st2.enter_context(nc.sbuf_tensor("sc1", [2, 4], F32))
    B["btile"] = st2.enter_context(nc.sbuf_tensor("btile", [2, 64], F32))
    B["btn"] = st2.enter_context(nc.sbuf_tensor("btn", [1, 64], F32))
    B["negb"] = st2.enter_context(nc.sbuf_tensor("negb", [128, 64], F32))
    B["ptb"] = [st2.enter_context(nc.sbuf_tensor(f"ptb{i}", [128, 512], BF16)) for i in range(2)]
    B["mskbuf"] = st2.enter_context(nc.sbuf_tensor("mskbuf", [128, 512], F32))
    B["denr"] = st2.enter_context(nc.sbuf_tensor("denr", [2, 512], F32))
    B["denri"] = st2.enter_context(nc.sbuf_tensor("denri", [1, 512], F32))
    B["recb"] = st2.enter_context(nc.sbuf_tensor("recb", [128, 512], F32))
    B["aost"] = st2.enter_context(nc.sbuf_tensor("aost", [128, 512], BF16))
    emit_attn(k, B, d)
    k.flush("attn")
    k.drain()
    st2.close()
    return nc, k, d


def prep_launchD_inputs(core, qkvc_all):
    import math
    q = np.concatenate([np.asarray(qkvc_all[b])[512 * core:512 * (core + 1), :] for b in range(8)], axis=1)
    krow = NQH * 128 + core * 128
    kin = np.concatenate([np.asarray(qkvc_all[b])[krow:krow + 128, :] for b in range(8)], axis=1)
    vrow = (NQH + NKV) * 128 + core * 128
    vin = np.concatenate([np.asarray(qkvc_all[b])[vrow:vrow + 128, :] for b in range(8)], axis=1)
    inv = 1.0 / 10000.0 ** (np.arange(0, HDIM, 2, dtype=np.float32) / HDIM)
    pos = np.arange(SEQ, dtype=np.float32)
    ang = pos[None, :] * inv[:, None]
    mask = np.zeros((128, 2048), np.float32)
    for p in range(4):
        s = np.arange(128)[:, None]
        tq = np.arange(512)[None, :]
        mask[:, 512 * p:512 * (p + 1)] = np.where(s <= tq - 128 * p, 0.0, -1e30)
    return {
        "qin": np.ascontiguousarray(q), "kin": np.ascontiguousarray(kin), "vin": np.ascontiguousarray(vin),
        "cosd": np.cos(ang).astype(np.float32), "sind": np.sin(ang).astype(np.float32),
        "maskd": mask,
    }



def emit_wo_combine(k, B, d):
    nc = k.nc
    PSA, PSB = B["PS_A"], B["PS_B"]
    att = B["att"]

    for o in range(NDC):
        par = o % 2
        k.op("g", lambda e, o_=o, p_=par: [e.dma_start(out=B["wbuf"][p_][:, :], in_=d["wot"].ap()[:, o_, :])],
             wr=(f"wb{par}",), dma=True)
        def mm(e, o_=o, p_=par):
            for j in range(NDC):
                w = B["wbuf"][p_][:, 128 * j:128 * (j + 1)]
                st, sp = (j == 0), (j == NDC - 1)
                e.matmul((PSA if p_ == 0 else PSB)[:, 0:512], w, att[:, j * T:j * T + 512], start=st, stop=sp)
                ti = e.matmul((PSA if p_ == 0 else PSB)[:, 512:1024], w, att[:, j * T + 512:j * T + 1024], start=st, stop=sp)
            return ti
        k.op("t", mm, rd=(f"wb{par}", "att"), wr=("psA0", "psA1") if par == 0 else ("psB0", "psB1"))
        k.op("sc", lambda e, o_=o, p_=par: e.activation(B["s1st"][p_][:, :], (PSA if p_ == 0 else PSB)[:, :],
                                                        AF.Identity, bias=B["boc"][:, o_:o_ + 1]),
             rd=(("psA0", "psA1", "boc") if par == 0 else ("psB0", "psB1", "boc")), wr=(f"s1st{par}",))
        k.op("s", lambda e, o_=o, p_=par: [e.dma_start(out=B["g1b"][p_][:, :],
                                                       in_=d["g1d"].ap()[128 * o_:128 * (o_ + 1), :])],
             wr=(f"g1b{par}",), dma=True)
        k.op("v", lambda e, p_=par: e.tensor_tensor(B["s1st"][p_][:, :], B["s1st"][p_][:, :],
                                                    B["g1b"][p_][:, :], ALU.mult),
             rd=(f"s1st{par}", f"g1b{par}"), wr=(f"s1st{par}",))
        k.op("s", lambda e, o_=o, p_=par: [e.dma_start(out=d["sgd"].ap()[128 * o_:128 * (o_ + 1), :],
                                                       in_=B["s1st"][p_][:, :])],
             rd=(f"s1st{par}",), dma=True, tag=f"sgo{par}")


def build_launchE():
    from contextlib import ExitStack
    nc = bass.Bass("TRN2", target_bir_lowering=False, detect_race_conditions=False)
    d = {}
    d["x1"] = nc.dram_tensor("x1", [T, D], F32, kind="ExternalInput")
    d["attin"] = nc.dram_tensor("attin", [D, T], BF16, kind="ExternalInput")
    d["g1d"] = nc.dram_tensor("g1d", [D, T], F32, kind="ExternalInput")
    d["wot"] = nc.dram_tensor("wot", [128, NDC, NDC * 128], F32, kind="ExternalInput")
    d["boc"] = nc.dram_tensor("boc", [128, NDC], F32, kind="ExternalInput")
    d["sgd"] = nc.dram_tensor("sgd", [D, T], F32, kind="Internal")
    d["x2"] = nc.dram_tensor("x2", [T, D], F32, kind="ExternalOutput")
    c = mkconsts(nc)
    k = K(nc)
    B = {}
    B["PS_A"] = nc.alloc_psum_tensor("PS_A", [128, 1024], F32)
    B["PS_B"] = nc.alloc_psum_tensor("PS_B", [128, 1024], F32)
    B["PS_C"] = nc.alloc_psum_tensor("PS_C", [128, 512], F32)
    B["PS_D"] = nc.alloc_psum_tensor("PS_D", [128, 512], F32)
    per = [("ident_f32", c["ident_f32"], [128, 128], F32, "identf"),
           ("boc", d["boc"], [128, NDC], F32, "boc")]
    for name, srcd, shape, dt_, key in per:
        t = nc.alloc_sbuf_tensor("sb_" + name, shape, dt_)
        B[name] = t
        k.op("s", lambda e, s_=srcd, t_=t: [e.dma_start(out=t_[:, :], in_=s_.ap())], wr=(key,), dma=True)
    B["att"] = nc.alloc_sbuf_tensor("att", [128, NDC * T], BF16)
    def ldatt(e):
        return [e.dma_start(out=B["att"][:, j * T:(j + 1) * T], in_=d["attin"].ap()[128 * j:128 * (j + 1), :])
                for j in range(NDC)]
    k.op("s", ldatt, wr=("att",), dma=True, n_dma=NDC)
    k.flush("constsE")
    k.drain()
    st = ExitStack()
    B["wbuf"] = [st.enter_context(nc.sbuf_tensor(f"wbufE{i}", [128, 4096], BF16)) for i in range(2)]
    B["s1st"] = [st.enter_context(nc.sbuf_tensor(f"s1st{i}", [128, 1024], F32)) for i in range(2)]
    B["g1b"] = [st.enter_context(nc.sbuf_tensor(f"g1b{i}", [128, 1024], F32)) for i in range(2)]
    emit_wo_combine(k, B, d)
    k.flush("wo")
    k.drain()
    st.close()
    st2 = ExitStack()
    B["xres"] = [st2.enter_context(nc.sbuf_tensor(f"xresE{i}", [128, D], F32)) for i in range(2)]
    B["x1st"] = st2.enter_context(nc.sbuf_tensor("x1stE", [128, D], F32))
    B["sgtile"] = [st2.enter_context(nc.sbuf_tensor(f"sgtileE{i}", [128, 128], F32)) for i in range(2)]
    emit_combine(k, B, d, d["x1"], d["x2"], d["sgd"])
    k.flush("combE")
    k.drain()
    st2.close()
    return nc, k, d


def prep_launchE_inputs(ins, core, x1, attnc_all):
    att = np.concatenate([np.asarray(attnc_all[b]) for b in range(8)], axis=0)
    return {
        "x1": np.ascontiguousarray(x1[T * core:T * (core + 1)]),
        "attin": np.ascontiguousarray(att[:, T * core:T * (core + 1)]),
        "g1d": None,
        "wot": tile_w(np.asarray(ins["L1_attn_w_o"])),
        "boc": col_layout(ins["L1_attn_b_o"]),
    }



def kernel(**inputs):
    ins = inputs
    ncA, kA, dA = build_launchA()
    mA = [prep_launchA_inputs(ins, c) for c in range(NCORES)]
    resA = run_bass_kernel_spmd(ncA, mA, core_ids=list(range(NCORES)))
    la = [resA.results[c] for c in range(NCORES)]
    ncB, kB, dB = build_launchB()
    mB = [prep_launchB_inputs(ins, c, la) for c in range(NCORES)]
    resB = run_bass_kernel_spmd(ncB, mB, core_ids=list(range(NCORES)))
    x1 = np.concatenate([np.asarray(resB.results[c]["x1"]) for c in range(NCORES)], axis=0)
    ncC, kC, dC = build_launchC()
    mC = [prep_launchC_inputs(ins, c, x1) for c in range(NCORES)]
    resC = run_bass_kernel_spmd(ncC, mC, core_ids=list(range(NCORES)))
    qkvc_all = [np.asarray(resC.results[c]["qkvc"]) for c in range(NCORES)]
    g1_all = [np.asarray(resC.results[c]["g1d"]) for c in range(NCORES)]
    ncD, kD, dD = build_launchD()
    mD = [prep_launchD_inputs(c, qkvc_all) for c in range(NCORES)]
    resD = run_bass_kernel_spmd(ncD, mD, core_ids=list(range(NCORES)))
    attnc_all = [np.asarray(resD.results[c]["attnc"]) for c in range(NCORES)]
    ncE, kE, dE = build_launchE()
    mE = []
    for c in range(NCORES):
        m = prep_launchE_inputs(ins, c, x1, attnc_all)
        m["g1d"] = g1_all[c]
        mE.append(m)
    resE = run_bass_kernel_spmd(ncE, mE, core_ids=list(range(NCORES)))
    x2 = np.concatenate([np.asarray(resE.results[c]["x2"]) for c in range(NCORES)], axis=0)
    return x2.reshape(1, SEQ, D)


# version 03
# speedup vs baseline: 1.0035x; 1.0035x over previous
import numpy as np
import ml_dtypes
import concourse.bass as bass
import concourse.mybir as mybir
from concourse.bass_utils import run_bass_kernel_spmd

F32 = mybir.dt.float32
BF16 = mybir.dt.bfloat16
AF = mybir.ActivationFunctionType
ALU = mybir.AluOpType
AX = mybir.AxisListType
BF = ml_dtypes.bfloat16

NCORES = 8
T = 1024
TH = T + 3
D = 4096
SEQ = 8192
NDC = D // 128
NTC = T // 128
NH, HDm, SS, NG, CHK = 128, 64, 128, 8, 128
INTER = NH * HDm
CONV_DIM = INTER + 2 * NG * SS
PROJ = INTER + CONV_DIM + NH
NOC_G = INTER // 128
NOC_X = CONV_DIM // 128
NOC_P = PROJ // 128
NCH = T // CHK
HPG = NH // NG
EPS = 1e-5
CAP = float(int(3.0 * SEQ / 8))
NQH, NKV, HDIM = 32, 8, 128
QKVD = (NQH + 2 * NKV) * HDIM
ISQ = float(1.0 / np.sqrt(128.0))


def tile_w(W):
    O, K = W.shape
    H = W.reshape(O // 128, 128, K // 128, 128).transpose(3, 0, 2, 1)
    return np.ascontiguousarray(H.reshape(128, O // 128, (K // 128) * 128))

def col_layout(v):
    return np.ascontiguousarray(np.asarray(v).reshape(-1, 128).T)


ENG_SEM = {"t": "pe", "v": "ve", "sc": "ac", "g": "po", "s": "sy"}
DMA_SEM = {"s": "dsy", "g": "dpo", "sc": "dsc", "v": "dve"}

class K:
    def __init__(self, nc):
        self.nc = nc
        self.sems = {}
        self.cnt = {n: 0 for n in list(ENG_SEM.values()) + ["cc"]}
        for n in list(self.cnt):
            self.sems[n] = nc.alloc_semaphore("S" + n)
        self.q = {e: [] for e in ENG_SEM}
        self.last_w = {}
        self.readers = {}
        self.satisfied = {e: {} for e in ENG_SEM}
        self.nops = 0
        self.ndrain = 0

    def _need(self, eng, waits, sem, val):
        cur = self.satisfied[eng].get(sem, 0)
        if val <= cur:
            return
        waits[sem] = max(waits.get(sem, 0), val)

    def op(self, eng, fn, rd=(), wr=(), ptr=(), dma=False, n_dma=1, tag=None):
        waits = {}
        for kk in tuple(rd) + tuple(ptr):
            lw = self.last_w.get(kk)
            if lw is not None:
                self._need(eng, waits, lw[0], lw[1])
        for kk in wr:
            lw = self.last_w.get(kk)
            if lw is not None:
                self._need(eng, waits, lw[0], lw[1])
            for sem, val in self.readers.get(kk, ()):
                self._need(eng, waits, sem, val)
        if dma:
            dk = tag if tag is not None else (wr[0] if wr else rd[0])
            sem = "D" + dk
            if sem not in self.cnt:
                self.cnt[sem] = 0
                self.sems[sem] = self.nc.alloc_semaphore(sem)
        else:
            sem = ENG_SEM[eng]
        inc = 16 * n_dma if dma else 1
        self.cnt[sem] += inc
        done = self.cnt[sem]
        self.q[eng].append((dict(waits), fn, sem, 16 if dma else 1))
        for s_, v_ in waits.items():
            self.satisfied[eng][s_] = max(self.satisfied[eng].get(s_, 0), v_)
        for kk in wr:
            self.last_w[kk] = (sem, done)
            self.readers[kk] = []
        for kk in tuple(rd) + tuple(ptr):
            self.readers.setdefault(kk, []).append((sem, done))
        self.nops += 1
        return done

    def cc_op(self, fn, rd=(), wr=()):
        waits = {}
        for kk in rd:
            lw = self.last_w.get(kk)
            if lw is not None:
                self._need("g", waits, lw[0], lw[1])
        for kk in wr:
            lw = self.last_w.get(kk)
            if lw is not None:
                self._need("g", waits, lw[0], lw[1])
            for sem, val in self.readers.get(kk, ()):
                self._need("g", waits, sem, val)
        self.cnt["cc"] += 1
        done = self.cnt["cc"]
        self.q["g"].append((dict(waits), fn, "cc", 1))
        for s_, v_ in waits.items():
            self.satisfied["g"][s_] = max(self.satisfied["g"].get(s_, 0), v_)
        for kk in wr:
            self.last_w[kk] = ("cc", done)
            self.readers[kk] = []
        for kk in rd:
            self.readers.setdefault(kk, []).append(("cc", done))
        return done

    def flush(self, name):
        nc = self.nc
        qs = self.q
        self.q = {e: [] for e in ENG_SEM}
        with nc.Block(name) as blk:
            def body(engkey):
                def run(e):
                    for waits, fn, sem, inc in qs[engkey]:
                        for s_, v_ in sorted(waits.items()):
                            e.wait_ge(self.sems[s_], v_)
                        res = fn(e)
                        if res is None:
                            raise RuntimeError(f"op fn returned None in {name}/{engkey}")
                        if isinstance(res, (list, tuple)):
                            for r in res:
                                r.then_inc(self.sems[sem], inc)
                        else:
                            res.then_inc(self.sems[sem], inc)
                return run
            if qs["s"]:
                blk.sync(body("s"))
            if qs["t"]:
                blk.tensor(body("t"))
            if qs["v"]:
                blk.vector(body("v"))
            if qs["sc"]:
                blk.scalar(body("sc"))
            if qs["g"]:
                blk.gpsimd(body("g"))

    def drain(self):
        tgt = dict(self.cnt)
        def body(e):
            for s_, v_ in tgt.items():
                if v_ > 0:
                    e.wait_ge(self.sems[s_], v_)
        self.ndrain += 1
        with self.nc.Block(f"drain{self.ndrain}") as blk:
            blk.sync(body); blk.tensor(body); blk.vector(body); blk.scalar(body); blk.gpsimd(body)
        for e in ENG_SEM:
            for s_, v_ in tgt.items():
                self.satisfied[e][s_] = max(self.satisfied[e].get(s_, 0), v_)



def mkconsts(nc):
    c = {}
    c["ident_bf"] = nc.inline_tensor(np.eye(128, dtype=BF), "c_ident_bf")
    c["ident_f32"] = nc.inline_tensor(np.eye(128, dtype=np.float32), "c_ident_f32")
    c["ones_row"] = nc.inline_tensor(np.ones((1, 128), dtype=np.float32), "c_ones_row")
    c["ones_col"] = nc.inline_tensor(np.ones((128, 1), dtype=np.float32), "c_ones_col")
    c["ones_col_bf"] = nc.inline_tensor(np.ones((128, 1), dtype=BF), "c_ones_col_bf")
    c["tri01"] = nc.inline_tensor(np.triu(np.ones((128, 128), dtype=np.float32)), "c_tri01")
    c["acol"] = nc.inline_tensor((-np.arange(1, 129, dtype=np.float32)).reshape(128, 1), "c_acol")
    bv = np.zeros((128, 4), dtype=np.float32)
    bv[:, 0] = 1e-5
    bv[:, 1] = 1.0
    c["biasv"] = nc.inline_tensor(bv, "c_biasv")
    e8 = np.zeros((8, 128), dtype=np.float32)
    for e in range(8):
        e8[e, e * 16:(e + 1) * 16] = 1.0
    c["e8"] = nc.inline_tensor(e8, "c_e8")
    return c


def sbuf_load_const(k, sbdict, name, dram, shape, dtype):
    t = k.nc.alloc_sbuf_tensor("sb_" + name, shape, dtype)
    sbdict[name] = t
    k.op("s", lambda e, d_=dram, t_=t: [e.dma_start(out=t_[:, :], in_=d_.ap())],
         wr=(name,), dma=True, n_dma=1)
    return t



LN_PARTS = ("stats", "apply", "tr")
LN_MAXCH = 99

def emit_ln(k, B, src_main, src_halo, xn_fm, tw, tag):
    nc = k.nc
    xtm, xsq, xbtm, st1, PST = B["xtm"], B["xsq"], B["xbtm"], B["st1"], B["PST"]
    identb = B["ident_bf"]
    nch = min(NTC + (1 if src_halo is not None else 0), LN_MAXCH)
    off = 3 if src_halo is not None else 0
    for i in range(nch):
        par = i % 2
        nr = 128 if i < NTC else 3
        r = slice(0, nr)
        co = 8 * par
        halo = i >= NTC
        if not halo:
            k.op("s", lambda e, i_=i, p_=par: [e.dma_start(out=xtm[p_][:], in_=src_main.ap()[128 * i_:128 * (i_ + 1), :])],
                 wr=(f"{tag}xtm{par}",), dma=True)
        else:
            k.op("s", lambda e, p_=par: [e.dma_start(out=xtm[p_][0:3, :], in_=src_halo.ap())],
                 wr=(f"{tag}xtm{par}",), dma=True)
        if "stats" not in LN_PARTS:
            continue
        k.op("sc", lambda e, p_=par, r_=r: e.activation(xsq[r_, :], xtm[p_][r_, :], AF.Square),
             rd=(f"{tag}xtm{par}",), wr=(f"{tag}xsq",))
        k.op("v", lambda e, r_=r, c_=co: e.reduce_sum(st1[r_, c_ + 1:c_ + 2], xsq[r_, :], AX.X),
             rd=(f"{tag}xsq",), wr=(f"{tag}s2{par}",))
        k.op("v", lambda e, p_=par, r_=r, c_=co: e.reduce_sum(st1[r_, c_ + 0:c_ + 1], xtm[p_][r_, :], AX.X),
             rd=(f"{tag}xtm{par}",), wr=(f"{tag}s1{par}",))
        k.op("v", lambda e, r_=r, c_=co: e.tensor_scalar(st1[r_, c_ + 2:c_ + 3], st1[r_, c_ + 0:c_ + 1],
                                                         1.0 / D, None, ALU.mult),
             rd=(f"{tag}s1{par}",), wr=(f"{tag}mn{par}",))
        k.op("v", lambda e, r_=r, c_=co: e.tensor_scalar(st1[r_, c_ + 3:c_ + 4], st1[r_, c_ + 1:c_ + 2],
                                                         1.0 / D, None, ALU.mult),
             rd=(f"{tag}s2{par}",), wr=(f"{tag}m2{par}",))
        k.op("v", lambda e, r_=r, c_=co: e.tensor_tensor(st1[r_, c_ + 4:c_ + 5], st1[r_, c_ + 2:c_ + 3],
                                                         st1[r_, c_ + 2:c_ + 3], ALU.mult),
             rd=(f"{tag}mn{par}",), wr=(f"{tag}mv{par}",))
        k.op("v", lambda e, r_=r, c_=co: e.tensor_tensor(st1[r_, c_ + 4:c_ + 5], st1[r_, c_ + 3:c_ + 4],
                                                         st1[r_, c_ + 4:c_ + 5], ALU.subtract),
             rd=(f"{tag}m2{par}", f"{tag}mv{par}"), wr=(f"{tag}mv{par}",))
        k.op("sc", lambda e, r_=r, c_=co: e.activation(st1[r_, c_ + 5:c_ + 6], st1[r_, c_ + 4:c_ + 5],
                                                       AF.Ln, bias=B["biasv"][r_, 0:1]),
             rd=(f"{tag}mv{par}", "biasv"), wr=(f"{tag}rs{par}",))
        k.op("sc", lambda e, r_=r, c_=co: e.activation(st1[r_, c_ + 5:c_ + 6], st1[r_, c_ + 5:c_ + 6],
                                                       AF.Exp, scale=-0.5),
             rd=(f"{tag}rs{par}",), wr=(f"{tag}rs{par}",))
        if "apply" not in LN_PARTS:
            continue
        k.op("v", lambda e, p_=par, r_=r, c_=co: e.tensor_scalar(xbtm[r_, :], xtm[p_][r_, :], st1[r_, c_ + 2:c_ + 3],
                                                                 st1[r_, c_ + 5:c_ + 6], ALU.subtract, ALU.mult),
             rd=(f"{tag}xtm{par}", f"{tag}mn{par}", f"{tag}rs{par}"), wr=(f"{tag}xb",))
        if "tr" not in LN_PARTS:
            continue
        for j in range(NDC):
            qp = j % 2
            k.op("t", lambda e, j_=j, qp_=qp: e.transpose(PST[:, 1024 * qp_:1024 * qp_ + 128],
                                                          xbtm[:, 128 * j_:128 * (j_ + 1)], identb[:, :]),
                 rd=(f"{tag}xb",), wr=(f"pst{qp}",))
            def evac(e, i_=i, j_=j, qp_=qp, halo_=halo):
                ncol = 128 if not halo_ else 3
                coff = (off + 128 * i_) if not halo_ else 0
                return e.activation(bass.AP(xn_fm, j_ * tw + coff, [[tw * NDC, 128], [1, ncol]]),
                                    PST[:, 1024 * qp_:1024 * qp_ + ncol], AF.Copy)
            k.op("sc", evac, rd=(f"pst{qp}",), wr=(f"{tag}xn",))



def emit_inproj(k, B, d):
    nc = k.nc
    xn_fm = B["xn_fm"]
    wbuf, gstage, ci, cbuf, sstage = B["wbuf"], B["gstage"], B["ci"], B["cbuf"], B["sstage"]
    dtraw, cw, cb = B["dtraw"], B["cw"], B["cb"]
    PS = [B["PS_A"], B["PS_B"]]
    PSTL = [B["PS_C"], B["PS_D"]]

    def fmslc(j, start, n):
        return xn_fm[:, j * TH + start: j * TH + start + n]

    for o in range(NOC_P):
        par = o % 2
        is_x = NOC_G <= o < NOC_G + NOC_X
        k.op("g", lambda e, o_=o, p_=par: [e.dma_start(out=wbuf[p_][:], in_=d["wip"].ap()[:, o_, :])],
             wr=(f"wb{par}",), dma=True)
        def mmg(e, o_=o, p_=par, isx=is_x):
            for j in range(NDC):
                w = wbuf[p_][:, 128 * j:128 * (j + 1)]
                st, sp = (j == 0), (j == NDC - 1)
                if isx:
                    e.matmul(PS[p_][:, 0:512], w, fmslc(j, 0, 512), start=st, stop=sp)
                    e.matmul(PS[p_][:, 512:1024], w, fmslc(j, 512, 512), start=st, stop=sp)
                    ti = e.matmul(PSTL[p_][:, 0:3], w, fmslc(j, 1024, 3), start=st, stop=sp)
                else:
                    e.matmul(PS[p_][:, 0:512], w, fmslc(j, 3, 512), start=st, stop=sp)
                    ti = e.matmul(PS[p_][:, 512:1024], w, fmslc(j, 515, 512), start=st, stop=sp)
            return ti
        wrps = (("psA0", "psA1", "psC") if par == 0 else ("psB0", "psB1", "psD")) if is_x else (("psA0", "psA1") if par == 0 else ("psB0", "psB1"))
        k.op("t", mmg, rd=(f"wb{par}", "ln0xn"), wr=wrps)
        if o < NOC_G:
            k.op("sc", lambda e, p_=par: e.activation(gstage[p_][:], PS[p_][:], AF.Copy),
                 rd=("psA0", "psA1") if par == 0 else ("psB0", "psB1"), wr=(f"gst{par}",))
            k.op("s", lambda e, o_=o, p_=par: [e.dma_start(out=d["gate"].ap()[128 * o_:128 * (o_ + 1), :],
                                                           in_=gstage[p_][:])],
                 rd=(f"gst{par}",), dma=True)
        elif is_x:
            cc = o - NOC_G
            parc = cc % 2
            def evx(e, p_=par, pc_=parc):
                e.activation(ci[pc_][:, 0:1024], PS[p_][:], AF.Copy)
                return e.activation(ci[pc_][:, 1024:1027], PSTL[p_][:, 0:3], AF.Copy)
            k.op("sc", evx, rd=("psA0", "psA1", "psC") if par == 0 else ("psB0", "psB1", "psD"), wr=(f"ci{parc}",))
            ceng = "v"
            k.op(ceng, lambda e, pc_=parc, cc_=cc: e.tensor_scalar(cbuf[pc_][:], ci[pc_][:, 0:1024],
                                                                   cw[:, 4 * cc_:4 * cc_ + 1], None, ALU.mult),
                 rd=(f"ci{parc}", "cw"), wr=(f"cb{parc}",))
            for jj in range(1, 4):
                k.op(ceng, lambda e, pc_=parc, cc_=cc, jj_=jj: e.scalar_tensor_tensor(
                         cbuf[pc_][:], ci[pc_][:, jj_:jj_ + 1024],
                         cw[:, 4 * cc_ + jj_:4 * cc_ + jj_ + 1], cbuf[pc_][:], ALU.mult, ALU.add),
                     rd=(f"ci{parc}", "cw", f"cb{parc}"), wr=(f"cb{parc}",))
            k.op("sc", lambda e, pc_=parc, cc_=cc: e.activation(sstage[pc_][:], cbuf[pc_][:], AF.Silu,
                                                                bias=cb[:, cc_:cc_ + 1]),
                 rd=(f"cb{parc}", "cbc"), wr=(f"ss{parc}",))
            if cc < 64:
                dst = d["xc"].ap()[128 * cc:128 * (cc + 1), :]
            elif cc < 72:
                dst = d["bmat"].ap()[128 * (cc - 64):128 * (cc - 63), :]
            else:
                dst = d["cmat"].ap()[128 * (cc - 72):128 * (cc - 71), :]
            k.op("s", lambda e, dst_=dst, pc_=parc: [e.dma_start(out=dst_, in_=sstage[pc_][:])],
                 rd=(f"ss{parc}",), dma=True)
        else:
            k.op("sc", lambda e, p_=par: e.activation(dtraw[:], PS[p_][:], AF.Copy),
                 rd=("psA0", "psA1") if par == 0 else ("psB0", "psB1"), wr=("dtraw",))



def emit_dtpost(k, B, d):
    nc = k.nc
    dtraw, dtclip, cs, expcs = B["dtraw"], B["dtclip"], B["cs"], B["expcs"]
    csT, dtT, cdall, cdrep, zz = B["csT"], B["dtT"], B["cdall"], B["cdrep"], B["zz"]
    t1, t2 = B["t1"], B["t2"]
    dtb, acol, identf, onesr = B["dtb"], B["acol"], B["ident_f32"], B["ones_row"]
    PSD, PSA = B["PS_D"], B["PS_A"]

    k.op("sc", lambda e: e.activation(t1[:], dtraw[:], AF.Abs, bias=dtb[:, :]),
         rd=("dtraw", "dtb"), wr=("sp_t1",))
    k.op("sc", lambda e: e.activation(t1[:], t1[:], AF.Exp, scale=-1.0), rd=("sp_t1",), wr=("sp_t1",))
    k.op("sc", lambda e: e.activation(t1[:], t1[:], AF.Ln, bias=B["biasv"][:, 1:2]),
         rd=("sp_t1", "biasv"), wr=("sp_t1",))
    k.op("sc", lambda e: e.activation(t2[:], dtraw[:], AF.Relu, bias=dtb[:, :]),
         rd=("dtraw", "dtb"), wr=("sp_t2",))
    k.op("v", lambda e: e.tensor_tensor(dtclip[:], t2[:], t1[:], ALU.add),
         rd=("sp_t1", "sp_t2"), wr=("dtclip",))
    k.op("v", lambda e: e.tensor_scalar(dtclip[:], dtclip[:], 0.001, 100.0, ALU.max, ALU.min),
         rd=("dtclip",), wr=("dtclip",))
    k.op("v", lambda e: e.tensor_scalar(t1[:], dtclip[:], acol[:, :], None, ALU.mult),
         rd=("dtclip", "acolc"), wr=("dA",))
    def mkcs(e):
        for kk in range(NCH):
            vi = e.tensor_tensor_scan(cs[:, 128 * kk:128 * (kk + 1)], t1[:, 128 * kk:128 * (kk + 1)],
                                      zz[:, :], 0.0, ALU.add, ALU.add)
        return vi
    k.op("v", mkcs, rd=("dA", "zz"), wr=("cs",))
    k.op("sc", lambda e: e.activation(expcs[:], cs[:], AF.Exp), rd=("cs",), wr=("expcs",))
    for q in range(16):
        src_ = cs if q < 8 else dtclip
        dstb = csT if q < 8 else dtT
        kx = q % 8
        qp = q % 2
        PCD = [B["PS_C"], B["PS_D"]]
        k.op("t", lambda e, s_=src_, kx_=kx, qp_=qp: e.transpose(PCD[qp_][:, 0:128],
                                                                 s_[:, 128 * kx_:128 * (kx_ + 1)], identf[:, :]),
             rd=("cs" if q < 8 else "dtclip",), wr=("psC" if qp == 0 else "psD",))
        k.op("sc", lambda e, d_=dstb, kx_=kx, qp_=qp: e.activation(d_[:, 128 * kx_:128 * (kx_ + 1)],
                                                                   PCD[qp_][:, 0:128], AF.Copy),
             rd=("psC" if qp == 0 else "psD",), wr=("csT" if q < 8 else "dtT",))
    k.op("v", lambda e: e.tensor_copy(cdall[:], bass.AP(expcs, 127, [[1024, 128], [128, 8]])),
         rd=("expcs",), wr=("cdall",))
    k.op("v", lambda e: e.tensor_tensor_scan(B["cdprod"][:], cdall[:], zz[:, 0:8], 1.0, ALU.mult, ALU.add),
         rd=("cdall",), wr=("cdprod",))
    k.op("s", lambda e: [e.dma_start(out=d["csf"].ap(), in_=cs[:])], rd=("cs",), wr=("csf_d",), dma=True)
    k.op("s", lambda e: [e.dma_start(out=d["ecsf"].ap(), in_=expcs[:])], rd=("expcs",), wr=("ecsf_d",), dma=True)
    k.op("s", lambda e: [e.dma_start(out=d["cdf"].ap(), in_=cdall[:])], rd=("cdall",), wr=("cdf_d",), dma=True)
    k.op("s", lambda e: [e.dma_start(out=B["cdflat"][0:1, :], in_=bass.AP(d["cdf"], 0, [[1024, 1], [1, 1024]]))],
         rd=("cdf_d",), wr=("cdflat",), dma=True)
    def cdmm(e):
        e.matmul(PSA[:, 0:512], onesr[:, :], B["cdflat"][0:1, 0:512], start=True, stop=True)
        return e.matmul(PSA[:, 512:1024], onesr[:, :], B["cdflat"][0:1, 512:1024], start=True, stop=True)
    k.op("t", cdmm, rd=("cdflat",), wr=("psA0", "psA1"))
    k.op("sc", lambda e: e.activation(cdrep[:], PSA[:], AF.Copy), rd=("psA0", "psA1"), wr=("cdrep",))



def emit_ffn0_early(k, B, d):
    nc = k.nc
    xn_fm = B["xn_fm"]
    PSD, PSA, PSB, PST = B["PS_D"], B["PS_A"], B["PS_B"], B["PST"]
    wbuf, gstage = B["wbuf"], B["gstage"]
    identf, onesr, onesc = B["ident_f32"], B["ones_row"], B["ones_col"]
    hfm, htm, hwrk = B["hfm"], B["htm"], B["hwrk"]
    gfm, gtm = B["gfm"], B["gtm"]
    st8 = B["st8"]
    m1w, m2w = B["m1w"], B["m2w"]
    w1T, w2T, cnt_sb = B["w1T"], B["w2T"], B["cnt_sb"]
    tmp_fm = B["tmp_fm"]
    gamB, betB = B["gamB"], B["betB"]

    def fmslc(j, start, n):
        return xn_fm[:, j * TH + start: j * TH + start + n]

    PSC = B["PS_C"]
    def gb(e):
        e.matmul(PSC[:, 0:32], onesr[:, :], B["g0gam"][0:1, :], start=True, stop=True)
        return e.matmul(PSC[:, 32:64], onesr[:, :], B["g0bet"][0:1, :], start=True, stop=True)
    k.op("t", gb, rd=("g0gam", "g0bet"), wr=("psC",))
    def gbev(e):
        e.activation(gamB[:], PSC[:, 0:32], AF.Copy)
        return e.activation(betB[:], PSC[:, 32:64], AF.Copy)
    k.op("sc", gbev, rd=("psC",), wr=("gamB",))
    k.op("v", lambda e: e.memset(cnt_sb[0:8, :], 0.0), wr=("cnt",))

    for tt in range(2):
        PCD = [B["PS_C"], B["PS_D"]]
        def hmm(e, tt_=tt):
            for j in range(NDC):
                ti = e.matmul(PCD[tt_][0:32, 0:512], B["gw1t"][:, 32 * j:32 * (j + 1)],
                              fmslc(j, 3 + 512 * tt_, 512), start=(j == 0), stop=(j == NDC - 1))
            return ti
        k.op("t", hmm, rd=("gw1t", "ln0xn"), wr=("psC" if tt == 0 else "psD",))
        k.op("sc", lambda e, tt_=tt: e.activation(hfm[0:32, 512 * tt_:512 * (tt_ + 1)], PCD[tt_][0:32, 0:512],
                                                  AF.Identity, bias=B["gb1c"][0:32, :]),
             rd=(("psC" if tt == 0 else "psD"), "gb1c"), wr=("hfm",))
    for i in range(NTC):
        k.op("t", lambda e, i_=i: e.transpose(PSA[:, 512 * (i_ % 2):512 * (i_ % 2) + 32],
                                              hfm[0:32, 128 * i_:128 * (i_ + 1)], identf[0:32, 0:32]),
             rd=("hfm",), wr=(f"psA{i % 2}",))
        k.op("sc", lambda e, i_=i: e.activation(htm[:, 32 * (i_ % 2):32 * (i_ % 2) + 32],
                                                PSA[:, 512 * (i_ % 2):512 * (i_ % 2) + 32], AF.Copy),
             rd=(f"psA{i % 2}",), wr=(f"htm{i % 2}",))
        k.op("sc", lambda e, i_=i: e.activation(hwrk[:, 0:32], htm[:, 32 * (i_ % 2):32 * (i_ % 2) + 32],
                                                AF.Square),
             rd=(f"htm{i % 2}",), wr=("hwsq",))
        k.op("v", lambda e: e.reduce_sum(st8[:, 1:2], hwrk[:, 0:32], AX.X),
             rd=("hwsq",), wr=("st8s2",))
        k.op("v", lambda e, i_=i: e.reduce_sum(st8[:, 0:1], htm[:, 32 * (i_ % 2):32 * (i_ % 2) + 32], AX.X),
             rd=(f"htm{i % 2}",), wr=("st8s1",))
        k.op("v", lambda e: e.tensor_scalar(st8[:, 2:3], st8[:, 0:1], 1.0 / 32, None, ALU.mult),
             rd=("st8s1",), wr=("st8mn",))
        k.op("v", lambda e: e.tensor_scalar(st8[:, 3:4], st8[:, 1:2], 1.0 / 32, None, ALU.mult),
             rd=("st8s2",), wr=("st8m2",))
        k.op("v", lambda e: e.tensor_tensor(st8[:, 4:5], st8[:, 2:3], st8[:, 2:3], ALU.mult),
             rd=("st8mn",), wr=("st8mv",))
        k.op("v", lambda e: e.tensor_tensor(st8[:, 4:5], st8[:, 3:4], st8[:, 4:5], ALU.subtract),
             rd=("st8m2", "st8mv"), wr=("st8mv",))
        k.op("sc", lambda e: e.activation(st8[:, 5:6], st8[:, 4:5], AF.Ln, bias=B["biasv"][:, 0:1]),
             rd=("st8mv", "biasv"), wr=("st8rs",))
        k.op("sc", lambda e: e.activation(st8[:, 5:6], st8[:, 5:6], AF.Exp, scale=-0.5),
             rd=("st8rs",), wr=("st8rs",))
        k.op("v", lambda e, i_=i: e.tensor_scalar(hwrk[:, 0:32], htm[:, 32 * (i_ % 2):32 * (i_ % 2) + 32],
                                                  st8[:, 2:3], st8[:, 5:6], ALU.subtract, ALU.mult),
             rd=(f"htm{i % 2}", "st8mn", "st8rs"), wr=(f"hr{i % 2}",))
        k.op("v", lambda e: e.tensor_tensor(hwrk[:, 0:32], hwrk[:, 0:32], gamB[:], ALU.mult),
             rd=(f"hr{i % 2}", "gamB"), wr=(f"hr{i % 2}",))
        k.op("v", lambda e: e.tensor_tensor(hwrk[:, 0:32], hwrk[:, 0:32], betB[:], ALU.add),
             rd=(f"hr{i % 2}", "gamB"), wr=(f"hr{i % 2}",))
        k.op("v", lambda e: e.tensor_scalar(hwrk[:, 0:32], hwrk[:, 0:32], 0.0, None, ALU.max),
             rd=(f"hr{i % 2}",), wr=(f"hr{i % 2}",))
        k.op("t", lambda e, i_=i: e.transpose(PSB[0:32, 512 * (i_ % 2):512 * (i_ % 2) + 128],
                                              hwrk[:, 0:32], identf[:, :]),
             rd=(f"hr{i % 2}",), wr=(f"psB{i % 2}",))
        k.op("sc", lambda e, i_=i: e.activation(hfm[0:32, 1024 + 128 * i_:1024 + 128 * (i_ + 1)],
                                                PSB[0:32, 512 * (i_ % 2):512 * (i_ % 2) + 128], AF.Copy),
             rd=(f"psB{i % 2}",), wr=("hrfm",))
    for tt in range(2):
        PCD = [B["PS_C"], B["PS_D"]]
        k.op("t", lambda e, tt_=tt: e.matmul(PCD[tt_][0:8, 0:512], B["gw2t"][0:32, :],
                                             hfm[0:32, 1024 + 512 * tt_:1024 + 512 * (tt_ + 1)], start=True, stop=True),
             rd=("gw2t", "hrfm"), wr=("psC" if tt == 0 else "psD",))
        k.op("sc", lambda e, tt_=tt: e.activation(gfm[0:8, 512 * tt_:512 * (tt_ + 1)], PCD[tt_][0:8, 0:512],
                                                  AF.Identity, bias=B["gb2c"][0:8, :]),
             rd=(("psC" if tt == 0 else "psD"), "gb2c"), wr=("gfm",))
    for i in range(NTC):
        k.op("t", lambda e, i_=i: e.transpose(PSA[:, 512 * (i_ % 2):512 * (i_ % 2) + 8],
                                              gfm[0:8, 128 * i_:128 * (i_ + 1)], identf[0:8, 0:8]),
             rd=("gfm",), wr=(f"psA{i % 2}",))
        k.op("sc", lambda e, i_=i: e.activation(gtm[:, 0:8], PSA[:, 512 * (i_ % 2):512 * (i_ % 2) + 8], AF.Copy),
             rd=(f"psA{i % 2}",), wr=("gtm",))
        k.op("v", lambda e: e.reduce_max(st8[:, 8:9], gtm[:, 0:8], AX.X), rd=("gtm",), wr=("m1s",))
        k.op("v", lambda e: e.tensor_single_scalar(m1w[:, 0:8], gtm[:, 0:8], st8[:, 8:9], ALU.is_equal),
             rd=("gtm", "m1s"), wr=("m1w",))
        k.op("v", lambda e: e.scalar_tensor_tensor(gtm[:, 8:16], m1w[:, 0:8], -1e30, gtm[:, 0:8], ALU.mult, ALU.add),
             rd=("gtm", "m1w"), wr=("gtm2",))
        k.op("v", lambda e: e.reduce_max(st8[:, 9:10], gtm[:, 8:16], AX.X), rd=("gtm2",), wr=("m2s",))
        k.op("v", lambda e: e.tensor_single_scalar(m2w[:, 0:8], gtm[:, 8:16], st8[:, 9:10], ALU.is_equal),
             rd=("gtm2", "m2s"), wr=("m2w",))
        k.op("v", lambda e: e.tensor_tensor(st8[:, 10:11], st8[:, 9:10], st8[:, 8:9], ALU.subtract),
             rd=("m1s", "m2s"), wr=("dms",))
        k.op("sc", lambda e: e.activation(st8[:, 11:12], st8[:, 10:11], AF.Exp), rd=("dms",), wr=("expd",))
        k.op("v", lambda e: e.tensor_scalar(st8[:, 12:13], st8[:, 11:12], 1.0, None, ALU.add),
             rd=("expd",), wr=("sgp",))
        k.op("v", lambda e: e.reciprocal(st8[:, 13:14], st8[:, 12:13]), rd=("sgp",), wr=("sg1",))
        k.op("v", lambda e: e.tensor_scalar(st8[:, 14:15], st8[:, 13:14], 1.0, -1.0, ALU.subtract, ALU.mult),
             rd=("sg1",), wr=("sgs",))
        def cntmm(e, i_=i):
            e.matmul(B["PS_C"][0:8, 0:1], m1w[:, 0:8], onesc[:, :], start=True, stop=True)
            return e.matmul(B["PS_C"][0:8, 1:2], m2w[:, 0:8], onesc[:, :], start=True, stop=True)
        k.op("t", cntmm, rd=("m1w", "m2w", "onesc"), wr=("psC",))
        k.op("v", lambda e: e.tensor_tensor(cnt_sb[0:8, :], cnt_sb[0:8, :], B["PS_C"][0:8, 0:2], ALU.add),
             rd=("psC", "cnt"), wr=("cnt",))
        k.op("v", lambda e: e.tensor_scalar(m1w[:, 8:16], m1w[:, 0:8], st8[:, 13:14], None, ALU.mult),
             rd=("m1w", "sg1"), wr=("w1tm",))
        k.op("v", lambda e: e.tensor_scalar(m2w[:, 8:16], m2w[:, 0:8], st8[:, 14:15], None, ALU.mult),
             rd=("m2w", "sgs"), wr=("w2tm",))
        k.op("t", lambda e, i_=i: e.transpose(PSB[0:8, 512 * (i_ % 2):512 * (i_ % 2) + 128],
                                              m1w[:, 8:16], identf[:, :]),
             rd=("w1tm",), wr=(f"psB{i % 2}",))
        k.op("sc", lambda e, i_=i: e.activation(w1T[0:8, 128 * i_:128 * (i_ + 1)],
                                                PSB[0:8, 512 * (i_ % 2):512 * (i_ % 2) + 128], AF.Copy),
             rd=(f"psB{i % 2}",), wr=("w1T",))
        k.op("t", lambda e, i_=i: e.transpose(PSB[0:8, 512 * ((i_ + 1) % 2):512 * ((i_ + 1) % 2) + 128],
                                              m2w[:, 8:16], identf[:, :]),
             rd=("w2tm",), wr=(f"psB{(i + 1) % 2}",))
        k.op("sc", lambda e, i_=i: e.activation(w2T[0:8, 128 * i_:128 * (i_ + 1)],
                                                PSB[0:8, 512 * ((i_ + 1) % 2):512 * ((i_ + 1) % 2) + 128], AF.Copy),
             rd=(f"psB{(i + 1) % 2}",), wr=("w2T",))
    
    k.op("g", lambda e: [e.dma_start(out=B["wbuf"][0][:], in_=d["a0t"].ap()[:, 0, :])], wr=("wb0",), dma=True)
    for tt in range(2):
        def tmm(e, tt_=tt):
            for j in range(NDC):
                ti = e.matmul(PSA[:, 512 * tt_:512 * (tt_ + 1)], B["wbuf"][0][:, 128 * j:128 * (j + 1)],
                              fmslc(j, 3 + 512 * tt_, 512), start=(j == 0), stop=(j == NDC - 1))
            return ti
        k.op("t", tmm, rd=("wb0", "ln0xn"), wr=(f"psA{tt}",))
        k.op("sc", lambda e, tt_=tt: e.activation(tmp_fm[:, 512 * tt_:512 * (tt_ + 1)], PSA[:, 512 * tt_:512 * (tt_ + 1)], AF.Copy),
             rd=(f"psA{tt}",), wr=("tmp",))
    for o in range(NDC):
        par = o % 2
        k.op("g", lambda e, o_=o, p_=par: [e.dma_start(out=B["wbuf"][p_][:], in_=d["w0t"].ap()[:, o_, :])],
             wr=(f"wb{par}",), dma=True)
        def mm(e, o_=o, p_=par):
            for j in range(NDC):
                w = B["wbuf"][p_][:, 128 * j:128 * (j + 1)]
                st, sp = (j == 0), (j == NDC - 1)
                e.matmul(PSA[:, 0:512] if p_ == 0 else PSB[:, 0:512], w, fmslc(j, 3, 512), start=st, stop=sp)
                ti = e.matmul(PSA[:, 512:1024] if p_ == 0 else PSB[:, 512:1024], w, fmslc(j, 515, 512), start=st, stop=sp)
            return ti
        k.op("t", mm, rd=(f"wb{par}", "ln0xn"), wr=("psA0", "psA1") if par == 0 else ("psB0", "psB1"))
        k.op("sc", lambda e, p_=par: e.activation(B["gstage"][p_][:], (PSA if p_ == 0 else PSB)[:], AF.Copy),
             rd=("psA0", "psA1") if par == 0 else ("psB0", "psB1"), wr=(f"gst{par}",))
        k.op("s", lambda e, o_=o, p_=par: [e.dma_start(out=d["ffmain"].ap()[128 * o_:128 * (o_ + 1), :],
                                                       in_=B["gstage"][p_][:])],
             rd=(f"gst{par}",), dma=True)



def emit_ssda(k, B, d):
    nc = k.nc
    PSA, PSB, PSC, PSD, PST = B["PS_A"], B["PS_B"], B["PS_C"], B["PS_D"], B["PST"]
    xch, bbf, cbf = B["xch"], B["bbf"], B["cbf"]
    csrow = B["csrow"]
    csT, dtT, cdrep = B["csT"], B["dtT"], B["cdrep"]
    gm, btok, xt, xdtp, xdd = B["gm"], B["btok"], B["xt"], B["xdtp"], B["xdd"]
    seg, m4, sm = B["seg"], B["m4"], B["sm"]
    carry, sts, yst = B["carry"], B["sts"], B["yst"]
    tri, onesr, identb = B["tri01"], B["ones_row"], B["ident_bf"]
    Dc = B["Dcols"]

    for kk in range(NCH):
        par = kk % 2
        for grp8 in range(8):
            def ldx(e, kk_=kk, g8=grp8, p_=par):
                out = []
                for q in range(8):
                    pr = 8 * g8 + q
                    out.append(e.dma_start(out=xch[p_][:, 128 * pr:128 * (pr + 1)],
                                           in_=d["xc"].ap()[128 * pr:128 * (pr + 1), 128 * kk_:128 * (kk_ + 1)]))
                return out
            k.op("s", ldx, wr=(f"xch{par}",), dma=True, n_dma=8)
        def ldb(e, kk_=kk, p_=par):
            return [e.dma_start(out=bbf[p_][:, 128 * g:128 * (g + 1)],
                                in_=d["bmat"].ap()[128 * g:128 * (g + 1), 128 * kk_:128 * (kk_ + 1)]) for g in range(8)]
        k.op("s", ldb, wr=(f"bbf{par}",), dma=True, n_dma=8)
        def ldc(e, kk_=kk, p_=par):
            return [e.dma_start(out=cbf[p_][:, 128 * g:128 * (g + 1)],
                                in_=d["cmat"].ap()[128 * g:128 * (g + 1), 128 * kk_:128 * (kk_ + 1)]) for g in range(8)]
        k.op("s", ldc, wr=(f"cbf{par}",), dma=True, n_dma=8)
        for g in range(NG):
            if g % 4 == 0:
                half = g // 4
                def ldcs(e, kk_=kk, h_=half):
                    src = bass.AP(d["csf"], 64 * h_ * 1024 + 128 * kk_, [[1024, 64], [1, 128]])
                    return [e.dma_start(out=csrow[0:1, :], in_=src)]
                k.op("s", ldcs, wr=("csrow",), dma=True)
            gp = g % 2
            k.op("t", lambda e, g_=g, p_=par, gp_=gp: e.matmul(
                     PSA[:, 512 * gp_:512 * gp_ + 128], bbf[p_][:, 128 * g_:128 * (g_ + 1)],
                     cbf[p_][:, 128 * g_:128 * (g_ + 1)], start=True, stop=True),
                 rd=(f"bbf{par}", f"cbf{par}"), wr=(f"psA{gp}",)),
            k.op("v", lambda e, gp_=gp: e.tensor_tensor(gm[gp_][:], PSA[:, 512 * gp_:512 * gp_ + 128],
                                                        tri[:, :], ALU.mult),
                 rd=(f"psA{gp}", "tri"), wr=(f"gm{gp}",))
            k.op("t", lambda e, g_=g, p_=par, gp_=gp: e.transpose(
                     PST[:, 1024 * gp_:1024 * gp_ + 128], bbf[p_][:, 128 * g_:128 * (g_ + 1)], identb[:, :]),
                 rd=(f"bbf{par}",), wr=(f"pst{gp}",))
            k.op("sc", lambda e, gp_=gp: e.activation(btok[gp_][:], PST[:, 1024 * gp_:1024 * gp_ + 128], AF.Copy),
                 rd=(f"pst{gp}",), wr=(f"btok{gp}",))
            for bq in range(4):
                h0 = 16 * g + 4 * bq
                bp = (4 * g + bq) % 2
                for pr2 in range(2):
                    jp = h0 // 2 + pr2
                    jpp = jp % 2
                    k.op("t", lambda e, jp_=jp, p_=par, jpp_=jpp: e.transpose(
                             PST[:, 1024 * jpp_ + 128:1024 * jpp_ + 256],
                             xch[p_][:, 128 * jp_:128 * (jp_ + 1)], identb[:, :]),
                         rd=(f"xch{par}",), wr=(f"pst{jpp}",))
                    k.op("sc", lambda e, jpp_=jpp: e.activation(xt[jpp_][:],
                             PST[:, 1024 * jpp_ + 128:1024 * jpp_ + 256], AF.Copy),
                         rd=(f"pst{jpp}",), wr=(f"xt{jpp}",))
                    k.op("v", lambda e, jpp_=jpp, kk_=kk, jp_=jp: e.tensor_tensor(
                             xdtp[jpp_][:].rearrange("p (a l) -> p a l", a=2),
                             xt[jpp_][:].rearrange("p (a l) -> p a l", a=2),
                             dtT[:, 128 * kk_ + 2 * jp_: 128 * kk_ + 2 * jp_ + 2].unsqueeze(2).broadcast_to((128, 2, 64)),
                             ALU.mult),
                         rd=(f"xt{jpp}", "dtT"), wr=(f"xdtp{jpp}",))
                k.op("t", lambda e, h0_=h0, bp_=bp: e.matmul(
                         PSA[:, 512 * bp_:512 * (bp_ + 1)], onesr[:, :],
                         csrow[0:1, 128 * (h0_ % 64):128 * (h0_ % 64) + 512], start=True, stop=True),
                     rd=("csrow", "onesr"), wr=(f"psA{bp}",))
                def segop(e, h0_=h0, bp_=bp, kk_=kk):
                    e.tensor_tensor(seg[bp_][:].rearrange("p (a l) -> p a l", a=4),
                                    PSA[:, 512 * bp_:512 * (bp_ + 1)].rearrange("p (a l) -> p a l", a=4),
                                    csT[:, 128 * kk_ + h0_:128 * kk_ + h0_ + 4].unsqueeze(2).broadcast_to((128, 4, 128)),
                                    ALU.subtract)
                    return e.tensor_scalar(seg[bp_][:], seg[bp_][:], 0.0, None, ALU.min)
                k.op("v", segop, rd=(f"psA{bp}", "csT"), wr=(f"seg{bp}",))
                k.op("sc", lambda e, bp_=bp: e.activation(m4[bp_][:], seg[bp_][:], AF.Exp),
                     rd=(f"seg{bp}",), wr=(f"m4{bp}",))
                k.op("v", lambda e, bp_=bp, gp_=gp: e.tensor_tensor(
                         sm[bp_][:].rearrange("p (a l) -> p a l", a=4),
                         m4[bp_][:].rearrange("p (a l) -> p a l", a=4),
                         gm[gp_][:].unsqueeze(1).broadcast_to((128, 4, 128)), ALU.mult),
                     rd=(f"m4{bp}", f"gm{gp}"), wr=(f"sm{bp}",))
                for pr2 in range(2):
                    jp = h0 // 2 + pr2
                    jpp = jp % 2
                    k.op("v", lambda e, jpp_=jpp, bp_=bp, pr2_=pr2: e.tensor_tensor(
                             xdd[jpp_][:].rearrange("p (a l) -> p a l", a=2),
                             xdtp[jpp_][:].rearrange("p (a l) -> p a l", a=2),
                             bass.AP(m4[bp_], 127 + 256 * pr2_, [[512, 128], [128, 2], [0, 64]]),
                             ALU.mult),
                         rd=(f"xdtp{jpp}", f"m4{bp}"), wr=(f"xdd{jpp}",))
                for pr2 in range(2):
                    jp = h0 // 2 + pr2
                    jpp = jp % 2
                    def ydiag(e, jpp_=jpp, bp_=bp, pr2_=pr2):
                        e.matmul(PSB[0:64, 512 * jpp_:512 * jpp_ + 128],
                                 xdtp[jpp_][:, 0:64], sm[bp_][:, 256 * pr2_:256 * pr2_ + 128], start=True, stop=True)
                        return e.matmul(PSB[64:128, 512 * jpp_:512 * jpp_ + 128],
                                        xdtp[jpp_][:, 64:128], sm[bp_][:, 256 * pr2_ + 128:256 * pr2_ + 256],
                                        start=True, stop=True)
                    k.op("t", ydiag, rd=(f"xdtp{jpp}", f"sm{bp}"), wr=(f"psB{jpp}",))
                PCD = [PSC, PSD]
                def stmm(e, bp_=bp, gp_=gp, h0_=h0):
                    for a in range(4):
                        jpp_ = ((h0_ + a) // 2) % 2
                        ti = e.matmul(PCD[bp_][:, 64 * a:64 * (a + 1)],
                                      btok[gp_][:], xdd[jpp_][:, 64 * (a % 2):64 * (a % 2) + 64],
                                      start=True, stop=True)
                    return ti
                k.op("t", stmm, rd=(f"btok{gp}", "xdd0", "xdd1"), wr=("psC" if bp == 0 else "psD",))
                k.op("sc", lambda e, bp_=bp, h0_=h0: e.activation(sts[:, 64 * h0_:64 * (h0_ + 4)],
                                                                  PCD[bp_][:, 0:256], AF.Copy),
                     rd=("psC" if bp == 0 else "psD",), wr=("sts",))
                def carryop(e, h0_=h0, kk_=kk):
                    for a in range(4):
                        h = h0_ + a
                        vi = e.scalar_tensor_tensor(carry[:, 64 * h:64 * (h + 1)], carry[:, 64 * h:64 * (h + 1)],
                                                    cdrep[:, 8 * h + kk_:8 * h + kk_ + 1],
                                                    sts[:, 64 * h:64 * (h + 1)], ALU.mult, ALU.add)
                    return vi
                k.op("v", carryop, rd=("sts", "carry", "cdrep"), wr=("carry",))
                for pr2 in range(2):
                    jp = h0 // 2 + pr2
                    jpp = jp % 2
                    ys = jp % 4
                    def ya(e, jp_=jp, jpp_=jpp, ys_=ys, p_=par):
                        return e.scalar_tensor_tensor(yst[ys_][:], xch[p_][:, 128 * jp_:128 * (jp_ + 1)],
                                                      Dc[:, jp_:jp_ + 1], PSB[:, 512 * jpp_:512 * jpp_ + 128],
                                                      ALU.mult, ALU.add)
                    k.op("v", ya, rd=(f"psB{jpp}", f"xch{par}", "Dcols"), wr=(f"yst{ys}",))
                    k.op("s", lambda e, jp_=jp, kk_=kk, ys_=ys: [e.dma_start(
                             out=d["ya"].ap()[128 * jp_:128 * (jp_ + 1), 128 * kk_:128 * (kk_ + 1)],
                             in_=yst[ys_][:])],
                         rd=(f"yst{ys}",), dma=True)
        k.op("s", lambda e, kk_=kk: [e.dma_start(out=d["std"].ap()[kk_, :, :], in_=sts[:, :])],
             rd=("sts",), dma=True)
    k.op("s", lambda e: [e.dma_start(out=bass.AP(d["ccin"], 0, [[8192, 128], [1, 8192]]), in_=carry[:, :])],
         rd=("carry",), wr=("ccin",), dma=True)
    k.op("s", lambda e: [e.dma_start(out=bass.AP(d["ccin"], 1048576, [[1, 128], [1, 1]]), in_=B["cdprod"][:, 7:8])],
         rd=("cdprod",), wr=("ccin",), dma=True)
    k.op("s", lambda e: [e.dma_start(out=bass.AP(d["ccin"], 1048704, [[2, 8], [1, 2]]), in_=B["cnt_sb"][0:8, :])],
         rd=("cnt",), wr=("ccin",), dma=True)



def build_launchA(external_scratch=True, phases=("ln", "ip", "dt", "ffn", "ssda"), dbg_xn=False):
    from contextlib import ExitStack
    nc = bass.Bass("TRN2", target_bir_lowering=False, detect_race_conditions=False)
    kind_s = "ExternalOutput" if external_scratch else "Internal"
    d = {}
    d["xin"] = nc.dram_tensor("xin", [T, D], F32, kind="ExternalInput")
    d["xhalo"] = nc.dram_tensor("xhalo", [3, D], F32, kind="ExternalInput")
    d["wip"] = nc.dram_tensor("wip", [128, NOC_P, NDC * 128], F32, kind="ExternalInput")
    d["cwc"] = nc.dram_tensor("cwc", [128, 320], F32, kind="ExternalInput")
    d["cbc"] = nc.dram_tensor("cbc", [128, 80], F32, kind="ExternalInput")
    d["dtbc"] = nc.dram_tensor("dtbc", [128, 1], F32, kind="ExternalInput")
    d["dcols"] = nc.dram_tensor("dcols", [128, 64], F32, kind="ExternalInput")
    d["gw1td"] = nc.dram_tensor("gw1td", [128, 1024], F32, kind="ExternalInput")
    d["gb1cd"] = nc.dram_tensor("gb1cd", [32, 1], F32, kind="ExternalInput")
    d["g0gamd"] = nc.dram_tensor("g0gamd", [1, 32], F32, kind="ExternalInput")
    d["g0betd"] = nc.dram_tensor("g0betd", [1, 32], F32, kind="ExternalInput")
    d["gw2td"] = nc.dram_tensor("gw2td", [32, 8], F32, kind="ExternalInput")
    d["gb2cd"] = nc.dram_tensor("gb2cd", [8, 1], F32, kind="ExternalInput")
    d["a0t"] = nc.dram_tensor("a0t", [128, 1, NDC * 128], F32, kind="ExternalInput")
    d["w0t"] = nc.dram_tensor("w0t", [128, NDC, NDC * 128], F32, kind="ExternalInput")
    d["gate"] = nc.dram_tensor("gate", [INTER, T], F32, kind=kind_s)
    d["xc"] = nc.dram_tensor("xc", [INTER, T], BF16, kind=kind_s)
    d["bmat"] = nc.dram_tensor("bmat", [NG * SS, T], BF16, kind=kind_s)
    d["cmat"] = nc.dram_tensor("cmat", [NG * SS, T], BF16, kind=kind_s)
    d["ya"] = nc.dram_tensor("ya", [INTER, T], F32, kind=kind_s)
    d["std"] = nc.dram_tensor("std", [NCH, 128, INTER], F32, kind=kind_s)
    d["csf"] = nc.dram_tensor("csf", [128, T], F32, kind=kind_s)
    d["ecsf"] = nc.dram_tensor("ecsf", [128, T], F32, kind=kind_s)
    d["cdf"] = nc.dram_tensor("cdf", [128, 8], F32, kind="Internal")
    d["ccin"] = nc.dram_tensor("ccin", [1, 1048720], F32, kind=kind_s)
    d["tmpd"] = nc.dram_tensor("tmpd", [128, T], F32, kind=kind_s)
    d["w12d"] = nc.dram_tensor("w12d", [16, T], F32, kind=kind_s)
    d["cdrepd"] = nc.dram_tensor("cdrepd", [128, T], F32, kind=kind_s)
    d["ffmain"] = nc.dram_tensor("ffmain", [D, T], F32, kind=kind_s)

    c = mkconsts(nc)
    k = K(nc)
    B = {}
    B["PS_A"] = nc.alloc_psum_tensor("PS_A", [128, 1024], F32)
    B["PS_B"] = nc.alloc_psum_tensor("PS_B", [128, 1024], F32)
    B["PS_C"] = nc.alloc_psum_tensor("PS_C", [128, 512], F32)
    B["PS_D"] = nc.alloc_psum_tensor("PS_D", [128, 512], F32)
    B["PST"] = nc.alloc_psum_tensor("PST", [128, 2048], BF16)
    per = [("ident_bf", c["ident_bf"], [128, 128], BF16, "identb"),
           ("ident_f32", c["ident_f32"], [128, 128], F32, "identf"),
           ("ones_row", c["ones_row"], [1, 128], F32, "onesr"),
           ("ones_col", c["ones_col"], [128, 1], F32, "onesc"),
           ("tri01", c["tri01"], [128, 128], F32, "tri"),
           ("acol", c["acol"], [128, 1], F32, "acolc"),
           ("cw", d["cwc"], [128, 320], F32, "cw"),
           ("cb", d["cbc"], [128, 80], F32, "cbc"),
           ("dtb", d["dtbc"], [128, 1], F32, "dtb"),
           ("Dcols", d["dcols"], [128, 64], F32, "Dcols"),
           ("gb1c", d["gb1cd"], [32, 1], F32, "gb1c"),
           ("g0gam", d["g0gamd"], [1, 32], F32, "g0gam"),
           ("g0bet", d["g0betd"], [1, 32], F32, "g0bet"),
           ("biasv", c["biasv"], [128, 4], F32, "biasv"),
           ("gw2t", d["gw2td"], [32, 8], F32, "gw2t"),
           ("gb2c", d["gb2cd"], [8, 1], F32, "gb2c")]
    for name, src, shape, dt_, key in per:
        t = nc.alloc_sbuf_tensor("sb_" + name, shape, dt_)
        B[name] = t
        k.op("s", lambda e, s_=src, t_=t: [e.dma_start(out=t_[:, :], in_=s_.ap())], wr=(key,), dma=True)
    B["gw1t"] = nc.alloc_sbuf_tensor("sb_gw1t", [128, 1024], BF16)
    k.op("g", lambda e: [e.dma_start(out=B["gw1t"][:, :], in_=d["gw1td"].ap())], wr=("gw1t",), dma=True)
    k.flush("consts")
    k.drain()
    for nm, shape, dt_ in [("csT", [128, 1024], F32), ("dtT", [128, 1024], F32),
                           ("cdrep", [128, 1024], F32), ("cdall", [128, 8], F32),
                           ("cdprod", [128, 8], F32), ("zz", [128, 128], F32),
                           ("cdflat", [1, 1024], F32), ("tmp_fm", [128, 1024], F32),
                           ("w1T", [8, 1024], F32), ("w2T", [8, 1024], F32),
                           ("cnt_sb", [8, 2], F32), ("st8", [128, 16], F32)]:
        B[nm] = nc.alloc_sbuf_tensor("sb_" + nm, shape, dt_)
    k.op("v", lambda e: e.memset(B["zz"][:], 0.0), wr=("zz",))

    stack = ExitStack()
    xn_g = nc.sbuf_tensor("xn_fm", [128, NDC * TH], BF16)
    B["xn_fm"] = stack.enter_context(xn_g)

    ln_stack = ExitStack()
    B["xtm"] = [ln_stack.enter_context(nc.sbuf_tensor(f"xtm{i}", [128, D], F32)) for i in range(2)]
    B["xsq"] = ln_stack.enter_context(nc.sbuf_tensor("xsq", [128, D], F32))
    B["xbtm"] = ln_stack.enter_context(nc.sbuf_tensor("xbtm", [128, D], BF16))
    B["st1"] = ln_stack.enter_context(nc.sbuf_tensor("st1", [128, 16], F32))
    if "ln" in phases:
        emit_ln(k, B, d["xin"], d["xhalo"], B["xn_fm"], TH, "ln0")
    if dbg_xn:
        d["xndbg"] = nc.dram_tensor("xndbg", [128, NDC * TH], BF16, kind="ExternalOutput")
        k.op("s", lambda e: [e.dma_start(out=d["xndbg"].ap(), in_=B["xn_fm"][:, :])], rd=("ln0xn",), dma=True)
    k.flush("ln0")
    k.drain()
    ln_stack.close()

    ip_stack = ExitStack()
    B["wbuf"] = [ip_stack.enter_context(nc.sbuf_tensor(f"wbuf{i}", [128, 4096], BF16)) for i in range(2)]
    B["gstage"] = [ip_stack.enter_context(nc.sbuf_tensor(f"gstage{i}", [128, 1024], F32)) for i in range(2)]
    B["ci"] = [ip_stack.enter_context(nc.sbuf_tensor(f"ci{i}", [128, 1027], F32)) for i in range(2)]
    B["cbuf"] = [ip_stack.enter_context(nc.sbuf_tensor(f"cbuf{i}", [128, 1024], F32)) for i in range(2)]
    B["sstage"] = [ip_stack.enter_context(nc.sbuf_tensor(f"sstage{i}", [128, 1024], BF16)) for i in range(2)]
    B["dtraw"] = ip_stack.enter_context(nc.sbuf_tensor("dtraw", [128, 1024], F32))
    B["dtclip"] = ip_stack.enter_context(nc.sbuf_tensor("dtclip", [128, 1024], F32))
    B["cs"] = ip_stack.enter_context(nc.sbuf_tensor("cs", [128, 1024], F32))
    B["expcs"] = ip_stack.enter_context(nc.sbuf_tensor("expcs", [128, 1024], F32))
    B["t1"] = ip_stack.enter_context(nc.sbuf_tensor("t1", [128, 1024], F32))
    B["t2"] = ip_stack.enter_context(nc.sbuf_tensor("t2", [128, 1024], F32))
    if "ip" in phases:
        emit_inproj(k, B, d)
    k.flush("inproj")
    if "dt" in phases:
        emit_dtpost(k, B, d)
    k.op("s", lambda e: [e.dma_start(out=d["cdrepd"].ap(), in_=B["cdrep"][:, :])], rd=("cdrep",), dma=True)
    k.flush("dtpost")
    k.drain()
    ip_stack.close()

    f_stack = ExitStack()
    B["wbuf"] = [f_stack.enter_context(nc.sbuf_tensor(f"fwbuf{i}", [128, 4096], BF16)) for i in range(2)]
    B["gstage"] = [f_stack.enter_context(nc.sbuf_tensor(f"fgst{i}", [128, 1024], F32)) for i in range(2)]
    B["hfm"] = f_stack.enter_context(nc.sbuf_tensor("hfm", [32, 2048], F32))
    B["htm"] = f_stack.enter_context(nc.sbuf_tensor("htm", [128, 64], F32))
    B["hwrk"] = f_stack.enter_context(nc.sbuf_tensor("hwrk", [128, 32], F32))
    B["gfm"] = f_stack.enter_context(nc.sbuf_tensor("gfm", [8, 1024], F32))
    B["gtm"] = f_stack.enter_context(nc.sbuf_tensor("gtm", [128, 16], F32))
    B["m1w"] = f_stack.enter_context(nc.sbuf_tensor("m1w", [128, 16], F32))
    B["m2w"] = f_stack.enter_context(nc.sbuf_tensor("m2w", [128, 16], F32))
    B["gamB"] = f_stack.enter_context(nc.sbuf_tensor("gamB", [128, 32], F32))
    B["betB"] = f_stack.enter_context(nc.sbuf_tensor("betB", [128, 32], F32))
    if "ffn" in phases:
        emit_ffn0_early(k, B, d)
    k.op("s", lambda e: [e.dma_start(out=d["tmpd"].ap(), in_=B["tmp_fm"][:, :])], rd=("tmp",), dma=True)
    k.op("s", lambda e: [e.dma_start(out=d["w12d"].ap()[0:8, :], in_=B["w1T"][0:8, :])], rd=("w1T",), dma=True)
    k.op("s", lambda e: [e.dma_start(out=d["w12d"].ap()[8:16, :], in_=B["w2T"][0:8, :])], rd=("w2T",), dma=True)
    k.flush("ffn0e")
    k.drain()
    f_stack.close()
    stack.close()

    s_stack = ExitStack()
    B["xch"] = [s_stack.enter_context(nc.sbuf_tensor(f"xch{i}", [128, INTER], BF16)) for i in range(2)]
    B["bbf"] = [s_stack.enter_context(nc.sbuf_tensor(f"bbf{i}", [128, 1024], BF16)) for i in range(2)]
    B["cbf"] = [s_stack.enter_context(nc.sbuf_tensor(f"cbf{i}", [128, 1024], BF16)) for i in range(2)]
    B["csrow"] = s_stack.enter_context(nc.sbuf_tensor("csrow", [1, 8192], F32))
    B["gm"] = [s_stack.enter_context(nc.sbuf_tensor(f"gmb{i}", [128, 128], BF16)) for i in range(2)]
    B["btok"] = [s_stack.enter_context(nc.sbuf_tensor(f"btok{i}", [128, 128], BF16)) for i in range(2)]
    B["xt"] = [s_stack.enter_context(nc.sbuf_tensor(f"xtb{i}", [128, 128], BF16)) for i in range(2)]
    B["xdtp"] = [s_stack.enter_context(nc.sbuf_tensor(f"xdtp{i}", [128, 128], BF16)) for i in range(2)]
    B["xdd"] = [s_stack.enter_context(nc.sbuf_tensor(f"xdd{i}", [128, 128], BF16)) for i in range(2)]
    B["seg"] = [s_stack.enter_context(nc.sbuf_tensor(f"seg{i}", [128, 512], F32)) for i in range(2)]
    B["m4"] = [s_stack.enter_context(nc.sbuf_tensor(f"m4{i}", [128, 512], BF16)) for i in range(2)]
    B["sm"] = [s_stack.enter_context(nc.sbuf_tensor(f"smb{i}", [128, 512], BF16)) for i in range(2)]
    B["carry"] = s_stack.enter_context(nc.sbuf_tensor("carry", [128, INTER], F32))
    B["sts"] = s_stack.enter_context(nc.sbuf_tensor("sts", [128, INTER], F32))
    B["yst"] = [s_stack.enter_context(nc.sbuf_tensor(f"yst{i}", [128, 128], F32)) for i in range(4)]
    k.op("v", lambda e: e.memset(B["carry"][:], 0.0), wr=("carry",))
    if "ssda" in phases:
        emit_ssda(k, B, d)
    k.flush("ssda")
    k.drain()
    s_stack.close()
    return nc, k, d



def prep_launchA_inputs(ins, core):
    x = np.asarray(ins["L0_attn_hidden_states"]).reshape(SEQ, D)
    t0 = T * core
    xin = np.ascontiguousarray(x[t0:t0 + T])
    xhalo = np.zeros((3, D), np.float32) if core == 0 else np.ascontiguousarray(x[t0 - 3:t0])
    W = np.asarray(ins["L0_attn_in_proj_w"])
    m = {
        "xin": xin, "xhalo": xhalo,
        "wip": tile_w(W),
        "cwc": np.ascontiguousarray(np.asarray(ins["L0_attn_conv_w"])[:, 0, :].reshape(80, 128, 4).transpose(1, 0, 2).reshape(128, 320)),
        "cbc": col_layout(ins["L0_attn_conv_b"]),
        "dtbc": np.ascontiguousarray(np.asarray(ins["L0_attn_dt_bias"]).reshape(128, 1)),
        "gw1td": np.ascontiguousarray(np.asarray(ins["L0_ffn_gw1"]).T.reshape(32, 128, 32).transpose(1, 0, 2).reshape(128, 1024)),
        "gb1cd": np.ascontiguousarray(np.asarray(ins["L0_ffn_gb1"]).reshape(32, 1)),
        "g0gamd": np.ascontiguousarray(np.asarray(ins["L0_ffn_ln_gamma"]).reshape(1, 32)),
        "g0betd": np.ascontiguousarray(np.asarray(ins["L0_ffn_ln_beta"]).reshape(1, 32)),
        "gw2td": np.ascontiguousarray(np.asarray(ins["L0_ffn_gw2"]).T),
        "gb2cd": np.ascontiguousarray(np.asarray(ins["L0_ffn_gb2"]).reshape(8, 1)),
        "a0t": tile_w(np.ascontiguousarray(np.asarray(ins["L0_ffn_lora_A"]).transpose(1, 0, 2).reshape(D, 128)).T),
        "w0t": tile_w(np.asarray(ins["L0_ffn_weight"])),
    }
    Dv = np.asarray(ins["L0_attn_D"])
    dc = np.zeros((128, 64), np.float32)
    dc[0:64, :] = Dv[0::2][None, :]
    dc[64:128, :] = Dv[1::2][None, :]
    m["dcols"] = dc
    return m



def emit_prefix(k, B, d):
    nc = k.nc
    drow, dprow, fall = B["drow"], B["dprow"], B["fall"]
    k.op("s", lambda e: [e.dma_start(out=drow[0:1, :], in_=bass.AP(d["ccg"], 1048576, [[1048720, 8], [1, 128]]))],
         wr=("drow",), dma=True)
    k.op("v", lambda e: e.tensor_tensor(dprow[0:1, :], drow[0:1, :], B["mrow"][0:1, :], ALU.mult),
         rd=("drow", "mrow"), wr=("dprow",))
    k.op("v", lambda e: e.tensor_tensor(dprow[0:1, :], dprow[0:1, :], B["invmrow"][0:1, :], ALU.add),
         rd=("dprow", "mrow"), wr=("dprow",))
    sfx = B["sfxrow"]
    k.op("v", lambda e: e.memset(sfx[0:1, :], 1.0), wr=("sfx",))
    for b in range(7, -1, -1):
        k.op("v", lambda e, b_=b: e.tensor_tensor(fall[0:1, 128 * b_:128 * (b_ + 1)], sfx[0:1, :],
                                                  B["mrow"][0:1, 128 * b_:128 * (b_ + 1)], ALU.mult),
             rd=("sfx", "mrow"), wr=("fall",))
        if b > 0:
            k.op("v", lambda e, b_=b: e.tensor_tensor(sfx[0:1, :], sfx[0:1, :],
                                                      dprow[0:1, 128 * b_:128 * (b_ + 1)], ALU.mult),
                 rd=("sfx", "dprow"), wr=("sfx",))
    for hh in range(2):
        k.op("v", lambda e, hh_=hh: e.memset(B["prev"][hh_][:], 0.0), wr=(f"prev{hh}",))
    for b in range(8):
        bp = b % 2
        k.op("t", lambda e, b_=b, bp_=bp: e.matmul(B["PS_A"][:, 512 * bp_:512 * bp_ + 128], B["ones_row"][:, :],
                                                   fall[0:1, 128 * b_:128 * (b_ + 1)], start=True, stop=True),
             rd=("fall", "onesr"), wr=(f"psA{bp}",))
        k.op("sc", lambda e, bp_=bp: e.activation(B["fbb"][bp_][:], B["PS_A"][:, 512 * bp_:512 * bp_ + 128], AF.Copy),
             rd=(f"psA{bp}",), wr=(f"fbb{bp}",))
        for hh in range(2):
            k.op("s", lambda e, b_=b, hh_=hh, bp_=bp: [e.dma_start(
                     out=B["lbuf"][bp_][:, :],
                     in_=bass.AP(d["ccg"], b_ * 1048720 + 4096 * hh_, [[8192, 128], [1, 4096]]))],
                 wr=(f"lbuf{bp}",), dma=True, tag=f"lbuf{bp}")
            k.op("v", lambda e, b_=b, hh_=hh, bp_=bp: e.tensor_tensor(
                     B["ptmp"][:].rearrange("p (h q) -> p h q", h=64),
                     B["lbuf"][bp_][:].rearrange("p (h q) -> p h q", h=64),
                     B["fbb"][bp_][:, 64 * hh_:64 * hh_ + 64].unsqueeze(2).broadcast_to((128, 64, 64)),
                     ALU.mult),
                 rd=(f"lbuf{bp}", f"fbb{bp}"), wr=("ptmp",))
            k.op("v", lambda e, hh_=hh: e.tensor_tensor(B["prev"][hh_][:], B["prev"][hh_][:], B["ptmp"][:], ALU.add),
                 rd=("ptmp", f"prev{hh}"), wr=(f"prev{hh}",))
    k.op("s", lambda e: [e.dma_start(out=B["cntg"][0:8, :], in_=bass.AP(d["ccg"], 1048704, [[2, 8], [1048720, 8], [1, 2]]))],
         wr=("cntg",), dma=True)
    k.op("v", lambda e: e.reduce_sum(B["cnt2"][0:8, 0:2],
                                     bass.AP(B["cntg"], 0, [[16, 8], [1, 2], [2, 8]]), AX.X),
         rd=("cntg",), wr=("cnt2",))
    k.op("v", lambda e: e.tensor_scalar(B["cnt2"][0:8, 2:4], B["cnt2"][0:8, 0:2], CAP + 0.5, 1.0 / 16.0,
                                        ALU.is_le, ALU.mult),
         rd=("cnt2",), wr=("a01",))
    k.op("v", lambda e: e.tensor_scalar(B["wT"][0:8, :], B["w12"][0:8, 0:1024], B["cnt2"][0:8, 2:3], None, ALU.mult),
         rd=("w12", "a01"), wr=("wTt",))
    k.op("v", lambda e: e.scalar_tensor_tensor(B["wT"][0:8, :], B["w12"][0:8, 1024:2048], B["cnt2"][0:8, 3:4],
                                               B["wT"][0:8, :], ALU.mult, ALU.add),
         rd=("w12", "a01", "wTt"), wr=("wTt",))


def emit_ssdb(k, B, d):
    nc = k.nc
    PSA, PSB, PSC, PSD = B["PS_A"], B["PS_B"], B["PS_C"], B["PS_D"]
    onesr = B["ones_row"]
    ecsrow, cbufB = B["ecsrow"], B["cbufB"]
    stbuf = B["stbuf"]
    prev, prevb = B["prev"], B["prevb"]
    yab, gateb, ygb = B["yab"], B["gateb"], B["ygb"]
    sq, ynst = B["sqb"], B["ynst"]
    csc = B["csc"]
    nw = B["nwcols"]
    PCD = [PSC, PSD]
    for hh in range(2):
        for kk in range(NCH):
            par = kk % 2
            k.op("s", lambda e, hh_=hh, kk_=kk, p_=par: [e.dma_start(
                     out=stbuf[p_][:, :], in_=d["std"].ap()[kk_, :, 4096 * hh_:4096 * (hh_ + 1)])],
                 wr=(f"stbuf{par}",), dma=True)
            def ldc(e, hh_=hh, kk_=kk, p_=par):
                return [e.dma_start(out=cbufB[p_][:, 128 * g:128 * (g + 1)],
                                    in_=d["cmat"].ap()[128 * (4 * hh_ + g):128 * (4 * hh_ + g + 1),
                                                       128 * kk_:128 * (kk_ + 1)]) for g in range(4)]
            k.op("s", ldc, wr=(f"cbufB{par}",), dma=True, n_dma=4)
            k.op("s", lambda e, hh_=hh, kk_=kk: [e.dma_start(
                     out=ecsrow[0:1, :], in_=bass.AP(d["ecsf"], 64 * hh_ * 1024 + 128 * kk_, [[1024, 64], [1, 128]]))],
                 wr=("ecsrow",), dma=True)
            def ldya(e, hh_=hh, kk_=kk, p_=par):
                return [e.dma_start(out=yab[p_][:, 128 * jp:128 * (jp + 1)],
                                    in_=d["ya"].ap()[128 * (32 * hh_ + jp):128 * (32 * hh_ + jp + 1),
                                                     128 * kk_:128 * (kk_ + 1)]) for jp in range(32)]
            k.op("s", ldya, wr=("yab",), dma=True, n_dma=32)
            def ldg(e, hh_=hh, kk_=kk, p_=par):
                return [e.dma_start(out=gateb[p_][:, 128 * jp:128 * (jp + 1)],
                                    in_=d["gate"].ap()[128 * (32 * hh_ + jp):128 * (32 * hh_ + jp + 1),
                                                       128 * kk_:128 * (kk_ + 1)]) for jp in range(32)]
            k.op("s", ldg, wr=("gateb",), dma=True, n_dma=32)
            k.op("v", lambda e, hh_=hh: e.tensor_copy(prevb[:, :], prev[hh_][:, :]),
                 rd=(f"prev{hh}",), wr=("prevb",))
            for h in range(64):
                hp = h % 2
                jp = h // 2
                jpp = jp % 2
                k.op("t", lambda e, h_=h, hp_=hp: e.matmul(PSA[:, 512 * hp_:512 * hp_ + 128], onesr[:, :],
                                                           ecsrow[0:1, 128 * h_:128 * (h_ + 1)], start=True, stop=True),
                     rd=("ecsrow", "onesr"), wr=(f"psA{hp}",))
                gw = (h // 16)
                k.op("v", lambda e, h_=h, hp_=hp, gw_=gw, p_=par: e.tensor_tensor(
                         csc[hp_][:, :], cbufB[p_][:, 128 * gw_:128 * (gw_ + 1)],
                         PSA[:, 512 * hp_:512 * hp_ + 128], ALU.mult),
                     rd=(f"psA{hp}", f"cbufB{par}"), wr=(f"csc{hp}",))
                def yoff(e, h_=h, hp_=hp):
                    return e.matmul(PSB[64 * (h_ % 2):64 * (h_ % 2) + 64, 512 * ((h_ // 2) % 2):512 * ((h_ // 2) % 2) + 128],
                                    prevb[:, 64 * h_:64 * (h_ + 1)], csc[hp_][:, :], start=True, stop=True)
                k.op("t", yoff, rd=("prevb", f"csc{hp}"), wr=(f"psB{jpp}",))
                if h % 2 == 1:
                    k.op("v", lambda e, jp_=jp, jpp_=jpp, p_=par: e.tensor_tensor(
                             ygb[:, 128 * jp_:128 * (jp_ + 1)], yab[p_][:, 128 * jp_:128 * (jp_ + 1)],
                             PSB[:, 512 * jpp_:512 * jpp_ + 128], ALU.add),
                         rd=(f"psB{jpp}", "yab"), wr=("ygb",))
                    k.op("sc", lambda e, jp_=jp, p_=par: e.activation(
                             sq[:, 128 * (jp_ % 2):128 * (jp_ % 2) + 128],
                             gateb[p_][:, 128 * jp_:128 * (jp_ + 1)], AF.Silu),
                         rd=("gateb",), wr=(f"sgate{jp % 2}",))
                    k.op("v", lambda e, jp_=jp: e.tensor_tensor(
                             ygb[:, 128 * jp_:128 * (jp_ + 1)], ygb[:, 128 * jp_:128 * (jp_ + 1)],
                             sq[:, 128 * (jp_ % 2):128 * (jp_ % 2) + 128], ALU.mult),
                         rd=("ygb", f"sgate{jp % 2}"), wr=("ygb",))
            def scan(e, hh_=hh, kk_=kk, p_=par):
                for h in range(64):
                    hg = 64 * hh_ + h
                    vi = e.scalar_tensor_tensor(prev[hh_][:, 64 * h:64 * (h + 1)], prev[hh_][:, 64 * h:64 * (h + 1)],
                                                B["cdrepB"][:, 8 * hg + kk_:8 * hg + kk_ + 1],
                                                stbuf[p_][:, 64 * h:64 * (h + 1)], ALU.mult, ALU.add)
                return vi
            k.op("v", scan, rd=(f"prev{hh}", f"stbuf{par}", "cdrepB", "prevb"), wr=(f"prev{hh}",))
            for g in range(4):
                for q in range(8):
                    jp = 8 * g + q
                    k.op("sc", lambda e, jp_=jp, q_=q: e.activation(sq[:, 256 + 128 * (q_ % 2):256 + 128 * (q_ % 2) + 128],
                                                                   ygb[:, 128 * jp_:128 * (jp_ + 1)], AF.Square),
                         rd=("ygb",), wr=(f"sqq{q % 2}",))
                    k.op("t", lambda e, q_=q, g_=g: e.matmul(PCD[g_ % 2][0:1, 0:128], B["ones_col"][:, :],
                                                             sq[:, 256 + 128 * (q_ % 2):256 + 128 * (q_ % 2) + 128],
                                                             start=(q_ == 0), stop=(q_ == 7)),
                         rd=(f"sqq{q % 2}", "onesc"), wr=(f"ps{'C' if g % 2 == 0 else 'D'}",))
                k.op("sc", lambda e, g_=g: e.activation(B["rstdrow"][0:1, :], PCD[g_ % 2][0:1, 0:128],
                                                        AF.Ln, scale=1.0 / 1024.0, bias=B["biasv"][0:1, 0:1]),
                     rd=(f"ps{'C' if g % 2 == 0 else 'D'}", "biasv"), wr=("rstdrow",))
                k.op("sc", lambda e: e.activation(B["rstdrow"][0:1, :], B["rstdrow"][0:1, :], AF.Exp, scale=-0.5),
                     rd=("rstdrow",), wr=("rstdrow",))
                k.op("t", lambda e, g_=g: e.matmul(PCD[(g_ + 1) % 2][:, 0:128], onesr[:, :], B["rstdrow"][0:1, :],
                                                   start=True, stop=True),
                     rd=("rstdrow", "onesr"), wr=(f"ps{'C' if (g + 1) % 2 == 0 else 'D'}",))
                for q in range(8):
                    jp = 8 * g + q
                    hg2 = 32 * hh + jp
                    k.op("v", lambda e, jp_=jp, g_=g: e.tensor_tensor(
                             ynst[:, 128 * (jp_ % 2):128 * (jp_ % 2) + 128], ygb[:, 128 * jp_:128 * (jp_ + 1)],
                             PCD[(g_ + 1) % 2][:, 0:128], ALU.mult),
                         rd=("ygb", f"ps{'C' if (g + 1) % 2 == 0 else 'D'}"), wr=(f"ynst{jp % 2}",))
                    k.op("v", lambda e, jp_=jp, hg2_=hg2: e.tensor_scalar(
                             B["ynstb"][:, 128 * (jp_ % 2):128 * (jp_ % 2) + 128],
                             ynst[:, 128 * (jp_ % 2):128 * (jp_ % 2) + 128],
                             nw[:, hg2_:hg2_ + 1], None, ALU.mult),
                         rd=(f"ynst{jp % 2}", "nwcols"), wr=(f"ynstb{jp % 2}",))
                    k.op("s", lambda e, jp_=jp, kk_=kk, hg2_=hg2: [e.dma_start(
                             out=d["yn"].ap()[128 * hg2_:128 * (hg2_ + 1), 128 * kk_:128 * (kk_ + 1)],
                             in_=B["ynstb"][:, 128 * (jp_ % 2):128 * (jp_ % 2) + 128])],
                         rd=(f"ynstb{jp % 2}",), dma=True, tag=f"ynstb{jp % 2}")


def emit_outproj(k, B, d):
    nc = k.nc
    PSA, PSB = B["PS_A"], B["PS_B"]
    ynb, wbuf = B["ynb"], B["wbuf"]
    for tt in range(2):
        pt = tt % 2
        def ldyn(e, tt_=tt):
            return [e.dma_start(out=ynb[:, 512 * j:512 * (j + 1)],
                                in_=d["yn"].ap()[128 * j:128 * (j + 1), 512 * tt_:512 * (tt_ + 1)])
                    for j in range(64)]
        k.op("s", ldyn, wr=("ynb",), dma=True, n_dma=64)
        for o in range(NDC):
            par = o % 2
            k.op("g", lambda e, o_=o, p_=par: [e.dma_start(out=wbuf[p_][:, :], in_=d["wopt"].ap()[:, o_, :])],
                 wr=(f"wb{par}",), dma=True)
            def mm(e, o_=o, p_=par, tt_=tt, pt_=pt):
                for j in range(64):
                    ti = e.matmul((PSA if p_ == 0 else PSB)[:, 512 * tt_:512 * (tt_ + 1)],
                                  wbuf[p_][:, 128 * j:128 * (j + 1)], ynb[:, 512 * j:512 * (j + 1)],
                                  start=(j == 0), stop=(j == 63))
                return ti
            k.op("t", mm, rd=(f"wb{par}", "ynb"), wr=(f"psA{tt}",) if par == 0 else (f"psB{tt}",))
            k.op("sc", lambda e, o_=o, p_=par, tt_=tt: e.activation(
                     B["sgst"][p_][:, :], (PSA if p_ == 0 else PSB)[:, 512 * tt_:512 * (tt_ + 1)], AF.Copy),
                 rd=(f"psA{tt}",) if par == 0 else (f"psB{tt}",), wr=(f"sgst{par}",))
            k.op("s", lambda e, o_=o, p_=par, tt_=tt: [e.dma_start(
                     out=d["sgd"].ap()[128 * o_:128 * (o_ + 1), 512 * tt_:512 * (tt_ + 1)],
                     in_=B["sgst"][p_][:, :])],
                 rd=(f"sgst{par}",), dma=True)


def emit_ffn0_late(k, B, d):
    nc = k.nc
    PSA, PSB, PSC, PSD = B["PS_A"], B["PS_B"], B["PS_C"], B["PS_D"]
    e8, wT = B["e8sb"], B["wT"]
    tmpw, bstk = B["tmpw"], B["bstk"]
    mainb = B["mainb"]
    PCD = [PSC, PSD]
    for tt in range(2):
        k.op("t", lambda e, tt_=tt: e.matmul(PCD[tt_][:, 0:512], e8[0:8, :], wT[0:8, 512 * tt_:512 * (tt_ + 1)],
                                             start=True, stop=True),
             rd=("e8c", "wTt"), wr=("psC" if tt == 0 else "psD",))
        k.op("v", lambda e, tt_=tt: e.tensor_tensor(tmpw[:, 512 * tt_:512 * (tt_ + 1)],
                                                    B["tmpsb"][:, 512 * tt_:512 * (tt_ + 1)],
                                                    PCD[tt_][:, 0:512], ALU.mult),
             rd=(("psC" if tt == 0 else "psD"), "tmpsb"), wr=("tmpw",))
    k.op("v", lambda e: e.tensor_copy(B["tmpwb"][:, :], tmpw[:, :]), rd=("tmpw",), wr=("tmpwb",))
    for o in range(NDC):
        par = o % 2
        PS = PSA if par == 0 else PSB
        k.op("s", lambda e, o_=o, p_=par: [e.dma_start(out=mainb[p_][:, :],
                                                       in_=d["ffmain"].ap()[128 * o_:128 * (o_ + 1), :])],
             wr=(f"mainb{par}",), dma=True)
        def lmm(e, o_=o, p_=par):
            e.matmul((PSA if p_ == 0 else PSB)[:, 0:512], bstk[:, 128 * o_:128 * (o_ + 1)],
                     B["tmpwb"][:, 0:512], start=True, stop=True)
            return e.matmul((PSA if p_ == 0 else PSB)[:, 512:1024], bstk[:, 128 * o_:128 * (o_ + 1)],
                            B["tmpwb"][:, 512:1024], start=True, stop=True)
        k.op("t", lmm, rd=("bstk", "tmpwb"), wr=("psA0", "psA1") if par == 0 else ("psB0", "psB1"))
        k.op("v", lambda e, p_=par: e.tensor_tensor(B["g0st"][p_][:, :], mainb[p_][:, :],
                                                    (PSA if p_ == 0 else PSB)[:, :], ALU.add),
             rd=(("psA0", "psA1", f"mainb{par}") if par == 0 else ("psB0", "psB1", f"mainb{par}")),
             wr=(f"g0st{par}",))
        k.op("sc", lambda e, p_=par: e.activation(B["g0st"][p_][:, :], B["g0st"][p_][:, :], AF.Silu),
             rd=(f"g0st{par}",), wr=(f"g0st{par}",))
        k.op("s", lambda e, o_=o, p_=par: [e.dma_start(out=B["s0b"][p_][:, :],
                                                       in_=d["sgd"].ap()[128 * o_:128 * (o_ + 1), :])],
             wr=(f"s0b{par}",), dma=True)
        k.op("v", lambda e, p_=par: e.tensor_tensor(B["g0st"][p_][:, :], B["g0st"][p_][:, :],
                                                    B["s0b"][p_][:, :], ALU.mult),
             rd=(f"g0st{par}", f"s0b{par}"), wr=(f"g0st{par}",))
        k.op("s", lambda e, o_=o, p_=par: [e.dma_start(out=d["sgd"].ap()[128 * o_:128 * (o_ + 1), :],
                                                       in_=B["g0st"][p_][:, :])],
             rd=(f"g0st{par}",), dma=True, tag=f"g0out{par}")


def emit_combine(k, B, d, src_res, dst, sgsrc):
    nc = k.nc
    PSC, PSD = B["PS_C"], B["PS_D"]
    PCD = [PSC, PSD]
    xres, x1st = B["xres"], B["x1st"]
    sgt = B["sgtile"]
    for i in range(NTC):
        par = i % 2
        k.op("s", lambda e, i_=i, p_=par: [e.dma_start(out=xres[p_][:, :], in_=src_res.ap()[128 * i_:128 * (i_ + 1), :])],
             wr=(f"xres{par}",), dma=True)
        for o in range(NDC):
            op_ = o % 2
            k.op("s", lambda e, i_=i, o_=o, op2=op_: [e.dma_start(
                     out=sgt[op2][:, :], in_=sgsrc.ap()[128 * o_:128 * (o_ + 1), 128 * i_:128 * (i_ + 1)])],
                 wr=(f"sgtile{op_}",), dma=True)
            k.op("t", lambda e, op2=op_: e.transpose(PCD[op2][:, 0:128], sgt[op2][:, :], B["ident_f32"][:, :]),
                 rd=(f"sgtile{op_}", "identf"), wr=("psC" if op_ == 0 else "psD",))
            k.op("sc", lambda e, o_=o, op2=op_: e.activation(x1st[:, 128 * o_:128 * (o_ + 1)], PCD[op2][:, 0:128], AF.Copy),
                 rd=("psC" if op_ == 0 else "psD",), wr=("x1st",))
        k.op("v", lambda e, p_=par: e.tensor_tensor(x1st[:, :], x1st[:, :], xres[p_][:, :], ALU.add),
             rd=(f"xres{par}", "x1st"), wr=("x1st",))
        k.op("s", lambda e, i_=i: [e.dma_start(out=dst.ap()[128 * i_:128 * (i_ + 1), :], in_=x1st[:, :])],
             rd=("x1st",), dma=True, tag="x1out")


def build_launchB(external_scratch=True):
    from contextlib import ExitStack
    nc = bass.Bass("TRN2", target_bir_lowering=False, detect_race_conditions=False)
    kind_s = "ExternalOutput" if external_scratch else "Internal"
    kin = "ExternalInput" if external_scratch else "Internal"
    d = {}
    d["xin"] = nc.dram_tensor("xin", [T, D], F32, kind="ExternalInput")
    d["gate"] = nc.dram_tensor("gate", [INTER, T], F32, kind=kin)
    d["cmat"] = nc.dram_tensor("cmat", [NG * SS, T], BF16, kind=kin)
    d["ya"] = nc.dram_tensor("ya", [INTER, T], F32, kind=kin)
    d["std"] = nc.dram_tensor("std", [NCH, 128, INTER], F32, kind=kin)
    d["ecsf"] = nc.dram_tensor("ecsf", [128, T], F32, kind=kin)
    d["ccg"] = nc.dram_tensor("ccg", [8, 1048720], F32, kind=kin)
    d["tmpd"] = nc.dram_tensor("tmpd", [128, T], F32, kind=kin)
    d["w12d"] = nc.dram_tensor("w12d", [16, T], F32, kind=kin)
    d["cdrepd"] = nc.dram_tensor("cdrepd", [128, T], F32, kind=kin)
    d["ffmain"] = nc.dram_tensor("ffmain", [D, T], F32, kind=kin)
    d["mrow"] = nc.dram_tensor("mrow", [1, 1024], F32, kind="ExternalInput")
    d["invmrow"] = nc.dram_tensor("invmrow", [1, 1024], F32, kind="ExternalInput")
    d["nwc"] = nc.dram_tensor("nwc", [128, 64], F32, kind="ExternalInput")
    d["wopt"] = nc.dram_tensor("wopt", [128, NDC, 64 * 128], F32, kind="ExternalInput")
    d["bstk0"] = nc.dram_tensor("bstk0", [128, D], F32, kind="ExternalInput")
    d["yn"] = nc.dram_tensor("yn", [INTER, T], BF16, kind=kind_s)
    d["sgd"] = nc.dram_tensor("sgd", [D, T], F32, kind=kind_s)
    d["x1"] = nc.dram_tensor("x1", [T, D], F32, kind="ExternalOutput")

    c = mkconsts(nc)
    k = K(nc)
    B = {}
    B["PS_A"] = nc.alloc_psum_tensor("PS_A", [128, 1024], F32)
    B["PS_B"] = nc.alloc_psum_tensor("PS_B", [128, 1024], F32)
    B["PS_C"] = nc.alloc_psum_tensor("PS_C", [128, 512], F32)
    B["PS_D"] = nc.alloc_psum_tensor("PS_D", [128, 512], F32)
    B["PST"] = nc.alloc_psum_tensor("PST", [128, 2048], BF16)
    per = [("ident_f32", c["ident_f32"], [128, 128], F32, "identf"),
           ("ones_row", c["ones_row"], [1, 128], F32, "onesr"),
           ("ones_col", c["ones_col"], [128, 1], F32, "onesc"),
           ("biasv", c["biasv"], [128, 4], F32, "biasv"),
           ("e8sb", c["e8"], [8, 128], F32, "e8c"),
           ("mrow", d["mrow"], [1, 1024], F32, "mrow"),
           ("invmrow", d["invmrow"], [1, 1024], F32, "mrow2"),
           ("nwcols", d["nwc"], [128, 64], F32, "nwcols")]
    for name, srcd, shape, dt_, key in per:
        t = nc.alloc_sbuf_tensor("sb_" + name, shape, dt_)
        B[name] = t
        k.op("s", lambda e, s_=srcd, t_=t: [e.dma_start(out=t_[:, :], in_=s_.ap())], wr=(key,), dma=True)
    B["cdrepB"] = nc.alloc_sbuf_tensor("cdrepB", [128, 1024], F32)
    k.op("s", lambda e: [e.dma_start(out=B["cdrepB"][:, :], in_=d["cdrepd"].ap())], wr=("cdrepB",), dma=True)
    B["w12"] = nc.alloc_sbuf_tensor("w12", [8, 2048], F32)
    k.op("s", lambda e: [e.dma_start(out=B["w12"][0:8, 0:1024], in_=d["w12d"].ap()[0:8, :]),
                         e.dma_start(out=B["w12"][0:8, 1024:2048], in_=d["w12d"].ap()[8:16, :])],
         wr=("w12",), dma=True, n_dma=2)
    B["tmpsb"] = nc.alloc_sbuf_tensor("tmpsb", [128, 1024], F32)
    k.op("s", lambda e: [e.dma_start(out=B["tmpsb"][:, :], in_=d["tmpd"].ap())], wr=("tmpsb",), dma=True)
    k.flush("constsB")
    k.drain()
    B["wT"] = nc.alloc_sbuf_tensor("wT", [8, 1024], F32)
    B["drow"] = nc.alloc_sbuf_tensor("drow", [1, 1024], F32)
    B["dprow"] = nc.alloc_sbuf_tensor("dprow", [1, 1024], F32)
    B["fall"] = nc.alloc_sbuf_tensor("fall", [1, 1024], F32)
    B["sfxrow"] = nc.alloc_sbuf_tensor("sfxrow", [1, 128], F32)
    B["cntg"] = nc.alloc_sbuf_tensor("cntg", [8, 16], F32)
    B["cnt2"] = nc.alloc_sbuf_tensor("cnt2", [8, 4], F32)
    B["prev"] = [nc.alloc_sbuf_tensor(f"prev{i}", [128, 4096], F32) for i in range(2)]

    from contextlib import ExitStack
    st1 = ExitStack()
    B["fbb"] = [st1.enter_context(nc.sbuf_tensor(f"fbb{i}", [128, 128], F32)) for i in range(2)]
    B["lbuf"] = [st1.enter_context(nc.sbuf_tensor(f"lbufs{i}", [128, 4096], F32)) for i in range(2)]
    B["ptmp"] = st1.enter_context(nc.sbuf_tensor("ptmp", [128, 4096], F32))
    emit_prefix(k, B, d)
    k.flush("prefix")
    k.drain()
    st1.close()

    st2 = ExitStack()
    B["stbuf"] = [st2.enter_context(nc.sbuf_tensor(f"stbuf{i}", [128, 4096], F32)) for i in range(2)]
    B["prevb"] = st2.enter_context(nc.sbuf_tensor("prevbb", [128, 4096], BF16))
    B["cbufB"] = [st2.enter_context(nc.sbuf_tensor(f"cbufB{i}", [128, 512], BF16)) for i in range(2)]
    B["ecsrow"] = st2.enter_context(nc.sbuf_tensor("ecsrow", [1, 8192], F32))
    B["yab"] = [st2.enter_context(nc.sbuf_tensor("yab0", [128, 4096], F32))] * 2
    B["gateb"] = [st2.enter_context(nc.sbuf_tensor("gatebb0", [128, 4096], F32))] * 2
    B["ygb"] = st2.enter_context(nc.sbuf_tensor("ygb", [128, 4096], F32))
    B["sqb"] = st2.enter_context(nc.sbuf_tensor("sqb", [128, 512], F32))
    B["ynst"] = st2.enter_context(nc.sbuf_tensor("ynst", [128, 256], F32))
    B["ynstb"] = st2.enter_context(nc.sbuf_tensor("ynstb", [128, 256], BF16))
    B["csc"] = [st2.enter_context(nc.sbuf_tensor(f"cscb{i}", [128, 128], BF16)) for i in range(2)]
    B["rstdrow"] = st2.enter_context(nc.sbuf_tensor("rstdrow", [1, 128], F32))
    emit_ssdb(k, B, d)
    k.flush("ssdb")
    k.drain()
    st2.close()

    st3 = ExitStack()
    B["ynb"] = st3.enter_context(nc.sbuf_tensor("ynb", [128, 64 * 512], BF16))
    B["sgst"] = [st3.enter_context(nc.sbuf_tensor(f"sgst{i}", [128, 512], F32)) for i in range(2)]
    B["wbuf"] = [st3.enter_context(nc.sbuf_tensor(f"wbufB{i}", [128, 64 * 128], BF16)) for i in range(2)]
    emit_outproj(k, B, d)
    k.flush("outproj")
    k.drain()
    st3.close()

    st4 = ExitStack()
    B["tmpw"] = st4.enter_context(nc.sbuf_tensor("tmpw", [128, 1024], F32))
    B["tmpwb"] = st4.enter_context(nc.sbuf_tensor("tmpwb", [128, 1024], BF16))
    B["bstk"] = st4.enter_context(nc.sbuf_tensor("bstk", [128, D], BF16))
    k.op("g", lambda e: [e.dma_start(out=B["bstk"][:, :], in_=d["bstk0"].ap())], wr=("bstk",), dma=True)
    B["mainb"] = [st4.enter_context(nc.sbuf_tensor(f"mainb{i}", [128, 1024], F32)) for i in range(2)]
    B["s0b"] = [st4.enter_context(nc.sbuf_tensor(f"s0b{i}", [128, 1024], F32)) for i in range(2)]
    B["g0st"] = [st4.enter_context(nc.sbuf_tensor(f"g0st{i}", [128, 1024], F32)) for i in range(2)]
    emit_ffn0_late(k, B, d)
    k.flush("ffn0l")
    k.drain()
    st4.close()

    st5 = ExitStack()
    B["xres"] = [st5.enter_context(nc.sbuf_tensor(f"xres{i}", [128, D], F32)) for i in range(2)]
    B["sgtile"] = [st5.enter_context(nc.sbuf_tensor(f"sgtile{i}", [128, 128], F32)) for i in range(2)]
    B["x1st"] = st5.enter_context(nc.sbuf_tensor("x1st", [128, D], F32))
    emit_combine(k, B, d, d["xin"], d["x1"], d["sgd"])
    k.flush("comb0")
    k.drain()
    st5.close()
    return nc, k, d


def prep_launchB_inputs(ins, core, la_outs):
    x = np.asarray(ins["L0_attn_hidden_states"]).reshape(SEQ, D)
    o = la_outs[core]
    ccg = np.stack([np.asarray(la_outs[b]["ccin"]).reshape(-1) for b in range(8)])
    mrow = np.zeros((1, 1024), np.float32)
    for b in range(core):
        mrow[0, 128 * b:128 * (b + 1)] = 1.0
    m = {
        "xin": np.ascontiguousarray(x[T * core:T * (core + 1)]),
        "gate": np.asarray(o["gate"]), "cmat": np.asarray(o["cmat"]),
        "ya": np.asarray(o["ya"]), "std": np.asarray(o["std"]),
        "ecsf": np.asarray(o["ecsf"]), "ccg": ccg,
        "tmpd": np.asarray(o["tmpd"]), "w12d": np.asarray(o["w12d"]),
        "cdrepd": np.asarray(o["cdrepd"]), "ffmain": np.asarray(o["ffmain"]),
        "mrow": mrow, "invmrow": 1.0 - mrow,
        "nwc": col_layout(ins["L0_attn_norm_w"]),
        "wopt": tile_w(np.asarray(ins["L0_attn_out_proj_w"])),
        "bstk0": np.ascontiguousarray(np.asarray(ins["L0_ffn_lora_B"]).reshape(128, D)),
    }
    return m



def emit_qkv(k, B, d):
    nc = k.nc
    xn_fm = B["xn_fm"]
    wbuf, qst = B["wbuf"], B["qst"]
    PSA, PSB = B["PS_A"], B["PS_B"]

    def fmslc(j, start, n):
        return xn_fm[:, j * T + start: j * T + start + n]

    for o in range(QKVD // 128):
        par = o % 2
        PS = PSA if par == 0 else PSB
        k.op("g", lambda e, o_=o, p_=par: [e.dma_start(out=wbuf[p_][:, :], in_=d["wqkvt"].ap()[:, o_, :])],
             wr=(f"wb{par}",), dma=True)
        def mmg(e, o_=o, p_=par):
            for j in range(NDC):
                w = wbuf[p_][:, 128 * j:128 * (j + 1)]
                st, sp = (j == 0), (j == NDC - 1)
                (PSA if p_ == 0 else PSB)[:, 0:512]
                e.matmul((PSA if p_ == 0 else PSB)[:, 0:512], w, fmslc(j, 0, 512), start=st, stop=sp)
                ti = e.matmul((PSA if p_ == 0 else PSB)[:, 512:1024], w, fmslc(j, 512, 512), start=st, stop=sp)
            return ti
        k.op("t", mmg, rd=(f"wb{par}", "ln1xn"), wr=("psA0", "psA1") if par == 0 else ("psB0", "psB1"))
        k.op("sc", lambda e, o_=o, p_=par: e.activation(qst[p_][:, :], (PSA if p_ == 0 else PSB)[:, :],
                                                        AF.Identity, bias=B["bqc"][:, o_:o_ + 1]),
             rd=(("psA0", "psA1", "bqc") if par == 0 else ("psB0", "psB1", "bqc")), wr=(f"qst{par}",))
        k.op("s", lambda e, o_=o, p_=par: [e.dma_start(out=d["qkvc"].ap()[128 * o_:128 * (o_ + 1), :],
                                                       in_=qst[p_][:, :])],
             rd=(f"qst{par}",), dma=True)


def emit_router_ffn1(k, B, d):
    nc = k.nc
    xn_fm = B["xn_fm"]
    PSA, PSB, PSC, PSD = B["PS_A"], B["PS_B"], B["PS_C"], B["PS_D"]
    PCD = [PSC, PSD]
    gfm, gtm, st8 = B["gfm"], B["gtm"], B["st8"]
    m1w, m2w = B["m1w"], B["m2w"]
    wT = B["wT"]
    identf = B["ident_f32"]

    def fmslc(j, start, n):
        return xn_fm[:, j * T + start: j * T + start + n]

    for tt in range(2):
        def rmm(e, tt_=tt):
            for j in range(NDC):
                ti = e.matmul(PCD[tt_][0:8, 0:512], B["wrt"][:, 8 * j:8 * (j + 1)],
                              fmslc(j, 512 * tt_, 512), start=(j == 0), stop=(j == NDC - 1))
            return ti
        k.op("t", rmm, rd=("wrt", "ln1xn"), wr=("psC" if tt == 0 else "psD",))
        k.op("sc", lambda e, tt_=tt: e.activation(gfm[0:8, 512 * tt_:512 * (tt_ + 1)], PCD[tt_][0:8, 0:512], AF.Copy),
             rd=("psC" if tt == 0 else "psD",), wr=("gfm",))
    for i in range(NTC):
        k.op("t", lambda e, i_=i: e.transpose(PSA[:, 512 * (i_ % 2):512 * (i_ % 2) + 8],
                                              gfm[0:8, 128 * i_:128 * (i_ + 1)], identf[0:8, 0:8]),
             rd=("gfm", "identf"), wr=(f"psA{i % 2}",))
        k.op("sc", lambda e, i_=i: e.activation(gtm[:, 0:8], PSA[:, 512 * (i_ % 2):512 * (i_ % 2) + 8], AF.Copy),
             rd=(f"psA{i % 2}",), wr=("gtm",))
        k.op("v", lambda e: e.reduce_max(st8[:, 8:9], gtm[:, 0:8], AX.X), rd=("gtm",), wr=("m1s",))
        k.op("v", lambda e: e.tensor_single_scalar(m1w[:, 0:8], gtm[:, 0:8], st8[:, 8:9], ALU.is_equal),
             rd=("gtm", "m1s"), wr=("m1w",))
        k.op("v", lambda e: e.scalar_tensor_tensor(gtm[:, 8:16], m1w[:, 0:8], -1e30, gtm[:, 0:8], ALU.mult, ALU.add),
             rd=("gtm", "m1w"), wr=("gtm2",))
        k.op("v", lambda e: e.reduce_max(st8[:, 9:10], gtm[:, 8:16], AX.X), rd=("gtm2",), wr=("m2s",))
        k.op("v", lambda e: e.tensor_single_scalar(m2w[:, 0:8], gtm[:, 8:16], st8[:, 9:10], ALU.is_equal),
             rd=("gtm2", "m2s"), wr=("m2w",))
        k.op("v", lambda e: e.tensor_tensor(st8[:, 10:11], st8[:, 9:10], st8[:, 8:9], ALU.subtract),
             rd=("m1s", "m2s"), wr=("dms",))
        k.op("sc", lambda e: e.activation(st8[:, 11:12], st8[:, 10:11], AF.Exp), rd=("dms",), wr=("expd",))
        k.op("v", lambda e: e.tensor_scalar(st8[:, 12:13], st8[:, 11:12], 1.0, None, ALU.add),
             rd=("expd",), wr=("sgp",))
        k.op("v", lambda e: e.reciprocal(st8[:, 13:14], st8[:, 12:13]), rd=("sgp",), wr=("sg1",))
        k.op("v", lambda e: e.tensor_scalar(st8[:, 14:15], st8[:, 13:14], 1.0, -1.0, ALU.subtract, ALU.mult),
             rd=("sg1",), wr=("sgs",))
        k.op("v", lambda e: e.tensor_scalar(m1w[:, 8:16], m1w[:, 0:8], st8[:, 13:14], None, ALU.mult),
             rd=("m1w", "sg1"), wr=("w1tm",))
        k.op("v", lambda e: e.scalar_tensor_tensor(m1w[:, 8:16], m2w[:, 0:8], st8[:, 14:15], m1w[:, 8:16],
                                                   ALU.mult, ALU.add),
             rd=("m2w", "sgs", "w1tm"), wr=("w1tm",))
        k.op("v", lambda e: e.tensor_scalar(m1w[:, 8:16], m1w[:, 8:16], 2.0, None, ALU.mult),
             rd=("w1tm",), wr=("w1tm",))
        k.op("t", lambda e, i_=i: e.transpose(PSB[0:8, 512 * (i_ % 2):512 * (i_ % 2) + 128],
                                              m1w[:, 8:16], identf[:, :]),
             rd=("w1tm", "identf"), wr=(f"psB{i % 2}",))
        k.op("sc", lambda e, i_=i: e.activation(wT[0:8, 128 * i_:128 * (i_ + 1)],
                                                PSB[0:8, 512 * (i_ % 2):512 * (i_ % 2) + 128], AF.Copy),
             rd=(f"psB{i % 2}",), wr=("wTt",))
    k.op("g", lambda e: [e.dma_start(out=B["wbuf"][0][:, :], in_=d["a1t"].ap()[:, 0, :])], wr=("wb0",), dma=True)
    for tt in range(2):
        def tmm(e, tt_=tt):
            for j in range(NDC):
                ti = e.matmul(PSA[:, 512 * tt_:512 * (tt_ + 1)], B["wbuf"][0][:, 128 * j:128 * (j + 1)],
                              fmslc(j, 512 * tt_, 512), start=(j == 0), stop=(j == NDC - 1))
            return ti
        k.op("t", tmm, rd=("wb0", "ln1xn"), wr=(f"psA{tt}",))
        k.op("sc", lambda e, tt_=tt: e.activation(B["tmpsb"][:, 512 * tt_:512 * (tt_ + 1)],
                                                  PSA[:, 512 * tt_:512 * (tt_ + 1)], AF.Copy),
             rd=(f"psA{tt}",), wr=("tmpsb",))
    for tt in range(2):
        k.op("t", lambda e, tt_=tt: e.matmul(PCD[tt_][:, 0:512], B["e8sb"][0:8, :], wT[0:8, 512 * tt_:512 * (tt_ + 1)],
                                             start=True, stop=True),
             rd=("e8c", "wTt"), wr=("psC" if tt == 0 else "psD",))
        k.op("v", lambda e, tt_=tt: e.tensor_tensor(B["tmpw"][:, 512 * tt_:512 * (tt_ + 1)],
                                                    B["tmpsb"][:, 512 * tt_:512 * (tt_ + 1)],
                                                    PCD[tt_][:, 0:512], ALU.mult),
             rd=(("psC" if tt == 0 else "psD"), "tmpsb"), wr=("tmpw",))
    k.op("v", lambda e: e.tensor_copy(B["tmpwb"][:, :], B["tmpw"][:, :]), rd=("tmpw",), wr=("tmpwb",))
    k.op("g", lambda e: [e.dma_start(out=B["bstk"][:, :], in_=d["bstk1"].ap())], wr=("bstk",), dma=True)
    for o in range(NDC):
        par = o % 2
        k.op("g", lambda e, o_=o, p_=par: [e.dma_start(out=B["wbuf"][p_][:, :], in_=d["wbaset"].ap()[:, o_, :])],
             wr=(f"wb{par}",), dma=True)
        def mm(e, o_=o, p_=par):
            for j in range(NDC):
                w = B["wbuf"][p_][:, 128 * j:128 * (j + 1)]
                st = (j == 0)
                e.matmul((PSA if p_ == 0 else PSB)[:, 0:512], w, fmslc(j, 0, 512), start=st, stop=False)
                ti = e.matmul((PSA if p_ == 0 else PSB)[:, 512:1024], w, fmslc(j, 512, 512), start=st, stop=False)
            e.matmul((PSA if p_ == 0 else PSB)[:, 0:512], B["bstk"][:, 128 * o_:128 * (o_ + 1)],
                     B["tmpwb"][:, 0:512], start=False, stop=True)
            return e.matmul((PSA if p_ == 0 else PSB)[:, 512:1024], B["bstk"][:, 128 * o_:128 * (o_ + 1)],
                            B["tmpwb"][:, 512:1024], start=False, stop=True)
        k.op("t", mm, rd=(f"wb{par}", "ln1xn", "bstk", "tmpwb"),
             wr=("psA0", "psA1") if par == 0 else ("psB0", "psB1"))
        k.op("sc", lambda e, o_=o, p_=par: e.activation(B["g1st"][p_][:, :], (PSA if p_ == 0 else PSB)[:, :],
                                                        AF.Silu, bias=B["bbc"][:, o_:o_ + 1]),
             rd=(("psA0", "psA1", "bbc") if par == 0 else ("psB0", "psB1", "bbc")), wr=(f"g1st{par}",))
        k.op("s", lambda e, o_=o, p_=par: [e.dma_start(out=d["g1d"].ap()[128 * o_:128 * (o_ + 1), :],
                                                       in_=B["g1st"][p_][:, :])],
             rd=(f"g1st{par}",), dma=True)


def build_launchC(external_scratch=True):
    from contextlib import ExitStack
    nc = bass.Bass("TRN2", target_bir_lowering=False, detect_race_conditions=False)
    kind_s = "ExternalOutput" if external_scratch else "Internal"
    kin = "ExternalInput" if external_scratch else "Internal"
    d = {}
    d["x1"] = nc.dram_tensor("x1", [T, D], F32, kind=kin)
    d["wqkvt"] = nc.dram_tensor("wqkvt", [128, QKVD // 128, NDC * 128], F32, kind="ExternalInput")
    d["bqc"] = nc.dram_tensor("bqc", [128, QKVD // 128], F32, kind="ExternalInput")
    d["wrt"] = nc.dram_tensor("wrt", [128, 256], F32, kind="ExternalInput")
    d["a1t"] = nc.dram_tensor("a1t", [128, 1, NDC * 128], F32, kind="ExternalInput")
    d["bstk1"] = nc.dram_tensor("bstk1", [128, D], F32, kind="ExternalInput")
    d["wbaset"] = nc.dram_tensor("wbaset", [128, NDC, NDC * 128], F32, kind="ExternalInput")
    d["bbc"] = nc.dram_tensor("bbc", [128, NDC], F32, kind="ExternalInput")
    d["qkvc"] = nc.dram_tensor("qkvc", [QKVD, T], BF16, kind="ExternalOutput")
    d["g1d"] = nc.dram_tensor("g1d", [D, T], F32, kind=kind_s)

    c = mkconsts(nc)
    k = K(nc)
    B = {}
    B["PS_A"] = nc.alloc_psum_tensor("PS_A", [128, 1024], F32)
    B["PS_B"] = nc.alloc_psum_tensor("PS_B", [128, 1024], F32)
    B["PS_C"] = nc.alloc_psum_tensor("PS_C", [128, 512], F32)
    B["PS_D"] = nc.alloc_psum_tensor("PS_D", [128, 512], F32)
    B["PST"] = nc.alloc_psum_tensor("PST", [128, 2048], BF16)
    per = [("ident_bf", c["ident_bf"], [128, 128], BF16, "identb"),
           ("ident_f32", c["ident_f32"], [128, 128], F32, "identf"),
           ("ones_row", c["ones_row"], [1, 128], F32, "onesr"),
           ("ones_col", c["ones_col"], [128, 1], F32, "onesc"),
           ("biasv", c["biasv"], [128, 4], F32, "biasv"),
           ("e8sb", c["e8"], [8, 128], F32, "e8c"),
           ("bqc", d["bqc"], [128, QKVD // 128], F32, "bqc"),
           ("bbc", d["bbc"], [128, NDC], F32, "bbc")]
    for name, srcd, shape, dt_, key in per:
        t = nc.alloc_sbuf_tensor("sb_" + name, shape, dt_)
        B[name] = t
        k.op("s", lambda e, s_=srcd, t_=t: [e.dma_start(out=t_[:, :], in_=s_.ap())], wr=(key,), dma=True)
    B["wrt"] = nc.alloc_sbuf_tensor("sb_wrt", [128, 256], BF16)
    k.op("g", lambda e: [e.dma_start(out=B["wrt"][:, :], in_=d["wrt"].ap())], wr=("wrt",), dma=True)
    k.flush("constsC")
    k.drain()

    stack = ExitStack()
    B["xn_fm"] = stack.enter_context(nc.sbuf_tensor("xn_fm", [128, NDC * T], BF16))

    ln_stack = ExitStack()
    B["xtm"] = [ln_stack.enter_context(nc.sbuf_tensor(f"xtm{i}", [128, D], F32)) for i in range(2)]
    B["xsq"] = ln_stack.enter_context(nc.sbuf_tensor("xsq", [128, D], F32))
    B["xbtm"] = ln_stack.enter_context(nc.sbuf_tensor("xbtm", [128, D], BF16))
    B["st1"] = ln_stack.enter_context(nc.sbuf_tensor("st1", [128, 16], F32))
    emit_ln(k, B, d["x1"], None, B["xn_fm"], T, "ln1")
    k.flush("ln1")
    k.drain()
    ln_stack.close()

    q_stack = ExitStack()
    B["wbuf"] = [q_stack.enter_context(nc.sbuf_tensor(f"wbuf{i}", [128, 4096], BF16)) for i in range(2)]
    B["qst"] = [q_stack.enter_context(nc.sbuf_tensor(f"qst{i}", [128, 1024], BF16)) for i in range(2)]
    emit_qkv(k, B, d)
    k.flush("qkv")
    k.drain()
    B["qst"] = None
    B["gfm"] = q_stack.enter_context(nc.sbuf_tensor("gfm", [8, 1024], F32))
    B["gtm"] = q_stack.enter_context(nc.sbuf_tensor("gtm", [128, 16], F32))
    B["st8"] = q_stack.enter_context(nc.sbuf_tensor("st8", [128, 16], F32))
    B["m1w"] = q_stack.enter_context(nc.sbuf_tensor("m1w", [128, 16], F32))
    B["m2w"] = q_stack.enter_context(nc.sbuf_tensor("m2w", [128, 16], F32))
    B["wT"] = q_stack.enter_context(nc.sbuf_tensor("wT", [8, 1024], F32))
    B["tmpsb"] = q_stack.enter_context(nc.sbuf_tensor("tmpsb", [128, 1024], F32))
    B["tmpw"] = q_stack.enter_context(nc.sbuf_tensor("tmpw", [128, 1024], F32))
    B["tmpwb"] = q_stack.enter_context(nc.sbuf_tensor("tmpwb", [128, 1024], BF16))
    B["bstk"] = q_stack.enter_context(nc.sbuf_tensor("bstk", [128, D], BF16))
    B["g1st"] = [q_stack.enter_context(nc.sbuf_tensor(f"g1st{i}", [128, 1024], F32)) for i in range(2)]
    emit_router_ffn1(k, B, d)
    k.flush("rffn1")
    k.drain()
    q_stack.close()
    stack.close()
    return nc, k, d


def permute_qkv_rows():
    deint = list(range(0, 128, 2)) + list(range(1, 128, 2))
    perm = []
    for h in range(NQH + NKV):
        base = h * 128
        perm += [base + j for j in deint]
    base = (NQH + NKV) * 128
    perm += list(range(base, base + NKV * 128))
    return np.array(perm)


def prep_launchC_inputs(ins, core, x1):
    perm = permute_qkv_rows()
    Wq = np.asarray(ins["L1_attn_w_qkv"])[perm]
    bq = np.asarray(ins["L1_attn_b_qkv"])[perm]
    Wr = np.asarray(ins["L1_ffn_W_router"])
    wrt = np.ascontiguousarray(Wr.T.reshape(32, 128, 8).transpose(1, 0, 2).reshape(128, 256))
    A1 = np.asarray(ins["L1_ffn_A"])
    astk1 = np.ascontiguousarray(A1.transpose(2, 0, 1).reshape(D, 128))
    B1 = np.asarray(ins["L1_ffn_B"])
    bstk1 = np.ascontiguousarray(B1.transpose(0, 2, 1).reshape(128, D))
    return {
        "x1": np.ascontiguousarray(x1[T * core:T * (core + 1)]),
        "wqkvt": tile_w(Wq), "bqc": col_layout(bq), "wrt": wrt,
        "a1t": tile_w(astk1.T), "bstk1": bstk1,
        "wbaset": tile_w(np.asarray(ins["L1_ffn_W_base"])),
        "bbc": col_layout(ins["L1_ffn_b_base"]),
    }



NQT = SEQ // 512
NKC = SEQ // 128

def emit_rope(k, B, key_in, key_out, nheads, tag):
    nc = k.nc
    cosb, sinb = B["cosb"], B["sinb"]
    TL = 2048
    ta, tb, tc, td, todd = B["rta"], B["rtb"], B["rtc"], B["rtd"], B["rtodd"]
    for h in range(nheads):
        src = B[key_in][h]
        for q in range(SEQ // TL):
            sl = slice(TL * q, TL * (q + 1))
            k.op("v", lambda e, s_=src, sl_=sl: e.tensor_copy(todd[:, :], s_[64:128, sl_]),
                 rd=(f"{tag}{h}",), wr=("rtodd",))
            k.op("v", lambda e, s_=src, sl_=sl: e.tensor_tensor(ta[:, :], s_[0:64, sl_], cosb[:, sl_], ALU.mult),
                 rd=(f"{tag}{h}", "cosb"), wr=("rta",))
            k.op("v", lambda e, sl_=sl: e.tensor_tensor(tb[:, :], todd[:, :], sinb[:, sl_], ALU.mult),
                 rd=("rtodd", "sinb"), wr=("rtb",))
            k.op("v", lambda e, s_=src, sl_=sl: e.tensor_tensor(tc[:, :], s_[0:64, sl_], sinb[:, sl_], ALU.mult),
                 rd=(f"{tag}{h}", "sinb"), wr=("rtc",))
            k.op("v", lambda e, sl_=sl: e.tensor_tensor(td[:, :], todd[:, :], cosb[:, sl_], ALU.mult),
                 rd=("rtodd", "cosb"), wr=("rtd",))
            k.op("v", lambda e, s_=src, sl_=sl: e.tensor_tensor(s_[0:64, sl_], ta[:, :], tb[:, :], ALU.subtract),
                 rd=("rta", "rtb"), wr=(f"{tag}{h}",))
            k.op("v", lambda e, s_=src, sl_=sl: e.tensor_tensor(s_[64:128, sl_], tc[:, :], td[:, :], ALU.add),
                 rd=("rtc", "rtd"), wr=(f"{tag}{h}",))


def emit_attn(k, B, d):
    nc = k.nc
    PSA, PSB, PSC, PSD = B["PS_A"], B["PS_B"], B["PS_C"], B["PS_D"]
    PST = B["PST"]
    onesr, onesc, identb = B["ones_row"], B["ones_col"], B["ident_bf"]
    kf = B["kf"][0]
    vt = B["vt"]
    denacc = None
    for cc2 in range(NKC):
        qp = cc2 % 2
        k.op("t", lambda e, c_=cc2, qp_=qp: e.transpose(PST[:, 1024 * qp_:1024 * qp_ + 128],
                                                        B["vf"][0][:, 128 * c_:128 * (c_ + 1)], identb[:, :]),
             rd=("vf0", "identb"), wr=(f"pst{qp}",))
        k.op("sc", lambda e, c_=cc2, qp_=qp: e.activation(vt[:, 128 * c_:128 * (c_ + 1)],
                                                          PST[:, 1024 * qp_:1024 * qp_ + 128], AF.Copy),
             rd=(f"pst{qp}",), wr=("vt",))
    sqt, nrow = B["sqt"], B["nrow"]

    def norms(tag, src, rdk):
        for w in range(16):
            qp = w % 2
            k.op("sc", lambda e, s_=src, w_=w, qp_=qp: e.activation(sqt[qp_][:, :], s_[:, 512 * w_:512 * (w_ + 1)],
                                                                    AF.Square),
                 rd=(rdk,), wr=(f"sqt{qp}",))
            k.op("t", lambda e, w_=w, qp_=qp: e.matmul(PSC[0:1, 0:512] if qp_ == 0 else PSD[0:1, 0:512],
                                                       onesc[:, :], sqt[qp_][:, :], start=True, stop=True),
                 rd=(f"sqt{qp}", "onesc"), wr=("psC" if qp == 0 else "psD",))
            k.op("sc", lambda e, w_=w, qp_=qp, t_=tag: e.activation(nrow[t_][0:1, 512 * w_:512 * (w_ + 1)],
                                                                    (PSC if qp_ == 0 else PSD)[0:1, 0:512], AF.Copy),
                 rd=("psC" if qp == 0 else "psD",), wr=("nrow",))

    norms("k", kf, "kf0")
    k.op("v", lambda e: e.reduce_max(B["sc1"][0:1, 0:1], nrow["k"][0:1, :], AX.X), rd=("nrow",), wr=("kmax",))
    k.op("sc", lambda e: e.activation(B["sc1"][0:1, 1:2], B["sc1"][0:1, 0:1], AF.Sqrt), rd=("kmax",), wr=("kmax",))
    k.op("v", lambda e: e.tensor_scalar(B["sc1"][0:1, 1:2], B["sc1"][0:1, 1:2], 1.02, None, ALU.mult),
         rd=("kmax",), wr=("kmax",))
    for h in range(4):
        norms(f"q{h}", B["qf"][h], f"qf{h}")
        k.op("sc", lambda e, h_=h: e.activation(nrow[f"q{h_}"][0:1, :], nrow[f"q{h_}"][0:1, :], AF.Sqrt),
             rd=("nrow",), wr=("nrow",))
        k.op("v", lambda e, h_=h: e.tensor_single_scalar(nrow[f"q{h_}"][0:1, :], nrow[f"q{h_}"][0:1, :],
                                                         B["sc1"][0:1, 1:2], ALU.mult),
             rd=("nrow", "kmax"), wr=("nrow",))
        k.op("v", lambda e, h_=h: e.tensor_single_scalar(nrow[f"q{h_}"][0:1, :], nrow[f"q{h_}"][0:1, :],
                                                         1.02, ALU.mult),
             rd=("nrow",), wr=("nrow",))
        k.op("v", lambda e, h_=h: e.reduce_max(B["btile"][0:1, 16 * h_:16 * (h_ + 1)],
                                               bass.AP(nrow[f"q{h_}"], 0, [[8192, 1], [512, 16], [1, 512]]), AX.X),
             rd=("nrow",), wr=(f"bt{h}",))
        k.op("v", lambda e, h_=h: e.tensor_scalar(B["btn"][0:1, 16 * h_:16 * (h_ + 1)],
                                                  B["btile"][0:1, 16 * h_:16 * (h_ + 1)], -ISQ, None, ALU.mult),
             rd=(f"bt{h}",), wr=(f"btn{h}",))
        k.op("t", lambda e, h_=h: e.matmul(PSC[:, 0:16], onesr[:, :], B["btn"][0:1, 16 * h_:16 * (h_ + 1)],
                                           start=True, stop=True),
             rd=(f"btn{h}", "onesr"), wr=("psC",))
        k.op("sc", lambda e, h_=h: e.activation(B["negb"][:, 16 * h_:16 * (h_ + 1)], PSC[:, 0:16], AF.Copy),
             rd=("psC",), wr=("negb",))
    PT = B["ptb"]
    SCB = [PSA[:, 0:512], PSA[:, 512:1024], PSC[:, 0:512], PSD[:, 0:512]]
    SCK = ["psA0", "psA1", "psC", "psD"]
    for h in range(4):
        qf = B["qf"][h]
        for qt in range(NQT):
            ncc = 4 * (qt + 1)
            for cc2 in range(ncc):
                sp = cc2 % 4
                k.op("t", lambda e, qf_=qf, qt_=qt, c_=cc2, sp_=sp: e.matmul(
                         SCB[sp_], kf[:, 128 * c_:128 * (c_ + 1)],
                         qf_[:, 512 * qt_:512 * (qt_ + 1)], start=True, stop=True),
                     rd=("kf0", f"qf{h}"), wr=(SCK[sp],))
                diag = cc2 >= 4 * qt
                if diag:
                    p = cc2 - 4 * qt
                    mp = cc2 % 2
                    k.op("v", lambda e, p_=p, sp_=sp, mp_=mp: e.tensor_tensor(
                             B["mskbuf"][:, 512 * mp_:512 * (mp_ + 1)], SCB[sp_],
                             B["maskd"][:, 512 * p_:512 * (p_ + 1)], ALU.add),
                         rd=(SCK[sp], "maskd"), wr=(f"mskbuf{mp}",))
                    k.op("sc", lambda e, h_=h, qt_=qt, sp_=sp, mp_=mp: e.activation(
                             PT[sp_][:, :], B["mskbuf"][:, 512 * mp_:512 * (mp_ + 1)], AF.Exp, scale=ISQ,
                             bias=B["negb"][:, 16 * h_ + qt_:16 * h_ + qt_ + 1]),
                         rd=(f"mskbuf{mp}", "negb"), wr=(f"pt{sp}",))
                else:
                    k.op("sc", lambda e, h_=h, qt_=qt, sp_=sp: e.activation(
                             PT[sp_][:, :], SCB[sp_], AF.Exp, scale=ISQ,
                             bias=B["negb"][:, 16 * h_ + qt_:16 * h_ + qt_ + 1]),
                         rd=(SCK[sp], "negb"), wr=(f"pt{sp}",))
                k.op("t", lambda e, c_=cc2, sp_=sp, first=(cc2 == 0), last=(cc2 == ncc - 1): e.matmul(
                         PSB[:, 0:512], vt[:, 128 * c_:128 * (c_ + 1)], PT[sp_][:, :],
                         start=first, stop=last),
                     rd=("vt", f"pt{sp}"), wr=("psB0",))
                k.op("t", lambda e, c_=cc2, sp_=sp, first=(cc2 == 0), last=(cc2 == ncc - 1): e.matmul(
                         PSB[0:1, 512:1024], B["ones_col_bf"][:, :], PT[sp_][:, :], start=first, stop=last),
                     rd=(f"pt{sp}", "onescb"), wr=("psB1",))
            k.op("sc", lambda e: e.activation(B["denr"][0:1, :], PSB[0:1, 512:1024], AF.Copy),
                 rd=("psB1",), wr=("denr",))
            k.op("v", lambda e: e.reciprocal(B["denri"][0:1, :], B["denr"][0:1, :]), rd=("denr",), wr=("denri",))
            k.op("t", lambda e: e.matmul(PSB[:, 512:1024], onesr[:, :], B["denri"][0:1, :], start=True, stop=True),
                 rd=("denri", "onesr"), wr=("psB1",))
            k.op("sc", lambda e: e.activation(B["recb"][:, :], PSB[:, 512:1024], AF.Copy), rd=("psB1",), wr=("recb",))
            k.op("v", lambda e: e.tensor_tensor(B["aost"][:, :], PSB[:, 0:512], B["recb"][:, :], ALU.mult),
                 rd=("psB0", "recb"), wr=("aost",))
            k.op("s", lambda e, h_=h, qt_=qt: [e.dma_start(
                     out=d["attnc"].ap()[128 * h_:128 * (h_ + 1), 512 * qt_:512 * (qt_ + 1)],
                     in_=B["aost"][:, :])],
                 rd=("aost",), dma=True, tag="aost")


def build_launchD():
    from contextlib import ExitStack
    nc = bass.Bass("TRN2", target_bir_lowering=False, detect_race_conditions=False)
    d = {}
    d["qin"] = nc.dram_tensor("qin", [512, SEQ], BF16, kind="ExternalInput")
    d["kin"] = nc.dram_tensor("kin", [128, SEQ], BF16, kind="ExternalInput")
    d["vin"] = nc.dram_tensor("vin", [128, SEQ], BF16, kind="ExternalInput")
    d["cosd"] = nc.dram_tensor("cosd", [64, SEQ], F32, kind="ExternalInput")
    d["sind"] = nc.dram_tensor("sind", [64, SEQ], F32, kind="ExternalInput")
    d["maskd"] = nc.dram_tensor("maskd", [128, 2048], F32, kind="ExternalInput")
    d["attnc"] = nc.dram_tensor("attnc", [512, SEQ], BF16, kind="ExternalOutput")
    c = mkconsts(nc)
    k = K(nc)
    B = {}
    B["PS_A"] = nc.alloc_psum_tensor("PS_A", [128, 1024], F32)
    B["PS_B"] = nc.alloc_psum_tensor("PS_B", [128, 1024], F32)
    B["PS_C"] = nc.alloc_psum_tensor("PS_C", [128, 512], F32)
    B["PS_D"] = nc.alloc_psum_tensor("PS_D", [128, 512], F32)
    B["PST"] = nc.alloc_psum_tensor("PST", [128, 2048], BF16)
    per = [("ident_bf", c["ident_bf"], [128, 128], BF16, "identb"),
           ("ones_row", c["ones_row"], [1, 128], F32, "onesr"),
           ("ones_col", c["ones_col"], [128, 1], F32, "onesc"),
           ("ones_col_bf", c["ones_col_bf"], [128, 1], BF16, "onescb"),
           ("maskd", d["maskd"], [128, 2048], F32, "maskd")]
    for name, srcd, shape, dt_, key in per:
        t = nc.alloc_sbuf_tensor("sb_" + name, shape, dt_)
        B[name] = t
        k.op("s", lambda e, s_=srcd, t_=t: [e.dma_start(out=t_[:, :], in_=s_.ap())], wr=(key,), dma=True)
    B["cosb"] = nc.alloc_sbuf_tensor("sb_cosb", [64, SEQ], BF16)
    B["sinb"] = nc.alloc_sbuf_tensor("sb_sinb", [64, SEQ], BF16)
    k.op("g", lambda e: [e.dma_start(out=B["cosb"][:, :], in_=d["cosd"].ap())], wr=("cosb",), dma=True)
    k.op("g", lambda e: [e.dma_start(out=B["sinb"][:, :], in_=d["sind"].ap())], wr=("sinb",), dma=True)
    B["qf"] = [nc.alloc_sbuf_tensor(f"qf{h}", [128, SEQ], BF16) for h in range(4)]
    B["kf"] = [nc.alloc_sbuf_tensor("kf0", [128, SEQ], BF16)]
    B["vf"] = [nc.alloc_sbuf_tensor("vf0", [128, SEQ], BF16)]
    for h in range(4):
        k.op("s", lambda e, h_=h: [e.dma_start(out=B["qf"][h_][:, :], in_=d["qin"].ap()[128 * h_:128 * (h_ + 1), :])],
             wr=(f"qf{h}",), dma=True)
    k.op("s", lambda e: [e.dma_start(out=B["kf"][0][:, :], in_=d["kin"].ap())], wr=("kf0",), dma=True)
    k.op("s", lambda e: [e.dma_start(out=B["vf"][0][:, :], in_=d["vin"].ap())], wr=("vf0",), dma=True)
    B["vt"] = nc.alloc_sbuf_tensor("vt", [128, SEQ], BF16)
    k.flush("constsD")
    k.drain()

    st = ExitStack()
    for nm in ["rta", "rtb", "rtc", "rtd", "rtodd"]:
        B[nm] = st.enter_context(nc.sbuf_tensor(nm, [64, 2048], F32))
    emit_rope(k, B, "kf", None, 1, "kf")
    emit_rope(k, B, "qf", None, 4, "qf")
    k.flush("rope")
    k.drain()
    st.close()

    st2 = ExitStack()
    B["sqt"] = [st2.enter_context(nc.sbuf_tensor(f"sqt{i}", [128, 512], F32)) for i in range(2)]
    nr1 = st2.enter_context(nc.sbuf_tensor("nrow0", [2, SEQ], F32))
    B["nrow"] = {t_: nr1 for t_ in ["k", "q0", "q1", "q2", "q3"]}
    B["sc1"] = st2.enter_context(nc.sbuf_tensor("sc1", [2, 4], F32))
    B["btile"] = st2.enter_context(nc.sbuf_tensor("btile", [2, 64], F32))
    B["btn"] = st2.enter_context(nc.sbuf_tensor("btn", [1, 64], F32))
    B["negb"] = st2.enter_context(nc.sbuf_tensor("negb", [128, 64], F32))
    B["ptb"] = [st2.enter_context(nc.sbuf_tensor(f"ptb{i}", [128, 512], BF16)) for i in range(4)]
    B["mskbuf"] = st2.enter_context(nc.sbuf_tensor("mskbuf", [128, 1024], F32))
    B["denr"] = st2.enter_context(nc.sbuf_tensor("denr", [2, 512], F32))
    B["denri"] = st2.enter_context(nc.sbuf_tensor("denri", [1, 512], F32))
    B["recb"] = st2.enter_context(nc.sbuf_tensor("recb", [128, 512], F32))
    B["aost"] = st2.enter_context(nc.sbuf_tensor("aost", [128, 512], BF16))
    emit_attn(k, B, d)
    k.flush("attn")
    k.drain()
    st2.close()
    return nc, k, d


def prep_launchD_inputs(core, qkvc_all):
    import math
    q = np.concatenate([np.asarray(qkvc_all[b])[512 * core:512 * (core + 1), :] for b in range(8)], axis=1)
    krow = NQH * 128 + core * 128
    kin = np.concatenate([np.asarray(qkvc_all[b])[krow:krow + 128, :] for b in range(8)], axis=1)
    vrow = (NQH + NKV) * 128 + core * 128
    vin = np.concatenate([np.asarray(qkvc_all[b])[vrow:vrow + 128, :] for b in range(8)], axis=1)
    inv = 1.0 / 10000.0 ** (np.arange(0, HDIM, 2, dtype=np.float32) / HDIM)
    pos = np.arange(SEQ, dtype=np.float32)
    ang = pos[None, :] * inv[:, None]
    mask = np.zeros((128, 2048), np.float32)
    for p in range(4):
        s = np.arange(128)[:, None]
        tq = np.arange(512)[None, :]
        mask[:, 512 * p:512 * (p + 1)] = np.where(s <= tq - 128 * p, 0.0, -1e30)
    return {
        "qin": np.ascontiguousarray(q), "kin": np.ascontiguousarray(kin), "vin": np.ascontiguousarray(vin),
        "cosd": np.cos(ang).astype(np.float32), "sind": np.sin(ang).astype(np.float32),
        "maskd": mask,
    }



def emit_wo_combine(k, B, d):
    nc = k.nc
    PSA, PSB = B["PS_A"], B["PS_B"]
    att = B["att"]

    for o in range(NDC):
        par = o % 2
        k.op("g", lambda e, o_=o, p_=par: [e.dma_start(out=B["wbuf"][p_][:, :], in_=d["wot"].ap()[:, o_, :])],
             wr=(f"wb{par}",), dma=True)
        def mm(e, o_=o, p_=par):
            for j in range(NDC):
                w = B["wbuf"][p_][:, 128 * j:128 * (j + 1)]
                st, sp = (j == 0), (j == NDC - 1)
                e.matmul((PSA if p_ == 0 else PSB)[:, 0:512], w, att[:, j * T:j * T + 512], start=st, stop=sp)
                ti = e.matmul((PSA if p_ == 0 else PSB)[:, 512:1024], w, att[:, j * T + 512:j * T + 1024], start=st, stop=sp)
            return ti
        k.op("t", mm, rd=(f"wb{par}", "att"), wr=("psA0", "psA1") if par == 0 else ("psB0", "psB1"))
        k.op("sc", lambda e, o_=o, p_=par: e.activation(B["s1st"][p_][:, :], (PSA if p_ == 0 else PSB)[:, :],
                                                        AF.Identity, bias=B["boc"][:, o_:o_ + 1]),
             rd=(("psA0", "psA1", "boc") if par == 0 else ("psB0", "psB1", "boc")), wr=(f"s1st{par}",))
        k.op("s", lambda e, o_=o, p_=par: [e.dma_start(out=B["g1b"][p_][:, :],
                                                       in_=d["g1d"].ap()[128 * o_:128 * (o_ + 1), :])],
             wr=(f"g1b{par}",), dma=True)
        k.op("v", lambda e, p_=par: e.tensor_tensor(B["s1st"][p_][:, :], B["s1st"][p_][:, :],
                                                    B["g1b"][p_][:, :], ALU.mult),
             rd=(f"s1st{par}", f"g1b{par}"), wr=(f"s1st{par}",))
        k.op("s", lambda e, o_=o, p_=par: [e.dma_start(out=d["sgd"].ap()[128 * o_:128 * (o_ + 1), :],
                                                       in_=B["s1st"][p_][:, :])],
             rd=(f"s1st{par}",), dma=True, tag=f"sgo{par}")


def build_launchE():
    from contextlib import ExitStack
    nc = bass.Bass("TRN2", target_bir_lowering=False, detect_race_conditions=False)
    d = {}
    d["x1"] = nc.dram_tensor("x1", [T, D], F32, kind="ExternalInput")
    d["attin"] = nc.dram_tensor("attin", [D, T], BF16, kind="ExternalInput")
    d["g1d"] = nc.dram_tensor("g1d", [D, T], F32, kind="ExternalInput")
    d["wot"] = nc.dram_tensor("wot", [128, NDC, NDC * 128], F32, kind="ExternalInput")
    d["boc"] = nc.dram_tensor("boc", [128, NDC], F32, kind="ExternalInput")
    d["sgd"] = nc.dram_tensor("sgd", [D, T], F32, kind="Internal")
    d["x2"] = nc.dram_tensor("x2", [T, D], F32, kind="ExternalOutput")
    c = mkconsts(nc)
    k = K(nc)
    B = {}
    B["PS_A"] = nc.alloc_psum_tensor("PS_A", [128, 1024], F32)
    B["PS_B"] = nc.alloc_psum_tensor("PS_B", [128, 1024], F32)
    B["PS_C"] = nc.alloc_psum_tensor("PS_C", [128, 512], F32)
    B["PS_D"] = nc.alloc_psum_tensor("PS_D", [128, 512], F32)
    per = [("ident_f32", c["ident_f32"], [128, 128], F32, "identf"),
           ("boc", d["boc"], [128, NDC], F32, "boc")]
    for name, srcd, shape, dt_, key in per:
        t = nc.alloc_sbuf_tensor("sb_" + name, shape, dt_)
        B[name] = t
        k.op("s", lambda e, s_=srcd, t_=t: [e.dma_start(out=t_[:, :], in_=s_.ap())], wr=(key,), dma=True)
    B["att"] = nc.alloc_sbuf_tensor("att", [128, NDC * T], BF16)
    def ldatt(e):
        return [e.dma_start(out=B["att"][:, j * T:(j + 1) * T], in_=d["attin"].ap()[128 * j:128 * (j + 1), :])
                for j in range(NDC)]
    k.op("s", ldatt, wr=("att",), dma=True, n_dma=NDC)
    k.flush("constsE")
    k.drain()
    st = ExitStack()
    B["wbuf"] = [st.enter_context(nc.sbuf_tensor(f"wbufE{i}", [128, 4096], BF16)) for i in range(2)]
    B["s1st"] = [st.enter_context(nc.sbuf_tensor(f"s1st{i}", [128, 1024], F32)) for i in range(2)]
    B["g1b"] = [st.enter_context(nc.sbuf_tensor(f"g1b{i}", [128, 1024], F32)) for i in range(2)]
    emit_wo_combine(k, B, d)
    k.flush("wo")
    k.drain()
    st.close()
    st2 = ExitStack()
    B["xres"] = [st2.enter_context(nc.sbuf_tensor(f"xresE{i}", [128, D], F32)) for i in range(2)]
    B["x1st"] = st2.enter_context(nc.sbuf_tensor("x1stE", [128, D], F32))
    B["sgtile"] = [st2.enter_context(nc.sbuf_tensor(f"sgtileE{i}", [128, 128], F32)) for i in range(2)]
    emit_combine(k, B, d, d["x1"], d["x2"], d["sgd"])
    k.flush("combE")
    k.drain()
    st2.close()
    return nc, k, d


def prep_launchE_inputs(ins, core, x1, attnc_all):
    att = np.concatenate([np.asarray(attnc_all[b]) for b in range(8)], axis=0)
    return {
        "x1": np.ascontiguousarray(x1[T * core:T * (core + 1)]),
        "attin": np.ascontiguousarray(att[:, T * core:T * (core + 1)]),
        "g1d": None,
        "wot": tile_w(np.asarray(ins["L1_attn_w_o"])),
        "boc": col_layout(ins["L1_attn_b_o"]),
    }



def kernel(**inputs):
    ins = inputs
    ncA, kA, dA = build_launchA()
    mA = [prep_launchA_inputs(ins, c) for c in range(NCORES)]
    resA = run_bass_kernel_spmd(ncA, mA, core_ids=list(range(NCORES)))
    la = [resA.results[c] for c in range(NCORES)]
    ncB, kB, dB = build_launchB()
    mB = [prep_launchB_inputs(ins, c, la) for c in range(NCORES)]
    resB = run_bass_kernel_spmd(ncB, mB, core_ids=list(range(NCORES)))
    x1 = np.concatenate([np.asarray(resB.results[c]["x1"]) for c in range(NCORES)], axis=0)
    ncC, kC, dC = build_launchC()
    mC = [prep_launchC_inputs(ins, c, x1) for c in range(NCORES)]
    resC = run_bass_kernel_spmd(ncC, mC, core_ids=list(range(NCORES)))
    qkvc_all = [np.asarray(resC.results[c]["qkvc"]) for c in range(NCORES)]
    g1_all = [np.asarray(resC.results[c]["g1d"]) for c in range(NCORES)]
    ncD, kD, dD = build_launchD()
    mD = [prep_launchD_inputs(c, qkvc_all) for c in range(NCORES)]
    resD = run_bass_kernel_spmd(ncD, mD, core_ids=list(range(NCORES)))
    attnc_all = [np.asarray(resD.results[c]["attnc"]) for c in range(NCORES)]
    ncE, kE, dE = build_launchE()
    mE = []
    for c in range(NCORES):
        m = prep_launchE_inputs(ins, c, x1, attnc_all)
        m["g1d"] = g1_all[c]
        mE.append(m)
    resE = run_bass_kernel_spmd(ncE, mE, core_ids=list(range(NCORES)))
    x2 = np.concatenate([np.asarray(resE.results[c]["x2"]) for c in range(NCORES)], axis=0)
    return x2.reshape(1, SEQ, D)
